# version 16
# baseline (speedup 1.0000x reference)
import numpy as np
import ml_dtypes
import concourse.bass as bass
import concourse.mybir as mybir
from concourse.bass_utils import run_bass_kernel_spmd

F32 = mybir.dt.float32
BF16 = mybir.dt.bfloat16
AF = mybir.ActivationFunctionType
ALU = mybir.AluOpType
AX = mybir.AxisListType

B, T, D = 4, 4096, 4096
NQH, NKVH, DH = 32, 8, 128
GRP = NQH // NKVH
NT = 4
NE, RANK = 28, 8
EPS = 1e-5
ISQ = float(1.0 / np.sqrt(128.0))


class Buf:
    __slots__ = ("name", "last_w", "readers", "sem_name", "dma_count", "sink")

    def __init__(self, name, sink=False):
        self.name = name
        self.last_w = None
        self.readers = []
        self.sem_name = None
        self.dma_count = 0
        self.sink = sink


class Em:
    def __init__(self):
        self.ops = []

    def buf(self, name, sink=False):
        return Buf(name, sink)

    def op(self, engine, fn, reads=(), writes=(), tag=""):
        assert engine in ("pe", "act", "dve", "pool"), engine
        self._rec(engine, fn, reads, writes, tag, False)

    def dma(self, queue, fn, reads=(), writes=(), tag=""):
        assert queue in ("sp", "act", "pool"), queue
        self._rec(queue, fn, reads, writes, tag, True)

    def barrier(self, engine, bufs, tag="bar"):
        self._rec(engine, None, bufs, (), tag, False)

    def _rec(self, engine, fn, reads, writes, tag, dma):
        idx = len(self.ops)
        deps = set()
        for b in reads:
            if b.sink:
                continue
            if b.last_w is not None:
                deps.add(b.last_w)
        for b in writes:
            if b.sink:
                continue
            deps.update(b.readers)
            if b.last_w is not None:
                deps.add(b.last_w)
        for b in reads:
            if not b.sink:
                b.readers.append(idx)
        for b in writes:
            if not b.sink:
                b.last_w = idx
                b.readers = []
        self.ops.append(dict(engine=engine, fn=fn, deps=sorted(deps), dma=dma,
                             tag=tag, idx=idx, marked=False,
                             writes=[b for b in writes if not b.sink]))

    def finalize(self, nc):
        ops = self.ops
        for o in ops:
            for j in o["deps"]:
                ops[j]["marked"] = True
        eng_sem = {e: nc.alloc_semaphore(f"es_{e}") for e in ("pe", "act", "dve", "pool", "sp")}
        self.eng_sem = eng_sem
        self.buf_sems = {}
        eng_cnt = {e: 0 for e in eng_sem}
        nsem = 0
        for o in ops:
            if o["dma"]:
                wb = o["writes"]
                assert wb, f"dma without tracked write: {o['tag']}"
                tgt = wb[0]
                if tgt.sem_name is None:
                    tgt.sem_name = f"ds{nsem}"
                    nsem += 1
                    self.buf_sems[tgt.sem_name] = nc.alloc_semaphore(tgt.sem_name)
                tgt.dma_count += 16
                o["sem_key"] = tgt.sem_name
                o["sem_val"] = tgt.dma_count
                for b in wb[1:]:
                    b.sem_name = tgt.sem_name
                    b.dma_count = tgt.dma_count
            else:
                if o["marked"]:
                    eng_cnt[o["engine"]] += 1
                    o["sem_val"] = eng_cnt[o["engine"]]
                else:
                    o["sem_val"] = None
        for o in ops:
            waits = []
            for j in o["deps"]:
                p = ops[j]
                if p["dma"]:
                    waits.append(("d", p["sem_key"], p["sem_val"]))
                else:
                    if p["engine"] == "pe" and o["engine"] == "pe" and not o["dma"]:
                        continue
                    assert p["sem_val"] is not None
                    waits.append(("e", p["engine"], p["sem_val"]))
            o["waits"] = waits

    def emit(self, eng_obj, engine):
        water = {}
        for o in self.ops:
            if o["engine"] != engine:
                continue
            for (kind, key, val) in o["waits"]:
                sem = self.buf_sems[key] if kind == "d" else self.eng_sem[key]
                wk = (kind, key)
                if water.get(wk, 0) >= val:
                    continue
                water[wk] = val
                eng_obj.wait_ge(sem, val)
            if o["fn"] is None:
                continue
            inst = o["fn"](eng_obj)
            if o["dma"]:
                inst.then_inc(self.buf_sems[o["sem_key"]], 16)
            elif o["sem_val"] is not None:
                inst.then_inc(self.eng_sem[engine], 1)


def build_program(taps=(), dbg=False):
    nc = bass.Bass("TRN2", target_bir_lowering=False)
    em = Em()

    def inp(name, shape, dtype=F32):
        return nc.dram_tensor(name, shape, dtype, kind="ExternalInput").ap(), em.buf(name)

    x_in, x_in_b = inp("x_in", [T, D])
    xo_in, xo_in_b = inp("xo_in", [2048, D])
    wq_in, wq_in_b = inp("wq_in", [D, D])
    wk_in, wk_in_b = inp("wk_in", [D, 1024])
    wv_in, wv_in_b = inp("wv_in", [D, 1024])
    wo_in, wo_in_b = inp("wo_in", [D, D])
    bw_in, bw_in_b = inp("bw_in", [D, D])
    gw_in, gw_in_b = inp("gw_in", [NE, D])
    la_in, la_in_b = inp("la_in", [NE * RANK, D])
    lb_in, lb_in_b = inp("lb_in", [NE * D, RANK])
    cosf_in, cosf_b = inp("cosf", [128, T], BF16)
    sinf_in, sinf_b = inp("sinf", [128, T], BF16)
    cost_in, cost_b = inp("cost", [2048, 128], BF16)
    sint_in, sint_b = inp("sint", [2048, 128], BF16)
    masks_in, masks_b = inp("masks", [4, 128, 1024], BF16)
    ident_in, ident_b = inp("ident", [128, 128], F32)
    r1_in, r1_b = inp("r1c", [NE, 128], BF16)
    r2_in, r2_b = inp("r2c", [NE, 128], BF16)

    out_ap = nc.dram_tensor("out", [2048, D], F32, kind="ExternalOutput").ap()
    out_b = em.buf("out")
    if dbg:
        dbg_biasc = nc.dram_tensor("dbg_biasc", [128, 1], F32, kind="ExternalOutput").ap()
        dbg_pt = nc.dram_tensor("dbg_pt", [128, 1024], BF16, kind="ExternalOutput").ap()
        dbg_l = nc.dram_tensor("dbg_l", [1, 512], F32, kind="ExternalOutput").ap()
        dbg_ctx = nc.dram_tensor("dbg_ctx", [128, 512], BF16, kind="ExternalOutput").ap()
        dbg_bufs = [em.buf(f"dbg{j}") for j in range(4)]

    def scr(name, shape, dtype):
        return nc.dram_tensor(name, shape, dtype, kind="Internal").ap(), em.buf(name)

    wq_bf, wq_bf_b = scr("wq_bf", [D, D], BF16)
    wk_bf, wk_bf_b = scr("wk_bf", [D, 1024], BF16)
    wv_bf, wv_bf_b = scr("wv_bf", [D, 1024], BF16)
    wo_bf, wo_bf_b = scr("wo_bf", [D, D], BF16)
    bw_tmp, bw_tmp_b = scr("bw_tmp", [D, D], BF16)
    bwT_bf, bwT_bf_b = scr("bwT_bf", [D, D], BF16)
    la_tmp, la_tmp_b = scr("la_tmp", [NE * RANK, D], BF16)
    lataug, lataug_b = scr("lataug", [D, 256], BF16)
    lbsc, lbsc_b = scr("lbsc", [NE * RANK, D], BF16)
    lnscr, _ = scr("lnscr", [T, D], BF16)
    lnscr_tb = [em.buf(f"lnscr{t}") for t in range(8)]
    lnoscr, _ = scr("lnoscr", [2048, D], BF16)
    lnoscr_tb = [em.buf(f"lnoscr{i}") for i in range(NT)]
    kscr, kscr_b = scr("kscr", [NKVH, 128, T], BF16)
    vscr, vscr_b = scr("vscr", [NKVH, 128, T], BF16)
    qscr, qscr_b = scr("qscr", [2048, D], BF16)
    ln4scr, _ = scr("ln4scr", [512, D], BF16)
    ln4scr_b = em.buf("ln4scr")
    x1scr, _ = scr("x1scr", [512, D], F32)
    x1scr_b = em.buf("x1scr")

    def sbuf(name, shape, dtype):
        return nc.alloc_sbuf_tensor(name, shape, dtype), em.buf(name)

    cosP = [sbuf("cosP0", [128, 512], BF16), sbuf("cosP1", [128, 512], BF16)]
    sinP = [sbuf("sinP0", [128, 512], BF16), sbuf("sinP1", [128, 512], BF16)]
    cos_tm, cos_tm_b = sbuf("cos_tm", [128, 4 * 128], BF16)
    sin_tm, sin_tm_b = sbuf("sin_tm", [128, 4 * 128], BF16)
    masks_sb, masks_sb_b = sbuf("masks_sb", [128, 4 * 1024], BF16)
    ident_sb, ident_sb_b = sbuf("ident_sb", [128, 128], F32)
    ones_bf, ones_bf_b = sbuf("ones_bf", [128, 1], BF16)
    onesr_f, onesr_f_b = sbuf("onesr_f", [1, 128], F32)
    cpool, cpool_b = sbuf("cpool", [128, 2], F32)
    r1_sb, r1_sb_b = sbuf("r1_sb", [NE, 128], BF16)
    r2_sb, r2_sb_b = sbuf("r2_sb", [NE, 128], BF16)
    knorm, knorm_b = sbuf("knorm", [1, 64], F32)
    qnorm, qnorm_b = sbuf("qnorm", [1, NT], F32)
    biasc, biasc_b = sbuf("biasc", [128, 1], F32)
    tiny, tiny_b = sbuf("tiny", [1, 16], F32)
    slab_a, slab_a_b = sbuf("slab_a", [128, D], F32)
    slab_b, slab_b_b = sbuf("slab_b", [128, D], F32)
    tsl = []
    for i in range(4):
        tsl.append(sbuf(f"tsl{i}", [128, D], BF16))
    lnout, lnout_b = sbuf("lnout", [128, D], BF16)
    fm0, fm0_b = sbuf("fm0", [128, 32 * 512], BF16)
    wrow0, wrow0_b = sbuf("wrow0", [128, 1024], BF16)
    wrow1, wrow1_b = sbuf("wrow1", [128, 1024], BF16)
    wrow2, wrow2_b = sbuf("wrow2", [128, 1024], BF16)
    wrow3, wrow3_b = sbuf("wrow3", [128, 1024], BF16)
    wrow4, wrow4_b = sbuf("wrow4", [128, 1024], BF16)
    wrow5, wrow5_b = sbuf("wrow5", [128, 1024], BF16)
    WR4 = [(wrow0, wrow0_b), (wrow1, wrow1_b), (wrow2, wrow2_b), (wrow3, wrow3_b),
           (wrow4, wrow4_b), (wrow5, wrow5_b)]
    augr0, augr0_b = sbuf("augr0", [128, 256], BF16)
    augr1, augr1_b = sbuf("augr1", [128, 256], BF16)
    lbr1, lbr1_b = sbuf("lbr1", [128, 1024], BF16)
    lbr2, lbr2_b = sbuf("lbr2", [128, 1024], BF16)
    kh = [sbuf("kh0", [128, T], BF16), sbuf("kh1", [128, T], BF16)]
    vh = [sbuf("vh0", [128, T], BF16), sbuf("vh1", [128, T], BF16)]
    qt = [sbuf("qt0", [128, 512], BF16), sbuf("qt1", [128, 512], BF16)]
    pt = [sbuf("pt0", [128, 1024], BF16), sbuf("pt1", [128, 1024], BF16)]
    vstage, vstage_b = sbuf("vstage", [128, 2 * 512], BF16)
    stage0, stage0_b = sbuf("stage0", [128, 1024], BF16)
    stage1, stage1_b = sbuf("stage1", [128, 1024], BF16)
    krotB = [sbuf("krot0", [128, 512], BF16), sbuf("krot1", [128, 512], BF16)]
    rt1B = [sbuf("rt1a", [128, 512], BF16), sbuf("rt1b", [128, 512], BF16)]
    rt2B = [sbuf("rt2a", [128, 512], BF16), sbuf("rt2b", [128, 512], BF16)]
    kstB = [sbuf("kst0", [128, 512], BF16), sbuf("kst1", [128, 512], BF16)]
    krot, krot_b = krotB[0]
    rt1, rt1_b = rt1B[0]
    rt2, rt2_b = rt2B[0]
    l_sb, l_sb_b = sbuf("l_sb", [1, 512], F32)
    linv, linv_b = sbuf("linv", [1, 512], F32)
    linvb, linvb_b = sbuf("linvb", [128, 512], BF16)
    sm0, sm0_b = sbuf("sm0", [128, 80], F32)
    gsm, gsm_b = sbuf("gsm", [128, 40], F32)
    gtmp, gtmp_b = sbuf("gtmp", [128, NE], F32)
    gtmp2, gtmp2_b = sbuf("gtmp2", [128, NE], F32)
    gate_tm, gate_tm_b = sbuf("gate_tm", [128, NE], F32)
    lgt, lgt_b = sbuf("lgt", [NE, 512], F32)
    g28, g28_b = sbuf("g28", [NE, 512], BF16)
    gf1, gf1_b = sbuf("gf1", [128, 512], BF16)
    gf2, gf2_b = sbuf("gf2", [128, 512], BF16)
    h1p, h1p_b = sbuf("h1p", [128, 512], BF16)
    h2p, h2p_b = sbuf("h2p", [128, 512], BF16)
    qacc, qacc_b = sbuf("qacc", [128, 32], F32)

    pA = nc.alloc_psum_tensor("pA", [128, 1024], F32)
    pB = nc.alloc_psum_tensor("pB", [128, 1024], F32)
    pC = nc.alloc_psum_tensor("pC", [128, 1024], F32)
    pD = nc.alloc_psum_tensor("pD", [128, 1024], F32)
    pbufs = {}
    for nm in ("A0", "A1", "B0", "B1", "C0", "C1", "D0", "D1"):
        pbufs[nm] = em.buf("p" + nm)
    PS = [(pA, 0, pbufs["A0"]), (pA, 512, pbufs["A1"]),
          (pB, 0, pbufs["B0"]), (pB, 512, pbufs["B1"]),
          (pC, 0, pbufs["C0"]), (pC, 512, pbufs["C1"]),
          (pD, 0, pbufs["D0"]), (pD, 512, pbufs["D1"])]

    em.dma("sp", lambda e: e.dma_start(masks_sb[:].rearrange("p (m f) -> p m f", m=4), masks_in.rearrange("m p f -> p m f")),
           [masks_b], [masks_sb_b], "c4")
    em.dma("sp", lambda e: e.dma_start(ident_sb[:], ident_in), [ident_b], [ident_sb_b], "c5")
    em.dma("sp", lambda e: e.dma_start(r1_sb[:], r1_in), [r1_b], [r1_sb_b], "c6")
    em.dma("sp", lambda e: e.dma_start(r2_sb[:], r2_in), [r2_b], [r2_sb_b], "c7")
    em.op("dve", lambda e: e.memset(ones_bf[:], 1.0), [], [ones_bf_b], "ones")
    em.op("dve", lambda e: e.memset(onesr_f[:], 1.0), [], [onesr_f_b], "onesr")
    em.op("dve", lambda e: e.memset(cpool[:, 0:1], EPS), [], [cpool_b], "eps")
    em.op("dve", lambda e: e.memset(qacc[:], 0.0), [], [qacc_b], "qacc0")
    em.op("dve", lambda e: e.memset(lbr1[0:32, :], 0.0), [], [lbr1_b], "lbr1z")

    def cast_w(src_ap, src_b, dst_ap, dst_b, rows, tag):
        for r0 in range(0, rows, 512):
            n = min(512, rows - r0)
            em.dma("pool", (lambda r0=r0, n=n: lambda e: e.dma_start(
                dst_ap[r0:r0 + n, :], src_ap[r0:r0 + n, :]))(),
                [src_b], [dst_b], f"cw_{tag}{r0}")

    wk_cb = [em.buf(f"wkc{j}") for j in range(8)]
    wv_cb = [em.buf(f"wvc{j}") for j in range(8)]
    for j in range(8):
        em.dma("pool", (lambda j=j: lambda e: e.dma_start(
            wk_bf[j * 512:(j + 1) * 512, :], wk_in[j * 512:(j + 1) * 512, :]))(),
            [wk_in_b], [wk_cb[j]], f"cw_k{j}")
        em.dma("pool", (lambda j=j: lambda e: e.dma_start(
            wv_bf[j * 512:(j + 1) * 512, :], wv_in[j * 512:(j + 1) * 512, :]))(),
            [wv_in_b], [wv_cb[j]], f"cw_v{j}")

    def p0_wq():
        cast_w(wq_in, wq_in_b, wq_bf, wq_bf_b, D, "q")

    def p0_wo():
        cast_w(wo_in, wo_in_b, wo_bf, wo_bf_b, D, "o")

    bw_cb = [em.buf(f"bwc{j}") for j in range(8)]

    def p0_bw():
        for j in range(8):
            em.dma("pool", (lambda j=j: lambda e: e.dma_start(
                bw_tmp[j * 512:(j + 1) * 512, :], bw_in[j * 512:(j + 1) * 512, :]))(),
                [bw_in_b], [bw_cb[j]], f"cw_b{j}")

    def p0_bwT():
        for db in range(32):
            for rb in range(4):
                em.dma("act", (lambda db=db, rb=rb: lambda e: e.dma_start(
                    wrow0[:], bw_tmp[rb * 1024:(rb + 1) * 1024, db * 128:(db + 1) * 128],
                    transpose=True))(),
                    [bw_cb[rb * 2], bw_cb[rb * 2 + 1]], [wrow0_b], f"bwTl{db}_{rb}")
                em.dma("act", (lambda db=db, rb=rb: lambda e: e.dma_start(
                    bwT_bf[db * 128:(db + 1) * 128, rb * 1024:(rb + 1) * 1024], wrow0[:]))(),
                    [wrow0_b], [bwT_bf_b], f"bwTs{db}_{rb}")

    def p0_lacast():
        cast_w(la_in, la_in_b, la_tmp, la_tmp_b, NE * RANK, "a")

    def p0_la():
        for db in range(32):
            em.dma("act", (lambda db=db: lambda e: e.dma_start(
                wrow1[:, 0:224], la_tmp[:, db * 128:(db + 1) * 128], transpose=True))(),
                [la_tmp_b], [wrow1_b], f"laTl{db}")
            em.dma("act", (lambda db=db: lambda e: e.dma_start(
                lataug[db * 128:(db + 1) * 128, 32:256], wrow1[:, 0:224]))(),
                [wrow1_b], [lataug_b], f"laTs{db}")

    def p0_gw():
        em.dma("sp", lambda e: e.dma_start(slab_a[0:NE, :], gw_in), [gw_in_b], [slab_a_b], "gwl")
        em.op("dve", lambda e: e.memset(slab_b[:, 1024:1028], 0.0), [], [slab_b_b], "gwz")
        for db in range(32):
            em.op("pe", (lambda db=db: lambda e: e.transpose(
                pA[0:128, 0:NE], slab_a[0:NE, db * 128:(db + 1) * 128], ident_sb[0:NE, 0:NE]))(),
                [slab_a_b, ident_sb_b], [pbufs["A0"]], f"gwT{db}")
            em.op("act", (lambda db=db: lambda e: e.copy(slab_b[:, 0:NE], pA[0:128, 0:NE]))(),
                [pbufs["A0"]], [slab_b_b], f"gwE{db}")
            em.dma("pool", (lambda db=db: lambda e: e.dma_start(
                lataug[db * 128:(db + 1) * 128, 0:NE], slab_b[:, 0:NE]))(),
                [slab_b_b], [lataug_b], f"gwS{db}")
            em.dma("pool", (lambda db=db: lambda e: e.dma_start(
                lataug[db * 128:(db + 1) * 128, NE:32], slab_b[:, 1024:1028]))(),
                [slab_b_b], [lataug_b], f"gwP{db}")

    def p0_lb():
        PBASES = [(pA, pbufs["A0"]), (pB, pbufs["B0"]), (pC, pbufs["C0"]), (pD, pbufs["D0"])]
        PBASES8 = PBASES + [(pA, pbufs["A1"]), (pB, pbufs["B1"]), (pC, pbufs["C1"]), (pD, pbufs["D1"])]
        for e_i in range(NE):
            em.dma("sp", (lambda e_i=e_i: lambda e: e.dma_start(
                slab_a[:, 0:256].rearrange("p (c r) -> p c r", c=32),
                lb_in[e_i * D:(e_i + 1) * D, :].rearrange("(c p) r -> p c r", p=128)))(),
                [lb_in_b], [slab_a_b], f"lbl{e_i}")
            for grp in range(8):
                for k4 in range(4):
                    oc = grp * 4 + k4
                    pt_, po_, ptb_ = [(pA, 0, pbufs["A0"]), (pB, 0, pbufs["B0"]),
                                      (pC, 0, pbufs["C0"]), (pD, 0, pbufs["D0"]),
                                      (pA, 512, pbufs["A1"]), (pB, 512, pbufs["B1"]),
                                      (pC, 512, pbufs["C1"]), (pD, 512, pbufs["D1"])][oc % 8]
                    em.op("pe", (lambda oc=oc, pt_=pt_, po_=po_: lambda e: e.transpose(
                        pt_[0:8, po_:po_ + 128], slab_a[:, oc * 8:(oc + 1) * 8], ident_sb[:]))(),
                        [slab_a_b, ident_sb_b], [ptb_], f"lbT{e_i}_{oc}")
                    em.op("act", (lambda oc=oc, pt_=pt_, po_=po_: lambda e: e.copy(
                        slab_b[0:8, (oc % 8) * 128:((oc % 8) + 1) * 128], pt_[0:8, po_:po_ + 128]))(),
                        [ptb_], [slab_b_b], f"lbE{e_i}_{oc}")
                if grp % 2 == 1:
                    c0 = (grp - 1) * 4 * 128
                    em.dma("pool", (lambda e_i=e_i, c0=c0, grp=grp: lambda e: e.dma_start(
                        lbsc[e_i * 8:(e_i + 1) * 8, c0:c0 + 1024], slab_b[0:8, 0:1024]))(),
                        [slab_b_b], [lbsc_b], f"lbS{e_i}_{grp}")

    P0_DEFER = {0: p0_wq, 1: p0_wo, 2: p0_bw, 3: p0_lacast,
                4: lambda: p2a_ln(0), 5: lambda: p2a_ln(1),
                6: lambda: p2a_ln(2), 7: lambda: p2a_ln(3)}

    def ln_tile(x_ap, x_b, out_ap, out_bb, tag, dump_ap=None, dump_b=None, col=64):
        if dump_ap is None:
            dump_ap, dump_b = lnout[:], lnout_b
        sq, s1 = sm0[:, col:col + 1], sm0[:, col + 1:col + 2]
        mm_, m2 = sm0[:, col + 2:col + 3], sm0[:, col + 3:col + 4]
        vv, rr, nm = sm0[:, col + 4:col + 5], sm0[:, col + 5:col + 6], sm0[:, col + 6:col + 7]
        em.op("act", lambda e: e.activation(dump_ap, x_ap, AF.Square, accum_out=sq),
              [x_b], [sm0_b, dump_b], f"{tag}sq")
        em.op("dve", lambda e: e.reduce_sum(s1, x_ap, axis=AX.X), [x_b], [sm0_b], f"{tag}s1")
        em.op("dve", lambda e: e.tensor_scalar(mm_, s1, 1.0 / D, None, ALU.mult),
              [sm0_b], [sm0_b], f"{tag}m")
        em.op("dve", lambda e: e.tensor_tensor(m2, mm_, mm_, ALU.mult), [sm0_b], [sm0_b], f"{tag}m2")
        em.op("dve", lambda e: e.scalar_tensor_tensor(vv, sq, 1.0 / D, m2, ALU.mult, ALU.subtract),
              [sm0_b], [sm0_b], f"{tag}v")
        em.op("act", lambda e: e.activation(vv, vv, AF.Ln, bias=cpool[:, 0:1]),
              [sm0_b, cpool_b], [sm0_b], f"{tag}ln")
        em.op("act", lambda e: e.activation(rr, vv, AF.Exp, scale=-0.5), [sm0_b], [sm0_b], f"{tag}ex")
        em.op("dve", lambda e: e.scalar_tensor_tensor(nm, mm_, -1.0, rr, ALU.mult, ALU.mult),
              [sm0_b], [sm0_b], f"{tag}nm")
        em.op("act", lambda e: e.activation(out_ap, x_ap, AF.Identity, bias=nm, scale=rr),
              [x_b, sm0_b], [out_bb], f"{tag}ap")

    def ln_finish(scol, sqcol, tag):
        s1, sq = gsm[:, 8:9], gsm[:, 9:10]
        mm_, m2 = gsm[:, 10:11], gsm[:, 11:12]
        vv, rr, nm = gsm[:, 12:13], gsm[:, 13:14], gsm[:, 14:15]
        em.op("dve", lambda e: e.reduce_sum(s1, sm0[:, scol:scol + 8], axis=AX.X),
              [sm0_b], [gsm_b], f"{tag}rs")
        em.op("dve", lambda e: e.reduce_sum(sq, sm0[:, sqcol:sqcol + 8], axis=AX.X),
              [sm0_b], [gsm_b], f"{tag}rq")
        em.op("dve", lambda e: e.tensor_scalar(mm_, s1, 1.0 / D, None, ALU.mult),
              [gsm_b], [gsm_b], f"{tag}m")
        em.op("dve", lambda e: e.tensor_tensor(m2, mm_, mm_, ALU.mult), [gsm_b], [gsm_b], f"{tag}m2")
        em.op("dve", lambda e: e.scalar_tensor_tensor(vv, sq, 1.0 / D, m2, ALU.mult, ALU.subtract),
              [gsm_b], [gsm_b], f"{tag}v")
        em.op("act", lambda e: e.activation(vv, vv, AF.Ln, bias=cpool[:, 0:1]),
              [gsm_b, cpool_b], [gsm_b], f"{tag}ln")
        em.op("act", lambda e: e.activation(rr, vv, AF.Exp, scale=-0.5), [gsm_b], [gsm_b], f"{tag}ex")
        em.op("dve", lambda e: e.scalar_tensor_tensor(nm, mm_, -1.0, rr, ALU.mult, ALU.mult),
              [gsm_b], [gsm_b], f"{tag}nm")
        return rr, nm

    aslab, aslab_b = tsl[2]
    alnout, alnout_b = tsl[3]

    def p2a_ln(i):
        for ts in range(4):
            r0 = i * 512 + ts * 128
            em.dma("pool", (lambda r0=r0: lambda e: e.dma_start(aslab[:], xo_in[r0:r0 + 128, :]))(),
                   [xo_in_b], [aslab_b], f"a_x{i}_{ts}")
            ln_tile(aslab[:], aslab_b, alnout[:], alnout_b, f"aL{i}_{ts}",
                    dump_ap=alnout[:], dump_b=alnout_b, col=72)
            em.dma("act", (lambda r0=r0: lambda e: e.dma_start(lnoscr[r0:r0 + 128, :], alnout[:]))(),
                   [alnout_b], [lnoscr_tb[i]], f"a_s{i}_{ts}")


    GB = [kh[0], kh[1], vh[0], vh[1]]

    def fmsl(group, dc, c0, c1):
        if group == 0:
            return fm0[:, dc * 512 + c0: dc * 512 + c1], fm0_b
        t, tb = GB[dc // 8]
        return t[:, (dc % 8) * 512 + c0: (dc % 8) * 512 + c1], tb

    p0_gw()
    p0_lb()

    def p1_ln(t8):
        t0 = t8 * 512
        for sub in range(4):
            sl, slb = (slab_a, slab_a_b) if sub % 2 == 0 else (slab_b, slab_b_b)
            r0 = t0 + sub * 128
            em.dma("pool", (lambda r0=r0, sl=sl: lambda e: e.dma_start(sl[:], x_in[r0:r0 + 128, :]))(),
                   [x_in_b], [slb], f"p1x{t8}_{sub}")
            ln_tile(sl[:], slb, lnout[:], lnout_b, f"p1L{t8}_{sub}", col=sub * 7)
            em.dma("act", (lambda r0=r0, t8=t8: lambda e: e.dma_start(lnscr[r0:r0 + 128, :], lnout[:]))(),
                   [lnout_b], [lnscr_tb[t8]], f"p1s{t8}_{sub}")

    def p1_fm(t8):
        t0 = t8 * 512
        for dc in range(32):
            dst, dstb = fmsl(t8 % 2, dc, 0, 512)
            em.dma("act", (lambda dc=dc, t0=t0, dst=dst: lambda e: e.dma_start(
                dst, lnscr[t0:t0 + 512, dc * 128:(dc + 1) * 128], transpose=True))(),
                [lnscr_tb[t8]], [dstb], f"p1f{t8}_{dc}")

    p1_ln(0)
    p1_fm(0)
    for t8 in range(8):
        t0 = t8 * 512
        grp_ = t8 % 2
        for ic in range(32):
            wr, wrb = WR4[ic % 6]
            em.dma("sp", (lambda ic=ic, wr=wr: lambda e: e.dma_start(
                wr[:], wk_bf[ic * 128:(ic + 1) * 128, :]))(),
                [wk_cb[ic // 4]], [wrb], f"p1wk{t8}_{ic}")
            src, srcb = fmsl(grp_, ic, 0, 512)
            for h in range(8):
                bank, off, bb = PS[h]
                em.op("pe", (lambda ic=ic, h=h, wr=wr, bank=bank, off=off, src=src: lambda e: e.matmul(
                    bank[:, off:off + 512], wr[:, h * 128:(h + 1) * 128],
                    src, start=(ic == 0), stop=(ic == 31)))(),
                    [wrb, srcb], [bb], f"p1k{t8}_{ic}_{h}")
        cp, cpb = cosP[t8 % 2]
        sp_, spb = sinP[t8 % 2]
        em.dma("sp", (lambda t0=t0, cp=cp: lambda e: e.dma_start(cp[:], cosf_in[:, t0:t0 + 512]))(),
               [cosf_b], [cpb], f"p1cp{t8}")
        em.dma("sp", (lambda t0=t0, sp_=sp_: lambda e: e.dma_start(sp_[:], sinf_in[:, t0:t0 + 512]))(),
               [sinf_b], [spb], f"p1sp{t8}")
        for h in range(8):
            bank, off, bb = PS[h]
            stg = stage1[:, (h % 2) * 512:(h % 2) * 512 + 512]
            kro, krob = krotB[h % 2]
            r1_, r1b = rt1B[h % 2]
            r2_, r2b = rt2B[h % 2]
            kst, kstb = kstB[h % 2]
            em.op("act", (lambda bank=bank, off=off, stg=stg: lambda e: e.copy(stg, bank[:, off:off + 512]))(),
                  [bb], [stage1_b], f"p1ke{t8}_{h}")
            em.op("dve", (lambda stg=stg, r1_=r1_, cp=cp: lambda e: e.tensor_tensor(
                r1_[:], stg, cp[:], ALU.mult))(),
                [stage1_b, cpb], [r1b], f"p1r1{t8}_{h}")
            em.op("act", (lambda h=h, kro=kro: lambda e: e.copy(
                kro[0:64, :], stage1[64:128, (h % 2) * 512:(h % 2) * 512 + 512]))(),
                [stage1_b], [krob], f"p1ra{t8}_{h}")
            em.op("act", (lambda h=h, kro=kro: lambda e: e.copy(
                kro[64:128, :], stage1[0:64, (h % 2) * 512:(h % 2) * 512 + 512]))(),
                [stage1_b], [krob], f"p1rb{t8}_{h}")
            em.op("dve", (lambda kro=kro, r2_=r2_, sp_=sp_: lambda e: e.tensor_tensor(
                r2_[:], kro[:], sp_[:], ALU.mult))(),
                [krob, spb], [r2b], f"p1rc{t8}_{h}")
            em.op("dve", (lambda kst=kst, r1_=r1_, r2_=r2_: lambda e: e.tensor_tensor(
                kst[:], r1_[:], r2_[:], ALU.add))(),
                [r1b, r2b], [kstb], f"p1ro{t8}_{h}")
            em.dma("pool", (lambda h=h, t0=t0, kst=kst: lambda e: e.dma_start(
                kscr[h, :, t0:t0 + 512], kst[:]))(),
                [kstb], [kscr_b], f"p1kw{t8}_{h}")
            em.op("dve", (lambda kst=kst, h=h: lambda e: e.tensor_tensor(
                stage0[:, (h % 2) * 512:(h % 2) * 512 + 512], kst[:], kst[:], ALU.mult))(),
                [kstb], [stage0_b], f"p1n2{t8}_{h}")
            em.op("pe", (lambda bank=bank, off=off, h=h: lambda e: e.matmul(
                bank[0:1, off:off + 512], ones_bf[:],
                stage0[:, (h % 2) * 512:(h % 2) * 512 + 512], start=True, stop=True))(),
                [stage0_b, ones_bf_b], [bb], f"p1nm{t8}_{h}")
            em.op("dve", (lambda bank=bank, off=off, t8=t8, h=h: lambda e: e.reduce_max(
                knorm[0:1, t8 * 8 + h:t8 * 8 + h + 1], bank[0:1, off:off + 512], axis=AX.X))(),
                [bb], [knorm_b], f"p1nx{t8}_{h}")
        if t8 + 1 < 8:
            p1_ln(t8 + 1)
            p1_fm(t8 + 1)
        for ic in range(32):
            wr, wrb = WR4[ic % 6]
            em.dma("sp", (lambda ic=ic, wr=wr: lambda e: e.dma_start(
                wr[:], wv_bf[ic * 128:(ic + 1) * 128, :]))(),
                [wv_cb[ic // 4]], [wrb], f"p1wv{t8}_{ic}")
            for ts in range(4):
                src, srcb = fmsl(grp_, ic, ts * 128, (ts + 1) * 128)
                for hf in range(2):
                    bank, off, bb = PS[ts * 2 + hf]
                    em.op("pe", (lambda ic=ic, ts=ts, hf=hf, wr=wr, bank=bank, off=off, src=src:
                                 lambda e: e.matmul(
                        bank[:, off:off + 512], src,
                        wr[:, hf * 512:(hf + 1) * 512], start=(ic == 0), stop=(ic == 31)))(),
                        [wrb, srcb], [bb], f"p1v{t8}_{ic}_{ts}_{hf}")
        for hw in range(4):
            hf, hp = hw // 2, (hw % 2) * 2
            for ts in range(4):
                bank, off, bb = PS[ts * 2 + hf]
                for hh in (hp, hp + 1):
                    em.op("act", (lambda bank=bank, off=off, hh=hh, hp=hp, ts=ts: lambda e: e.copy(
                        vstage[:, (hh - hp) * 512 + ts * 128: (hh - hp) * 512 + (ts + 1) * 128],
                        bank[:, off + hh * 128: off + (hh + 1) * 128]))(),
                        [bb], [vstage_b], f"p1ve{t8}_{ts}_{hf*4+hh}")
            for hh in (hp, hp + 1):
                h = hf * 4 + hh
                em.dma("act", (lambda h=h, hh=hh, hp=hp, t0=t0: lambda e: e.dma_start(
                    vscr[h, :, t0:t0 + 512], vstage[:, (hh - hp) * 512:(hh - hp + 1) * 512]))(),
                    [vstage_b], [vscr_b], f"p1vw{t8}_{h}")
        if t8 in P0_DEFER:
            P0_DEFER[t8]()

    def p2a_main(i):
        for dc in range(32):
            em.dma("act", (lambda dc=dc, i=i: lambda e: e.dma_start(
                fm0[:, dc * 512:(dc + 1) * 512],
                lnoscr[i * 512:(i + 1) * 512, dc * 128:(dc + 1) * 128], transpose=True))(),
                [lnoscr_tb[i]], [fm0_b], f"a_f{i}_{dc}")
        em.op("dve", lambda e: e.memset(qacc[:], 0.0), [], [qacc_b], f"a_q0{i}")
        em.dma("sp", (lambda i=i: lambda e: e.dma_start(
            cos_tm[:].rearrange("p (c d) -> p c d", c=4),
            cost_in[i * 512:(i + 1) * 512, :].rearrange("(c p) d -> p c d", p=128)))(),
            [cost_b], [cos_tm_b], f"a_ct{i}")
        em.dma("sp", (lambda i=i: lambda e: e.dma_start(
            sin_tm[:].rearrange("p (c d) -> p c d", c=4),
            sint_in[i * 512:(i + 1) * 512, :].rearrange("(c p) d -> p c d", p=128)))(),
            [sint_b], [sin_tm_b], f"a_st{i}")
        for qtr in range(4):
            for ic in range(32):
                wr, wrb = WR4[ic % 6]
                em.dma("sp", (lambda ic=ic, qtr=qtr, wr=wr: lambda e: e.dma_start(
                    wr[:], wq_bf[ic * 128:(ic + 1) * 128, qtr * 1024:(qtr + 1) * 1024]))(),
                    [wq_bf_b], [wrb], f"a_w{i}_{qtr}_{ic}")
                for ts in range(4):
                    for hf in range(2):
                        bank, off, bb = PS[ts * 2 + hf]
                        em.op("pe", (lambda ic=ic, ts=ts, hf=hf, wr=wr, bank=bank, off=off:
                                     lambda e: e.matmul(
                            bank[:, off:off + 512],
                            fm0[:, ic * 512 + ts * 128: ic * 512 + (ts + 1) * 128],
                            wr[:, hf * 512:(hf + 1) * 512], start=(ic == 0), stop=(ic == 31)))(),
                            [wrb, fm0_b], [bb], f"a_m{i}_{qtr}_{ic}_{ts}_{hf}")
            for ts in range(4):
                sub = ts
                for hf in range(2):
                    bank, off, bb = PS[ts * 2 + hf]
                    em.op("act", (lambda bank=bank, off=off, hf=hf: lambda e: e.copy(
                        stage0[:, hf * 512:(hf + 1) * 512], bank[:, off:off + 512]))(),
                        [bb], [stage0_b], f"a_e{i}_{qtr}_{ts}_{hf}")
                cs = cos_tm[:, sub * 128:(sub + 1) * 128]
                sn = sin_tm[:, sub * 128:(sub + 1) * 128]
                for hb in range(8):
                    c0 = hb * 128
                    em.op("dve", (lambda c0=c0, cs=cs: lambda e: e.tensor_tensor(
                        rt1[:, 0:128], stage0[:, c0:c0 + 128], cs, ALU.mult))(),
                        [stage0_b, cos_tm_b], [rt1_b], f"a_r1{i}_{qtr}_{ts}_{hb}")
                    em.op("dve", (lambda c0=c0, sn=sn: lambda e: e.tensor_tensor(
                        rt2[:, 0:64], stage0[:, c0 + 64:c0 + 128], sn[:, 0:64], ALU.mult))(),
                        [stage0_b, sin_tm_b], [rt2_b], f"a_r2{i}_{qtr}_{ts}_{hb}")
                    em.op("dve", (lambda c0=c0, sn=sn: lambda e: e.tensor_tensor(
                        rt2[:, 64:128], stage0[:, c0:c0 + 64], sn[:, 64:128], ALU.mult))(),
                        [stage0_b, sin_tm_b], [rt2_b], f"a_r3{i}_{qtr}_{ts}_{hb}")
                    em.op("dve", (lambda c0=c0: lambda e: e.tensor_tensor(
                        stage1[:, c0:c0 + 128], rt1[:, 0:128], rt2[:, 0:128], ALU.add))(),
                        [rt1_b, rt2_b], [stage1_b], f"a_ro{i}_{qtr}_{ts}_{hb}")
                em.dma("pool", (lambda i=i, ts=ts, qtr=qtr: lambda e: e.dma_start(
                    qscr[i * 512 + ts * 128:i * 512 + (ts + 1) * 128, qtr * 1024:(qtr + 1) * 1024],
                    stage1[:]))(),
                    [stage1_b], [qscr_b], f"a_qs{i}_{qtr}_{ts}")
                em.op("dve", lambda e: e.tensor_tensor(pt[0][0][:], stage1[:], stage1[:], ALU.mult),
                      [stage1_b], [pt[0][1]], f"a_n2{i}_{qtr}_{ts}")
                em.op("dve", lambda e: e.reduce_sum(gsm[:, 0:8],
                                                    pt[0][0][:].rearrange("p (s d) -> p s d", s=8),
                                                    axis=AX.X),
                      [pt[0][1]], [gsm_b], f"a_nr{i}_{qtr}_{ts}")
                em.op("dve", (lambda qtr=qtr: lambda e: e.tensor_tensor(
                    qacc[:, qtr * 8:(qtr + 1) * 8], qacc[:, qtr * 8:(qtr + 1) * 8],
                    gsm[:, 0:8], ALU.max))(),
                    [qacc_b, gsm_b], [qacc_b], f"a_nx{i}_{qtr}_{ts}")
        em.op("dve", lambda e: e.reduce_max(gsm[:, 16:17], qacc[:], axis=AX.X),
              [qacc_b], [gsm_b], f"a_fx{i}")
        em.op("pe", lambda e: e.transpose(pD[0:8, 0:128], gsm[:, 16:24], ident_sb[:]),
              [gsm_b, ident_sb_b], [pbufs["D0"]], f"a_tr{i}")
        em.op("dve", (lambda i=i: lambda e: e.reduce_max(qnorm[0:1, i:i + 1], pD[0:1, 0:128], axis=AX.X))(),
              [pbufs["D0"]], [qnorm_b], f"a_qn{i}")
        em.op("dve", (lambda i=i: lambda e: e.reduce_max(tiny[0:1, 0:1],
                                                         knorm[0:1, 0:(2 * i + 2) * 8], axis=AX.X))(),
              [knorm_b], [tiny_b], f"b_k{i}")
        em.op("dve", (lambda i=i: lambda e: e.tensor_tensor(tiny[0:1, 1:2], tiny[0:1, 0:1],
                                                            qnorm[0:1, i:i + 1], ALU.mult))(),
              [tiny_b, qnorm_b], [tiny_b], f"b_m{i}")
        em.op("act", lambda e: e.activation(tiny[0:1, 2:3], tiny[0:1, 1:2], AF.Ln),
              [tiny_b], [tiny_b], f"b_l{i}")
        em.op("act", lambda e: e.activation(tiny[0:1, 3:4], tiny[0:1, 2:3], AF.Exp, scale=0.5),
              [tiny_b], [tiny_b], f"b_e{i}")
        em.op("dve", lambda e: e.tensor_scalar(tiny[0:1, 4:5], tiny[0:1, 3:4], -ISQ, None, ALU.mult),
              [tiny_b], [tiny_b], f"b_s{i}")
        em.op("pe", lambda e: e.matmul(pD[:, 512:513], onesr_f[:], tiny[0:1, 4:5],
                                       start=True, stop=True),
              [tiny_b, onesr_f_b], [pbufs["D1"]], f"b_b{i}")
        em.op("act", lambda e: e.copy(biasc[:], pD[:, 512:513]), [pbufs["D1"]], [biasc_b], f"b_c{i}")
        if dbg and i == 0:
            em.dma("sp", lambda e: e.dma_start(dbg_biasc, biasc[:]), [biasc_b], [dbg_bufs[0]], "dbgb")

    def p2b(i, nch, pairs):
        for h in range(NKVH):
            khh, khb = kh[h % 2]
            vhh, vhb = vh[h % 2]
            em.dma("sp", (lambda h=h, khh=khh, nch=nch: lambda e: e.dma_start(
                khh[:, 0:nch * 128], kscr[h, :, 0:nch * 128]))(),
                [kscr_b], [khb], f"t_k{i}_{h}")
            em.dma("sp", (lambda h=h, vhh=vhh, nch=nch: lambda e: e.dma_start(
                vhh[:, 0:nch * 128], vscr[h, :, 0:nch * 128]))(),
                [vscr_b], [vhb], f"t_v{i}_{h}")
            for g in range(GRP):
                hq = h * GRP + g
                qtt, qtb = qt[g % 2]
                em.dma("act", (lambda i=i, hq=hq, qtt=qtt: lambda e: e.dma_start(
                    qtt[:], qscr[i * 512:(i + 1) * 512, hq * 128:(hq + 1) * 128], transpose=True))(),
                    [qscr_b], [qtb], f"t_q{i}_{hq}")
                def score_pair(pr, khh=khh, khb=khb, qtt=qtt, qtb=qtb):
                    c0, c1 = 2 * pr, 2 * pr + 1
                    sbank, sb0, sb1 = (pA, pbufs["A0"], pbufs["A1"]) if pr % 2 == 0 else \
                                      (pB, pbufs["B0"], pbufs["B1"])
                    ptt, ptb = pt[pr % 2]
                    em.op("pe", (lambda c0=c0, sbank=sbank, khh=khh, qtt=qtt: lambda e: e.matmul(
                        sbank[:, 0:512], khh[:, c0 * 128:(c0 + 1) * 128], qtt[:],
                        start=True, stop=True))(),
                        [khb, qtb], [sb0], f"t_s{i}_{hq}_{c0}")
                    em.op("pe", (lambda c1=c1, sbank=sbank, khh=khh, qtt=qtt: lambda e: e.matmul(
                        sbank[:, 512:1024], khh[:, c1 * 128:(c1 + 1) * 128], qtt[:],
                        start=True, stop=True))(),
                        [khb, qtb], [sb1], f"t_s{i}_{hq}_{c1}")
                    em.op("act", (lambda sbank=sbank, ptt=ptt: lambda e: e.activation(
                        ptt[:], sbank[:], AF.Exp, bias=biasc[:], scale=ISQ))(),
                        [sb0, sb1, biasc_b], [ptb], f"t_e{i}_{hq}_{pr}")
                    if pr >= 4 * i:
                        fr = pr - 4 * i
                        em.op("dve", (lambda fr=fr, ptt=ptt: lambda e: e.tensor_tensor(
                            ptt[:], ptt[:], masks_sb[:, fr * 1024:(fr + 1) * 1024], ALU.mult))(),
                            [ptb, masks_sb_b], [ptb], f"t_m{i}_{hq}_{pr}")

                def consume_pair(pr, vhh=vhh, vhb=vhb, pairs=pairs):
                    c0, c1 = 2 * pr, 2 * pr + 1
                    ptt, ptb = pt[pr % 2]
                    em.op("pe", (lambda pr=pr, ptt=ptt: lambda e: e.matmul(
                        pD[0:1, 0:512], ones_bf[:], ptt[:, 0:512],
                        start=(pr == 0), stop=False))(),
                        [ptb, ones_bf_b], [pbufs["D0"]], f"t_l0{i}_{hq}_{pr}")
                    em.op("pe", (lambda pr=pr, ptt=ptt, pairs=pairs: lambda e: e.matmul(
                        pD[0:1, 0:512], ones_bf[:], ptt[:, 512:1024],
                        start=False, stop=(pr == pairs - 1)))(),
                        [ptb, ones_bf_b], [pbufs["D0"]], f"t_l1{i}_{hq}_{pr}")
                    em.op("pe", (lambda pr=pr, c0=c0, ptt=ptt, vhh=vhh: lambda e: e.matmul(
                        pC[:, 0:512], vhh[:, c0 * 128:(c0 + 1) * 128], ptt[:, 0:512],
                        start=(pr == 0), stop=False))(),
                        [vhb, ptb], [pbufs["C0"]], f"t_c0{i}_{hq}_{pr}")
                    em.op("pe", (lambda pr=pr, c1=c1, ptt=ptt, vhh=vhh, pairs=pairs: lambda e: e.matmul(
                        pC[:, 0:512], vhh[:, c1 * 128:(c1 + 1) * 128], ptt[:, 512:1024],
                        start=False, stop=(pr == pairs - 1)))(),
                        [vhb, ptb], [pbufs["C0"]], f"t_c1{i}_{hq}_{pr}")

                for pr in range(pairs):
                    score_pair(pr)
                    if pr > 0:
                        consume_pair(pr - 1)
                consume_pair(pairs - 1)
                em.op("dve", lambda e: e.tensor_copy(l_sb[:], pD[0:1, 0:512]),
                      [pbufs["D0"]], [l_sb_b], f"t_le{i}_{hq}")
                if dbg and i == 0 and hq == 0:
                    em.dma("sp", lambda e: e.dma_start(dbg_l, l_sb[:]), [l_sb_b], [dbg_bufs[2]], "dbgl")
                em.op("dve", lambda e: e.reciprocal(linv[:], l_sb[:]), [l_sb_b], [linv_b], f"t_li{i}_{hq}")
                em.op("pe", lambda e: e.matmul(pD[:, 512:1024], onesr_f[:], linv[:],
                                               start=True, stop=True),
                      [linv_b, onesr_f_b], [pbufs["D1"]], f"t_lb{i}_{hq}")
                em.op("act", lambda e: e.copy(linvb[:], pD[:, 512:1024]),
                      [pbufs["D1"]], [linvb_b], f"t_lv{i}_{hq}")
                em.op("dve", (lambda hq=hq: lambda e: e.tensor_tensor(
                    fm0[:, hq * 512:(hq + 1) * 512], pC[:, 0:512], linvb[:], ALU.mult))(),
                    [pbufs["C0"], linvb_b], [fm0_b], f"t_cn{i}_{hq}")
                if dbg and i == 0 and hq == 0:
                    em.dma("sp", lambda e: e.dma_start(dbg_ctx, fm0[:, 0:512]), [fm0_b], [dbg_bufs[3]], "dbgc")

    def p2c_wo(i):
        for qtr in range(4):
            for dc in range(32):
                wr, wrb = WR4[dc % 6]
                em.dma("sp", (lambda dc=dc, qtr=qtr, wr=wr: lambda e: e.dma_start(
                    wr[:], wo_bf[dc * 128:(dc + 1) * 128, qtr * 1024:(qtr + 1) * 1024]))(),
                    [wo_bf_b], [wrb], f"c_w{i}_{qtr}_{dc}")
                for ts in range(4):
                    for hf in range(2):
                        bank, off, bb = PS[ts * 2 + hf]
                        em.op("pe", (lambda dc=dc, ts=ts, hf=hf, wr=wr, bank=bank, off=off:
                                     lambda e: e.matmul(
                            bank[:, off:off + 512],
                            fm0[:, dc * 512 + ts * 128: dc * 512 + (ts + 1) * 128],
                            wr[:, hf * 512:(hf + 1) * 512], start=(dc == 0), stop=(dc == 31)))(),
                            [wrb, fm0_b], [bb], f"c_m{i}_{qtr}_{dc}_{ts}_{hf}")
            for ts in range(4):
                for hf in range(2):
                    bank, off, bb = PS[ts * 2 + hf]
                    seg = qtr * 2 + hf
                    em.op("act", (lambda bank=bank, off=off, ts=ts, qtr=qtr, hf=hf, seg=seg:
                                  lambda e: e.activation(
                        tsl[ts][0][:, qtr * 1024 + hf * 512: qtr * 1024 + (hf + 1) * 512],
                        bank[:, off:off + 512], AF.Copy,
                        accum_out=sm0[:, ts * 16 + seg:ts * 16 + seg + 1]))(),
                        [bb], [tsl[ts][1], sm0_b], f"c_e{i}_{qtr}_{ts}_{hf}")
                    em.op("act", (lambda bank=bank, off=off, ts=ts, seg=seg: lambda e: e.activation(
                        lnout[:, 0:512], bank[:, off:off + 512], AF.Square,
                        accum_out=sm0[:, ts * 16 + 8 + seg:ts * 16 + 8 + seg + 1]))(),
                        [bb], [lnout_b, sm0_b], f"c_q{i}_{qtr}_{ts}_{hf}")
        for ts in range(4):
            r0 = i * 512 + ts * 128
            rr, nm = ln_finish(ts * 16, ts * 16 + 8, f"c_L{i}_{ts}")
            em.op("act", (lambda ts=ts, rr=rr, nm=nm: lambda e: e.activation(
                lnout[:], tsl[ts][0][:], AF.Identity, bias=nm, scale=rr))(),
                [tsl[ts][1], gsm_b], [lnout_b], f"c_a{i}_{ts}")
            em.dma("pool", (lambda r0=r0: lambda e: e.dma_start(slab_a[:], xo_in[r0:r0 + 128, :]))(),
                   [xo_in_b], [slab_a_b], f"c_x{i}_{ts}")
            em.op("dve", lambda e: e.tensor_tensor(slab_b[:], slab_a[:], lnout[:], ALU.add),
                  [slab_a_b, lnout_b], [slab_b_b], f"c_r{i}_{ts}")
            em.dma("pool", (lambda ts=ts: lambda e: e.dma_start(
                x1scr[ts * 128:(ts + 1) * 128, :], slab_b[:]))(),
                [slab_b_b], [x1scr_b], f"c_1{i}_{ts}")
            ln_tile(slab_b[:], slab_b_b, lnout[:], lnout_b, f"c_N{i}_{ts}")
            em.dma("act", (lambda ts=ts: lambda e: e.dma_start(
                ln4scr[ts * 128:(ts + 1) * 128, :], lnout[:]))(),
                [lnout_b], [ln4scr_b], f"c_4{i}_{ts}")
    def p2c_ffn(i):
        for dc in range(32):
            dst, dstb = fmsl(1, dc, 0, 512)
            em.dma("act", (lambda dc=dc, dst=dst: lambda e: e.dma_start(
                dst, ln4scr[0:512, dc * 128:(dc + 1) * 128], transpose=True))(),
                [ln4scr_b], [dstb], f"c_f{i}_{dc}")
        for ic in range(32):
            ar, arb = (augr0, augr0_b) if ic % 2 == 0 else (augr1, augr1_b)
            em.dma("sp", (lambda ic=ic, ar=ar: lambda e: e.dma_start(
                ar[:], lataug[ic * 128:(ic + 1) * 128, :]))(),
                [lataug_b], [arb], f"f_a{i}_{ic}")
            src, srcb = fmsl(1, ic, 0, 512)
            em.op("pe", (lambda ic=ic, ar=ar, src=src: lambda e: e.matmul(
                pA[:, 0:512], ar[:, 0:128], src,
                start=(ic == 0), stop=(ic == 31)))(),
                [arb, srcb], [pbufs["A0"]], f"f_h1{i}_{ic}")
            em.op("pe", (lambda ic=ic, ar=ar, src=src: lambda e: e.matmul(
                pA[:, 512:1024], ar[:, 128:256], src,
                start=(ic == 0), stop=(ic == 31)))(),
                [arb, srcb], [pbufs["A1"]], f"f_h2{i}_{ic}")
        em.op("act", lambda e: e.copy(lgt[:], pA[0:NE, 0:512]), [pbufs["A0"]], [lgt_b], f"f_lg{i}")
        for ts in range(4):
            em.op("pe", (lambda ts=ts: lambda e: e.transpose(
                pC[0:128, 0:NE], lgt[:, ts * 128:(ts + 1) * 128], ident_sb[0:NE, 0:NE]))(),
                [lgt_b, ident_sb_b], [pbufs["C0"]], f"f_t{i}_{ts}")
            m1, m2 = gsm[:, 20:21], gsm[:, 21:22]
            d12, ee, p1 = gsm[:, 22:23], gsm[:, 23:24], gsm[:, 24:25]
            p2 = gsm[:, 25:26]
            em.op("dve", lambda e: e.reduce_max(m1, pC[0:128, 0:NE], axis=AX.X),
                  [pbufs["C0"]], [gsm_b], f"f_m1{i}_{ts}")
            em.op("dve", lambda e: e.tensor_scalar(gtmp[:], pC[0:128, 0:NE], m1, None, ALU.is_ge),
                  [pbufs["C0"], gsm_b], [gtmp_b], f"f_i1{i}_{ts}")
            em.op("dve", lambda e: e.scalar_tensor_tensor(gtmp2[:], gtmp[:], -1e30, pC[0:128, 0:NE],
                                                          ALU.mult, ALU.add),
                  [gtmp_b, pbufs["C0"]], [gtmp2_b], f"f_msk{i}_{ts}")
            em.op("dve", lambda e: e.reduce_max(m2, gtmp2[:], axis=AX.X), [gtmp2_b], [gsm_b], f"f_m2{i}_{ts}")
            em.op("dve", lambda e: e.tensor_scalar(gtmp2[:], gtmp2[:], m2, None, ALU.is_ge),
                  [gtmp2_b, gsm_b], [gtmp2_b], f"f_i2{i}_{ts}")
            em.op("dve", lambda e: e.tensor_tensor(d12, m1, m2, ALU.subtract), [gsm_b], [gsm_b], f"f_d{i}_{ts}")
            em.op("act", lambda e: e.activation(ee, d12, AF.Exp, scale=-1.0), [gsm_b], [gsm_b], f"f_e{i}_{ts}")
            em.op("dve", lambda e: e.tensor_scalar(ee, ee, 1.0, None, ALU.add), [gsm_b], [gsm_b], f"f_e1{i}_{ts}")
            em.op("dve", lambda e: e.reciprocal(p1, ee), [gsm_b], [gsm_b], f"f_p1{i}_{ts}")
            em.op("dve", lambda e: e.tensor_scalar(p2, p1, -1.0, 1.0, ALU.mult, ALU.add),
                  [gsm_b], [gsm_b], f"f_p2{i}_{ts}")
            em.op("dve", lambda e: e.tensor_scalar(gate_tm[:], gtmp[:], p1, None, ALU.mult),
                  [gtmp_b, gsm_b], [gate_tm_b], f"f_g1{i}_{ts}")
            em.op("dve", lambda e: e.scalar_tensor_tensor(gate_tm[:], gtmp2[:], p2, gate_tm[:],
                                                          ALU.mult, ALU.add),
                  [gtmp2_b, gsm_b, gate_tm_b], [gate_tm_b], f"f_g2{i}_{ts}")
            em.op("pe", (lambda ts=ts: lambda e: e.transpose(
                pD[0:NE, 0:128], gate_tm[:], ident_sb[:]))(),
                [gate_tm_b, ident_sb_b], [pbufs["D0"]], f"f_gt{i}_{ts}")
            em.op("act", (lambda ts=ts: lambda e: e.copy(g28[:, ts * 128:(ts + 1) * 128],
                                                         pD[0:NE, 0:128]))(),
                  [pbufs["D0"]], [g28_b], f"f_ge{i}_{ts}")
        em.op("pe", lambda e: e.matmul(pC[0:128, 0:512], r1_sb[:], g28[:], start=True, stop=True),
              [r1_sb_b, g28_b], [pbufs["C0"]], f"f_G1{i}")
        em.op("pe", lambda e: e.matmul(pC[0:128, 512:1024], r2_sb[:], g28[:], start=True, stop=True),
              [r2_sb_b, g28_b], [pbufs["C1"]], f"f_G2{i}")
        em.op("act", lambda e: e.copy(gf1[:], pC[0:128, 0:512]), [pbufs["C0"]], [gf1_b], f"f_ge1{i}")
        em.op("act", lambda e: e.copy(gf2[:], pC[0:128, 512:1024]), [pbufs["C1"]], [gf2_b], f"f_ge2{i}")
        em.op("dve", lambda e: e.tensor_tensor(h1p[:], pA[0:128, 0:512], gf1[:], ALU.mult),
              [pbufs["A0"], gf1_b], [h1p_b], f"f_hp1{i}")
        em.op("dve", lambda e: e.tensor_tensor(h2p[:], pA[0:128, 512:1024], gf2[:], ALU.mult),
              [pbufs["A1"], gf2_b], [h2p_b], f"f_hp2{i}")
        for qtr in range(4):
            em.dma("sp", (lambda qtr=qtr: lambda e: e.dma_start(
                lbr1[32:128, :], lbsc[0:96, qtr * 1024:(qtr + 1) * 1024]))(),
                [lbsc_b], [lbr1_b], f"f_b1{i}_{qtr}")
            em.dma("sp", (lambda qtr=qtr: lambda e: e.dma_start(
                lbr2[:], lbsc[96:224, qtr * 1024:(qtr + 1) * 1024]))(),
                [lbsc_b], [lbr2_b], f"f_b2{i}_{qtr}")
            for dc in range(32):
                wr, wrb = WR4[dc % 6]
                em.dma("sp", (lambda dc=dc, qtr=qtr, wr=wr: lambda e: e.dma_start(
                    wr[:], bwT_bf[dc * 128:(dc + 1) * 128, qtr * 1024:(qtr + 1) * 1024]))(),
                    [bwT_bf_b], [wrb], f"f_w{i}_{qtr}_{dc}")
                for ts in range(4):
                    src, srcb = fmsl(1, dc, ts * 128, (ts + 1) * 128)
                    for hf in range(2):
                        bank, off, bb = PS[ts * 2 + hf]
                        em.op("pe", (lambda dc=dc, ts=ts, hf=hf, wr=wr, bank=bank, off=off, src=src:
                                     lambda e: e.matmul(
                            bank[:, off:off + 512], src,
                            wr[:, hf * 512:(hf + 1) * 512], start=(dc == 0), stop=False))(),
                            [wrb, srcb], [bb], f"f_m{i}_{qtr}_{dc}_{ts}_{hf}")
            for ts in range(4):
                for hf in range(2):
                    bank, off, bb = PS[ts * 2 + hf]
                    em.op("pe", (lambda ts=ts, hf=hf, bank=bank, off=off: lambda e: e.matmul(
                        bank[:, off:off + 512], h1p[:, ts * 128:(ts + 1) * 128],
                        lbr1[:, hf * 512:(hf + 1) * 512], start=False, stop=False))(),
                        [h1p_b, lbr1_b], [bb], f"f_l1{i}_{qtr}_{ts}_{hf}")
                    em.op("pe", (lambda ts=ts, hf=hf, bank=bank, off=off: lambda e: e.matmul(
                        bank[:, off:off + 512], h2p[:, ts * 128:(ts + 1) * 128],
                        lbr2[:, hf * 512:(hf + 1) * 512], start=False, stop=True))(),
                        [h2p_b, lbr2_b], [bb], f"f_l2{i}_{qtr}_{ts}_{hf}")
                    seg = qtr * 2 + hf
                    em.op("act", (lambda bank=bank, off=off, ts=ts, qtr=qtr, hf=hf, seg=seg:
                                  lambda e: e.activation(
                        tsl[ts][0][:, qtr * 1024 + hf * 512: qtr * 1024 + (hf + 1) * 512],
                        bank[:, off:off + 512], AF.Copy,
                        accum_out=sm0[:, ts * 16 + seg:ts * 16 + seg + 1]))(),
                        [bb], [tsl[ts][1], sm0_b], f"f_e{i}_{qtr}_{ts}_{hf}")
                    em.op("act", (lambda bank=bank, off=off, ts=ts, seg=seg: lambda e: e.activation(
                        lnout[:, 0:512], bank[:, off:off + 512], AF.Square,
                        accum_out=sm0[:, ts * 16 + 8 + seg:ts * 16 + 8 + seg + 1]))(),
                        [bb], [lnout_b, sm0_b], f"f_q{i}_{qtr}_{ts}_{hf}")
        for ts in range(4):
            r0 = i * 512 + ts * 128
            rr, nm = ln_finish(ts * 16, ts * 16 + 8, f"o_L{i}_{ts}")
            em.op("act", (lambda ts=ts, rr=rr, nm=nm: lambda e: e.activation(
                lnout[:], tsl[ts][0][:], AF.Identity, bias=nm, scale=rr))(),
                [tsl[ts][1], gsm_b], [lnout_b], f"o_a{i}_{ts}")
            em.dma("pool", (lambda ts=ts: lambda e: e.dma_start(
                slab_a[:], x1scr[ts * 128:(ts + 1) * 128, :]))(),
                [x1scr_b], [slab_a_b], f"o_x{i}_{ts}")
            em.op("dve", lambda e: e.tensor_tensor(slab_b[:], slab_a[:], lnout[:], ALU.add),
                  [slab_a_b, lnout_b], [slab_b_b], f"o_r{i}_{ts}")
            em.dma("pool", (lambda r0=r0: lambda e: e.dma_start(out_ap[r0:r0 + 128, :], slab_b[:]))(),
                   [slab_b_b], [out_b], f"o_o{i}_{ts}")

    p2a_main(0)
    p0_bwT()
    p0_la()
    for i in range(NT):
        nch = 8 * (i + 1)
        p2b(i, nch, nch // 2)
        p2c_wo(i)
        if i + 1 < NT:
            p2a_main(i + 1)
        p2c_ffn(i)

    tap_map = {"lnscr": (lnscr, lnscr_tb, [T, D], BF16), "kscr": (kscr, [kscr_b], [NKVH, 128, T], BF16),
               "vscr": (vscr, [vscr_b], [NKVH, 128, T], BF16), "qscr": (qscr, [qscr_b], [2048, D], BF16),
               "lnoscr": (lnoscr, lnoscr_tb, [2048, D], BF16),
               "lataug": (lataug, [lataug_b], [D, 256], BF16), "lbsc": (lbsc, [lbsc_b], [NE * RANK, D], BF16),
               "bwT": (bwT_bf, [bwT_bf_b], [D, D], BF16),
               "x1scr": (x1scr, [x1scr_b], [512, D], F32),
               "ln4scr": (ln4scr, [ln4scr_b], [512, D], BF16)}
    for tname in taps:
        ap, bufs_, shp, dt = tap_map[tname]
        tout = nc.dram_tensor("tap_" + tname, shp, dt, kind="ExternalOutput").ap()
        tout_b = em.buf("tap_" + tname)
        em.dma("sp", (lambda ap=ap, tout=tout: lambda e: e.dma_start(tout, ap))(),
               list(bufs_), [tout_b], f"tap_{tname}")
        em.barrier("sp", [tout_b], f"tapw_{tname}")

    if dbg:
        em.barrier("sp", dbg_bufs, "final_dbg")
    em.barrier("sp", [out_b], "final_out")

    em.finalize(nc)
    with nc.Block() as blk:
        @blk.tensor
        def _(t):
            em.emit(t, "pe")

        @blk.scalar
        def _(s):
            em.emit(s, "act")

        @blk.vector
        def _(v):
            em.emit(v, "dve")

        @blk.gpsimd
        def _(g):
            em.emit(g, "pool")

        @blk.sync
        def _(s):
            em.emit(s, "sp")
    return nc


def _rope_tables():
    idx = (np.arange(0, 128, 2, dtype=np.float32) / np.float32(128.0))
    theta = np.float32(500000.0)
    freq = (np.float32(1.0 / (2.0 * np.pi)) * theta ** (-idx)).astype(np.float32)
    factor, lo, hi = np.float32(32.0), np.float32(1.0), np.float32(4.0)
    L0 = np.float32(8192.0)
    freq_low, freq_high = lo / L0, hi / L0
    freq_scaled = np.where(freq < freq_low, freq / factor, freq)
    smooth = np.clip((L0 * freq - lo) / (hi - lo), 0.0, 1.0).astype(np.float32)
    freq_smooth = ((1.0 - smooth) * (freq / factor) + smooth * freq).astype(np.float32)
    is_mid = (freq >= freq_low) & (freq <= freq_high)
    freqf = np.where(is_mid, freq_smooth, freq_scaled).astype(np.float32)
    pos = np.arange(T, dtype=np.float32)
    phase = (np.float32(2.0 * np.pi) * pos[:, None] * freqf[None, :]).astype(np.float32)
    emb = np.concatenate([phase, phase], axis=-1)
    return np.cos(emb).astype(np.float32), np.sin(emb).astype(np.float32)


def _own_rows(role):
    rows = []
    for i in range(NT):
        q0 = 1024 * i + 512 * role
        rows.extend(range(q0, q0 + 512))
    return np.asarray(rows)


def _masks(role):
    m = np.zeros((4, 128, 1024), dtype=np.float32)
    q = np.arange(512)
    kk = np.arange(128)
    for fr in range(4):
        for hh in range(2):
            f = 2 * fr + hh
            valid = (128 * f + kk[:, None]) <= (512 * role + q[None, :])
            m[fr][:, hh * 512:(hh + 1) * 512] = valid.astype(np.float32)
    return m.astype(ml_dtypes.bfloat16)


_BUILD_CACHE = {}


def prepare_in_maps(L0_attn_resid, L0_attn_Wq, L0_attn_Wk, L0_attn_Wv, L0_attn_Wo,
                    L0_ffn_base_W, L0_ffn_gate_W, L0_ffn_lora_A, L0_ffn_lora_B):
    x = np.ascontiguousarray(L0_attn_resid, dtype=np.float32)
    cosT, sinT = _rope_tables()
    sinm_fm = sinT.T.copy()
    sinm_fm[0:64, :] *= -1.0
    cos_fm = cosT.T.copy()
    ident = np.eye(128, dtype=np.float32)
    r1 = np.zeros((NE, 128), dtype=np.float32)
    for j in range(96):
        r1[j // 8, 32 + j] = 2.0
    r2 = np.zeros((NE, 128), dtype=np.float32)
    for j in range(128):
        r2[12 + j // 8, j] = 2.0

    la = np.ascontiguousarray(L0_ffn_lora_A, dtype=np.float32).reshape(NE * RANK, D)
    lb = np.ascontiguousarray(L0_ffn_lora_B, dtype=np.float32).reshape(NE * D, RANK)

    bf = ml_dtypes.bfloat16
    common = dict(
        wq_in=np.ascontiguousarray(L0_attn_Wq, np.float32),
        wk_in=np.ascontiguousarray(L0_attn_Wk, np.float32),
        wv_in=np.ascontiguousarray(L0_attn_Wv, np.float32),
        wo_in=np.ascontiguousarray(L0_attn_Wo, np.float32),
        bw_in=np.ascontiguousarray(L0_ffn_base_W, np.float32),
        gw_in=np.ascontiguousarray(L0_ffn_gate_W, np.float32),
        la_in=la, lb_in=lb,
        cosf=cos_fm.astype(bf), sinf=sinm_fm.astype(bf),
        ident=ident, r1c=r1.astype(bf), r2c=r2.astype(bf),
    )
    in_maps = []
    own = [_own_rows(r) for r in (0, 1)]
    for core in range(8):
        s, r = core // 2, core % 2
        rows = own[r]
        sin_own = sinT[rows].copy()
        sin_own[:, 0:64] *= -1.0
        m = dict(common)
        m["x_in"] = x[s]
        m["xo_in"] = np.ascontiguousarray(x[s][rows])
        m["cost"] = cosT[rows].astype(bf)
        m["sint"] = sin_own.astype(bf)
        m["masks"] = _masks(r)
        in_maps.append(m)
    return in_maps, own


def kernel(**inputs):
    in_maps, own = prepare_in_maps(**inputs)
    if "nc" not in _BUILD_CACHE:
        _BUILD_CACHE["nc"] = build_program()
    nc = _BUILD_CACHE["nc"]
    res = run_bass_kernel_spmd(nc, in_maps, core_ids=list(range(8)))
    out = np.empty((B, T, D), dtype=np.float32)
    for core in range(8):
        s, r = core // 2, core % 2
        out[s][own[r]] = res.results[core]["out"]
    return out


# version 17
# speedup vs baseline: 1.0393x; 1.0393x over previous
import numpy as np
import ml_dtypes
import concourse.bass as bass
import concourse.mybir as mybir
from concourse.bass_utils import run_bass_kernel_spmd

F32 = mybir.dt.float32
BF16 = mybir.dt.bfloat16
AF = mybir.ActivationFunctionType
ALU = mybir.AluOpType
AX = mybir.AxisListType

B, T, D = 4, 4096, 4096
NQH, NKVH, DH = 32, 8, 128
GRP = NQH // NKVH
NT = 4
NE, RANK = 28, 8
EPS = 1e-5
ISQ = float(1.0 / np.sqrt(128.0))


class Buf:
    __slots__ = ("name", "last_w", "readers", "sem_name", "dma_count", "sink")

    def __init__(self, name, sink=False):
        self.name = name
        self.last_w = None
        self.readers = []
        self.sem_name = None
        self.dma_count = 0
        self.sink = sink


class Em:
    def __init__(self):
        self.ops = []

    def buf(self, name, sink=False):
        return Buf(name, sink)

    def op(self, engine, fn, reads=(), writes=(), tag=""):
        assert engine in ("pe", "act", "dve", "pool"), engine
        self._rec(engine, fn, reads, writes, tag, False)

    def dma(self, queue, fn, reads=(), writes=(), tag=""):
        assert queue in ("sp", "act", "pool"), queue
        self._rec(queue, fn, reads, writes, tag, True)

    def barrier(self, engine, bufs, tag="bar"):
        self._rec(engine, None, bufs, (), tag, False)

    def _rec(self, engine, fn, reads, writes, tag, dma):
        idx = len(self.ops)
        deps = set()
        for b in reads:
            if b.sink:
                continue
            if b.last_w is not None:
                deps.add(b.last_w)
        for b in writes:
            if b.sink:
                continue
            deps.update(b.readers)
            if b.last_w is not None:
                deps.add(b.last_w)
        for b in reads:
            if not b.sink:
                b.readers.append(idx)
        for b in writes:
            if not b.sink:
                b.last_w = idx
                b.readers = []
        self.ops.append(dict(engine=engine, fn=fn, deps=sorted(deps), dma=dma,
                             tag=tag, idx=idx, marked=False,
                             writes=[b for b in writes if not b.sink]))

    def finalize(self, nc):
        ops = self.ops
        for o in ops:
            for j in o["deps"]:
                ops[j]["marked"] = True
        eng_sem = {e: nc.alloc_semaphore(f"es_{e}") for e in ("pe", "act", "dve", "pool", "sp")}
        self.eng_sem = eng_sem
        self.buf_sems = {}
        eng_cnt = {e: 0 for e in eng_sem}
        nsem = 0
        for o in ops:
            if o["dma"]:
                wb = o["writes"]
                assert wb, f"dma without tracked write: {o['tag']}"
                tgt = wb[0]
                if tgt.sem_name is None:
                    tgt.sem_name = f"ds{nsem}"
                    nsem += 1
                    self.buf_sems[tgt.sem_name] = nc.alloc_semaphore(tgt.sem_name)
                tgt.dma_count += 16
                o["sem_key"] = tgt.sem_name
                o["sem_val"] = tgt.dma_count
                for b in wb[1:]:
                    b.sem_name = tgt.sem_name
                    b.dma_count = tgt.dma_count
            else:
                if o["marked"]:
                    eng_cnt[o["engine"]] += 1
                    o["sem_val"] = eng_cnt[o["engine"]]
                else:
                    o["sem_val"] = None
        for o in ops:
            waits = []
            for j in o["deps"]:
                p = ops[j]
                if p["dma"]:
                    waits.append(("d", p["sem_key"], p["sem_val"]))
                else:
                    if p["engine"] == "pe" and o["engine"] == "pe" and not o["dma"]:
                        continue
                    assert p["sem_val"] is not None
                    waits.append(("e", p["engine"], p["sem_val"]))
            o["waits"] = waits

    def emit(self, eng_obj, engine):
        water = {}
        for o in self.ops:
            if o["engine"] != engine:
                continue
            for (kind, key, val) in o["waits"]:
                sem = self.buf_sems[key] if kind == "d" else self.eng_sem[key]
                wk = (kind, key)
                if water.get(wk, 0) >= val:
                    continue
                water[wk] = val
                eng_obj.wait_ge(sem, val)
            if o["fn"] is None:
                continue
            inst = o["fn"](eng_obj)
            if o["dma"]:
                inst.then_inc(self.buf_sems[o["sem_key"]], 16)
            elif o["sem_val"] is not None:
                inst.then_inc(self.eng_sem[engine], 1)


def build_program(taps=(), dbg=False):
    nc = bass.Bass("TRN2", target_bir_lowering=False)
    em = Em()

    def inp(name, shape, dtype=F32):
        return nc.dram_tensor(name, shape, dtype, kind="ExternalInput").ap(), em.buf(name)

    x_in, x_in_b = inp("x_in", [T, D])
    xo_in, xo_in_b = inp("xo_in", [2048, D])
    wq_in, wq_in_b = inp("wq_in", [D, D])
    wk_in, wk_in_b = inp("wk_in", [D, 1024])
    wv_in, wv_in_b = inp("wv_in", [D, 1024])
    wo_in, wo_in_b = inp("wo_in", [D, D])
    bwt_in, bwt_in_b = inp("bwt_in", [D, D])
    lataug_in, lataug_in_b = inp("lataug_in", [D, 256])
    lb2_in, lb2_in_b = inp("lb2_in", [NE * RANK, D])
    cosf_in, cosf_b = inp("cosf", [128, T], BF16)
    sinf_in, sinf_b = inp("sinf", [128, T], BF16)
    cost_in, cost_b = inp("cost", [2048, 128], BF16)
    sint_in, sint_b = inp("sint", [2048, 128], BF16)
    masks_in, masks_b = inp("masks", [4, 128, 1024], BF16)
    ident_in, ident_b = inp("ident", [128, 128], F32)
    r1_in, r1_b = inp("r1c", [NE, 128], BF16)
    r2_in, r2_b = inp("r2c", [NE, 128], BF16)

    out_ap = nc.dram_tensor("out", [2048, D], F32, kind="ExternalOutput").ap()
    out_b = em.buf("out")
    if dbg:
        dbg_biasc = nc.dram_tensor("dbg_biasc", [128, 1], F32, kind="ExternalOutput").ap()
        dbg_pt = nc.dram_tensor("dbg_pt", [128, 1024], BF16, kind="ExternalOutput").ap()
        dbg_l = nc.dram_tensor("dbg_l", [1, 512], F32, kind="ExternalOutput").ap()
        dbg_ctx = nc.dram_tensor("dbg_ctx", [128, 512], BF16, kind="ExternalOutput").ap()
        dbg_bufs = [em.buf(f"dbg{j}") for j in range(4)]

    def scr(name, shape, dtype):
        return nc.dram_tensor(name, shape, dtype, kind="Internal").ap(), em.buf(name)

    wq_bf, wq_bf_b = scr("wq_bf", [D, D], BF16)
    wk_bf, wk_bf_b = scr("wk_bf", [D, 1024], BF16)
    wv_bf, wv_bf_b = scr("wv_bf", [D, 1024], BF16)
    wo_bf, wo_bf_b = scr("wo_bf", [D, D], BF16)
    bwT_bf, bwT_bf_b = scr("bwT_bf", [D, D], BF16)
    lataug, lataug_b = scr("lataug", [D, 256], BF16)
    lbsc, lbsc_b = scr("lbsc", [NE * RANK, D], BF16)
    lnscr, _ = scr("lnscr", [T, D], BF16)
    lnscr_tb = [em.buf(f"lnscr{t}") for t in range(8)]
    lnoscr, _ = scr("lnoscr", [2048, D], BF16)
    lnoscr_tb = [em.buf(f"lnoscr{i}") for i in range(NT)]
    kscr, kscr_b = scr("kscr", [NKVH, 128, T], BF16)
    vscr, vscr_b = scr("vscr", [NKVH, 128, T], BF16)
    qscr, qscr_b = scr("qscr", [2048, D], BF16)
    ln4scr, _ = scr("ln4scr", [512, D], BF16)
    ln4scr_b = em.buf("ln4scr")
    x1scr, _ = scr("x1scr", [512, D], F32)
    x1scr_b = em.buf("x1scr")

    def sbuf(name, shape, dtype):
        return nc.alloc_sbuf_tensor(name, shape, dtype), em.buf(name)

    cosP = [sbuf("cosP0", [128, 512], BF16), sbuf("cosP1", [128, 512], BF16)]
    sinP = [sbuf("sinP0", [128, 512], BF16), sbuf("sinP1", [128, 512], BF16)]
    cos_tm, cos_tm_b = sbuf("cos_tm", [128, 4 * 128], BF16)
    sin_tm, sin_tm_b = sbuf("sin_tm", [128, 4 * 128], BF16)
    masks_sb, masks_sb_b = sbuf("masks_sb", [128, 4 * 1024], BF16)
    ident_sb, ident_sb_b = sbuf("ident_sb", [128, 128], F32)
    ones_bf, ones_bf_b = sbuf("ones_bf", [128, 1], BF16)
    onesr_f, onesr_f_b = sbuf("onesr_f", [1, 128], F32)
    cpool, cpool_b = sbuf("cpool", [128, 2], F32)
    r1_sb, r1_sb_b = sbuf("r1_sb", [NE, 128], BF16)
    r2_sb, r2_sb_b = sbuf("r2_sb", [NE, 128], BF16)
    knorm, knorm_b = sbuf("knorm", [1, 64], F32)
    qnorm, qnorm_b = sbuf("qnorm", [1, NT], F32)
    biasc, biasc_b = sbuf("biasc", [128, 1], F32)
    tiny, tiny_b = sbuf("tiny", [1, 16], F32)
    slab_a, slab_a_b = sbuf("slab_a", [128, D], F32)
    slab_b, slab_b_b = sbuf("slab_b", [128, D], F32)
    tsl = []
    for i in range(4):
        tsl.append(sbuf(f"tsl{i}", [128, D], BF16))
    lnout, lnout_b = sbuf("lnout", [128, D], BF16)
    fm0, fm0_b = sbuf("fm0", [128, 32 * 512], BF16)
    wrow0, wrow0_b = sbuf("wrow0", [128, 1024], BF16)
    wrow1, wrow1_b = sbuf("wrow1", [128, 1024], BF16)
    wrow2, wrow2_b = sbuf("wrow2", [128, 1024], BF16)
    wrow3, wrow3_b = sbuf("wrow3", [128, 1024], BF16)
    wrow4, wrow4_b = sbuf("wrow4", [128, 1024], BF16)
    wrow5, wrow5_b = sbuf("wrow5", [128, 1024], BF16)
    WR4 = [(wrow0, wrow0_b), (wrow1, wrow1_b), (wrow2, wrow2_b), (wrow3, wrow3_b),
           (wrow4, wrow4_b), (wrow5, wrow5_b)]
    augr0, augr0_b = sbuf("augr0", [128, 256], BF16)
    augr1, augr1_b = sbuf("augr1", [128, 256], BF16)
    lbr1, lbr1_b = sbuf("lbr1", [128, 1024], BF16)
    lbr2, lbr2_b = sbuf("lbr2", [128, 1024], BF16)
    kh = [sbuf("kh0", [128, T], BF16), sbuf("kh1", [128, T], BF16)]
    vh = [sbuf("vh0", [128, T], BF16), sbuf("vh1", [128, T], BF16)]
    qt = [sbuf("qt0", [128, 512], BF16), sbuf("qt1", [128, 512], BF16)]
    pt = [sbuf("pt0", [128, 1024], BF16), sbuf("pt1", [128, 1024], BF16)]
    vstage, vstage_b = sbuf("vstage", [128, 2 * 512], BF16)
    stage0, stage0_b = sbuf("stage0", [128, 1024], BF16)
    stage1, stage1_b = sbuf("stage1", [128, 1024], BF16)
    krotB = [sbuf("krot0", [128, 512], BF16), sbuf("krot1", [128, 512], BF16)]
    rt1B = [sbuf("rt1a", [128, 512], BF16), sbuf("rt1b", [128, 512], BF16)]
    rt2B = [sbuf("rt2a", [128, 512], BF16), sbuf("rt2b", [128, 512], BF16)]
    kstB = [sbuf("kst0", [128, 512], BF16), sbuf("kst1", [128, 512], BF16)]
    krot, krot_b = krotB[0]
    rt1, rt1_b = rt1B[0]
    rt2, rt2_b = rt2B[0]
    psum2 = [sbuf("psum2a", [128, 512], BF16), sbuf("psum2b", [128, 512], BF16)]
    l_sb, l_sb_b = sbuf("l_sb", [1, 512], F32)
    linv, linv_b = sbuf("linv", [1, 512], F32)
    linvb, linvb_b = sbuf("linvb", [128, 512], BF16)
    sm0, sm0_b = sbuf("sm0", [128, 80], F32)
    gsm, gsm_b = sbuf("gsm", [128, 40], F32)
    gtmp, gtmp_b = sbuf("gtmp", [128, NE], F32)
    gtmp2, gtmp2_b = sbuf("gtmp2", [128, NE], F32)
    gate_tm, gate_tm_b = sbuf("gate_tm", [128, NE], F32)
    lgt, lgt_b = sbuf("lgt", [NE, 512], F32)
    g28, g28_b = sbuf("g28", [NE, 512], BF16)
    gf1, gf1_b = sbuf("gf1", [128, 512], BF16)
    gf2, gf2_b = sbuf("gf2", [128, 512], BF16)
    h1p, h1p_b = sbuf("h1p", [128, 512], BF16)
    h2p, h2p_b = sbuf("h2p", [128, 512], BF16)
    qacc, qacc_b = sbuf("qacc", [128, 32], F32)

    pA = nc.alloc_psum_tensor("pA", [128, 1024], F32)
    pB = nc.alloc_psum_tensor("pB", [128, 1024], F32)
    pC = nc.alloc_psum_tensor("pC", [128, 1024], F32)
    pD = nc.alloc_psum_tensor("pD", [128, 1024], F32)
    pbufs = {}
    for nm in ("A0", "A1", "B0", "B1", "C0", "C1", "D0", "D1"):
        pbufs[nm] = em.buf("p" + nm)
    PS = [(pA, 0, pbufs["A0"]), (pA, 512, pbufs["A1"]),
          (pB, 0, pbufs["B0"]), (pB, 512, pbufs["B1"]),
          (pC, 0, pbufs["C0"]), (pC, 512, pbufs["C1"]),
          (pD, 0, pbufs["D0"]), (pD, 512, pbufs["D1"])]

    em.dma("sp", lambda e: e.dma_start(masks_sb[:].rearrange("p (m f) -> p m f", m=4), masks_in.rearrange("m p f -> p m f")),
           [masks_b], [masks_sb_b], "c4")
    em.dma("sp", lambda e: e.dma_start(ident_sb[:], ident_in), [ident_b], [ident_sb_b], "c5")
    em.dma("sp", lambda e: e.dma_start(r1_sb[:], r1_in), [r1_b], [r1_sb_b], "c6")
    em.dma("sp", lambda e: e.dma_start(r2_sb[:], r2_in), [r2_b], [r2_sb_b], "c7")
    em.op("dve", lambda e: e.memset(ones_bf[:], 1.0), [], [ones_bf_b], "ones")
    em.op("dve", lambda e: e.memset(onesr_f[:], 1.0), [], [onesr_f_b], "onesr")
    em.op("dve", lambda e: e.memset(cpool[:, 0:1], EPS), [], [cpool_b], "eps")
    em.op("dve", lambda e: e.memset(qacc[:], 0.0), [], [qacc_b], "qacc0")
    em.op("dve", lambda e: e.memset(lbr1[0:32, :], 0.0), [], [lbr1_b], "lbr1z")

    def cast_w(src_ap, src_b, dst_ap, dst_b, rows, tag):
        for r0 in range(0, rows, 512):
            n = min(512, rows - r0)
            em.dma("pool", (lambda r0=r0, n=n: lambda e: e.dma_start(
                dst_ap[r0:r0 + n, :], src_ap[r0:r0 + n, :]))(),
                [src_b], [dst_b], f"cw_{tag}{r0}")

    wk_cb = [em.buf(f"wkc{j}") for j in range(8)]
    wv_cb = [em.buf(f"wvc{j}") for j in range(8)]
    for j in range(8):
        em.dma("pool", (lambda j=j: lambda e: e.dma_start(
            wk_bf[j * 512:(j + 1) * 512, :], wk_in[j * 512:(j + 1) * 512, :]))(),
            [wk_in_b], [wk_cb[j]], f"cw_k{j}")
        em.dma("pool", (lambda j=j: lambda e: e.dma_start(
            wv_bf[j * 512:(j + 1) * 512, :], wv_in[j * 512:(j + 1) * 512, :]))(),
            [wv_in_b], [wv_cb[j]], f"cw_v{j}")

    def p0_wq():
        cast_w(wq_in, wq_in_b, wq_bf, wq_bf_b, D, "q")

    def p0_wo():
        cast_w(wo_in, wo_in_b, wo_bf, wo_bf_b, D, "o")

    def p0_bwT():
        cast_w(bwt_in, bwt_in_b, bwT_bf, bwT_bf_b, D, "bt")

    def p0_ffn_small():
        cast_w(lataug_in, lataug_in_b, lataug, lataug_b, D, "lat")
        cast_w(lb2_in, lb2_in_b, lbsc, lbsc_b, NE * RANK, "lb2")

    P0_DEFER = {0: p0_wq, 1: p0_wo, 2: p0_bwT, 3: p0_ffn_small,
                4: lambda: p2a_ln(0), 5: lambda: p2a_ln(1),
                6: lambda: p2a_ln(2), 7: lambda: p2a_ln(3)}

    def ln_tile(x_ap, x_b, out_ap, out_bb, tag, dump_ap=None, dump_b=None, col=64):
        if dump_ap is None:
            dump_ap, dump_b = lnout[:], lnout_b
        sq, s1 = sm0[:, col:col + 1], sm0[:, col + 1:col + 2]
        mm_, m2 = sm0[:, col + 2:col + 3], sm0[:, col + 3:col + 4]
        vv, rr, nm = sm0[:, col + 4:col + 5], sm0[:, col + 5:col + 6], sm0[:, col + 6:col + 7]
        em.op("act", lambda e: e.activation(dump_ap, x_ap, AF.Square, accum_out=sq),
              [x_b], [sm0_b, dump_b], f"{tag}sq")
        em.op("dve", lambda e: e.reduce_sum(s1, x_ap, axis=AX.X), [x_b], [sm0_b], f"{tag}s1")
        em.op("dve", lambda e: e.tensor_scalar(mm_, s1, 1.0 / D, None, ALU.mult),
              [sm0_b], [sm0_b], f"{tag}m")
        em.op("dve", lambda e: e.tensor_tensor(m2, mm_, mm_, ALU.mult), [sm0_b], [sm0_b], f"{tag}m2")
        em.op("dve", lambda e: e.scalar_tensor_tensor(vv, sq, 1.0 / D, m2, ALU.mult, ALU.subtract),
              [sm0_b], [sm0_b], f"{tag}v")
        em.op("act", lambda e: e.activation(vv, vv, AF.Ln, bias=cpool[:, 0:1]),
              [sm0_b, cpool_b], [sm0_b], f"{tag}ln")
        em.op("act", lambda e: e.activation(rr, vv, AF.Exp, scale=-0.5), [sm0_b], [sm0_b], f"{tag}ex")
        em.op("dve", lambda e: e.scalar_tensor_tensor(nm, mm_, -1.0, rr, ALU.mult, ALU.mult),
              [sm0_b], [sm0_b], f"{tag}nm")
        em.op("act", lambda e: e.activation(out_ap, x_ap, AF.Identity, bias=nm, scale=rr),
              [x_b, sm0_b], [out_bb], f"{tag}ap")

    def ln_finish(scol, sqcol, tag):
        s1, sq = gsm[:, 8:9], gsm[:, 9:10]
        mm_, m2 = gsm[:, 10:11], gsm[:, 11:12]
        vv, rr, nm = gsm[:, 12:13], gsm[:, 13:14], gsm[:, 14:15]
        em.op("dve", lambda e: e.reduce_sum(s1, sm0[:, scol:scol + 8], axis=AX.X),
              [sm0_b], [gsm_b], f"{tag}rs")
        em.op("dve", lambda e: e.reduce_sum(sq, sm0[:, sqcol:sqcol + 8], axis=AX.X),
              [sm0_b], [gsm_b], f"{tag}rq")
        em.op("dve", lambda e: e.tensor_scalar(mm_, s1, 1.0 / D, None, ALU.mult),
              [gsm_b], [gsm_b], f"{tag}m")
        em.op("dve", lambda e: e.tensor_tensor(m2, mm_, mm_, ALU.mult), [gsm_b], [gsm_b], f"{tag}m2")
        em.op("dve", lambda e: e.scalar_tensor_tensor(vv, sq, 1.0 / D, m2, ALU.mult, ALU.subtract),
              [gsm_b], [gsm_b], f"{tag}v")
        em.op("act", lambda e: e.activation(vv, vv, AF.Ln, bias=cpool[:, 0:1]),
              [gsm_b, cpool_b], [gsm_b], f"{tag}ln")
        em.op("act", lambda e: e.activation(rr, vv, AF.Exp, scale=-0.5), [gsm_b], [gsm_b], f"{tag}ex")
        em.op("dve", lambda e: e.scalar_tensor_tensor(nm, mm_, -1.0, rr, ALU.mult, ALU.mult),
              [gsm_b], [gsm_b], f"{tag}nm")
        return rr, nm

    aslab, aslab_b = tsl[2]
    alnout, alnout_b = tsl[3]

    def p2a_ln(i):
        for ts in range(4):
            r0 = i * 512 + ts * 128
            em.dma("pool", (lambda r0=r0: lambda e: e.dma_start(aslab[:], xo_in[r0:r0 + 128, :]))(),
                   [xo_in_b], [aslab_b], f"a_x{i}_{ts}")
            ln_tile(aslab[:], aslab_b, alnout[:], alnout_b, f"aL{i}_{ts}",
                    dump_ap=alnout[:], dump_b=alnout_b, col=72)
            em.dma("act", (lambda r0=r0: lambda e: e.dma_start(lnoscr[r0:r0 + 128, :], alnout[:]))(),
                   [alnout_b], [lnoscr_tb[i]], f"a_s{i}_{ts}")


    GB = [kh[0], kh[1], vh[0], vh[1]]

    def fmsl(group, dc, c0, c1):
        if group == 0:
            return fm0[:, dc * 512 + c0: dc * 512 + c1], fm0_b
        t, tb = GB[dc // 8]
        return t[:, (dc % 8) * 512 + c0: (dc % 8) * 512 + c1], tb

    def p1_ln(t8):
        t0 = t8 * 512
        for sub in range(4):
            sl, slb = (slab_a, slab_a_b) if sub % 2 == 0 else (slab_b, slab_b_b)
            r0 = t0 + sub * 128
            em.dma("pool", (lambda r0=r0, sl=sl: lambda e: e.dma_start(sl[:], x_in[r0:r0 + 128, :]))(),
                   [x_in_b], [slb], f"p1x{t8}_{sub}")
            ln_tile(sl[:], slb, lnout[:], lnout_b, f"p1L{t8}_{sub}", col=sub * 7)
            em.dma("act", (lambda r0=r0, t8=t8: lambda e: e.dma_start(lnscr[r0:r0 + 128, :], lnout[:]))(),
                   [lnout_b], [lnscr_tb[t8]], f"p1s{t8}_{sub}")

    def p1_fm(t8):
        t0 = t8 * 512
        for dc in range(32):
            dst, dstb = fmsl(t8 % 2, dc, 0, 512)
            em.dma("act", (lambda dc=dc, t0=t0, dst=dst: lambda e: e.dma_start(
                dst, lnscr[t0:t0 + 512, dc * 128:(dc + 1) * 128], transpose=True))(),
                [lnscr_tb[t8]], [dstb], f"p1f{t8}_{dc}")

    p1_ln(0)
    p1_fm(0)
    for t8 in range(8):
        t0 = t8 * 512
        grp_ = t8 % 2
        for ic in range(32):
            wr, wrb = WR4[ic % 6]
            em.dma("sp", (lambda ic=ic, wr=wr: lambda e: e.dma_start(
                wr[:], wk_bf[ic * 128:(ic + 1) * 128, :]))(),
                [wk_cb[ic // 4]], [wrb], f"p1wk{t8}_{ic}")
            src, srcb = fmsl(grp_, ic, 0, 512)
            for h in range(8):
                bank, off, bb = PS[h]
                em.op("pe", (lambda ic=ic, h=h, wr=wr, bank=bank, off=off, src=src: lambda e: e.matmul(
                    bank[:, off:off + 512], wr[:, h * 128:(h + 1) * 128],
                    src, start=(ic == 0), stop=(ic == 31)))(),
                    [wrb, srcb], [bb], f"p1k{t8}_{ic}_{h}")
        cp, cpb = cosP[t8 % 2]
        sp_, spb = sinP[t8 % 2]
        em.dma("sp", (lambda t0=t0, cp=cp: lambda e: e.dma_start(cp[:], cosf_in[:, t0:t0 + 512]))(),
               [cosf_b], [cpb], f"p1cp{t8}")
        em.dma("sp", (lambda t0=t0, sp_=sp_: lambda e: e.dma_start(sp_[:], sinf_in[:, t0:t0 + 512]))(),
               [sinf_b], [spb], f"p1sp{t8}")
        for h in range(8):
            bank, off, bb = PS[h]
            stg = stage1[:, (h % 2) * 512:(h % 2) * 512 + 512]
            kro, krob = krotB[h % 2]
            r1_, r1b = rt1B[h % 2]
            r2_, r2b = rt2B[h % 2]
            kst, kstb = kstB[h % 2]
            em.op("act", (lambda bank=bank, off=off, stg=stg: lambda e: e.copy(stg, bank[:, off:off + 512]))(),
                  [bb], [stage1_b], f"p1ke{t8}_{h}")
            em.op("dve", (lambda stg=stg, r1_=r1_, cp=cp: lambda e: e.tensor_tensor(
                r1_[:], stg, cp[:], ALU.mult))(),
                [stage1_b, cpb], [r1b], f"p1r1{t8}_{h}")
            em.op("act", (lambda h=h, kro=kro: lambda e: e.copy(
                kro[0:64, :], stage1[64:128, (h % 2) * 512:(h % 2) * 512 + 512]))(),
                [stage1_b], [krob], f"p1ra{t8}_{h}")
            em.op("act", (lambda h=h, kro=kro: lambda e: e.copy(
                kro[64:128, :], stage1[0:64, (h % 2) * 512:(h % 2) * 512 + 512]))(),
                [stage1_b], [krob], f"p1rb{t8}_{h}")
            em.op("dve", (lambda kro=kro, r2_=r2_, sp_=sp_: lambda e: e.tensor_tensor(
                r2_[:], kro[:], sp_[:], ALU.mult))(),
                [krob, spb], [r2b], f"p1rc{t8}_{h}")
            em.op("dve", (lambda kst=kst, r1_=r1_, r2_=r2_: lambda e: e.tensor_tensor(
                kst[:], r1_[:], r2_[:], ALU.add))(),
                [r1b, r2b], [kstb], f"p1ro{t8}_{h}")
            em.dma("pool", (lambda h=h, t0=t0, kst=kst: lambda e: e.dma_start(
                kscr[h, :, t0:t0 + 512], kst[:]))(),
                [kstb], [kscr_b], f"p1kw{t8}_{h}")
            em.op("dve", (lambda kst=kst, h=h: lambda e: e.tensor_tensor(
                stage0[:, (h % 2) * 512:(h % 2) * 512 + 512], kst[:], kst[:], ALU.mult))(),
                [kstb], [stage0_b], f"p1n2{t8}_{h}")
            em.op("pe", (lambda bank=bank, off=off, h=h: lambda e: e.matmul(
                bank[0:1, off:off + 512], ones_bf[:],
                stage0[:, (h % 2) * 512:(h % 2) * 512 + 512], start=True, stop=True))(),
                [stage0_b, ones_bf_b], [bb], f"p1nm{t8}_{h}")
            em.op("dve", (lambda bank=bank, off=off, t8=t8, h=h: lambda e: e.reduce_max(
                knorm[0:1, t8 * 8 + h:t8 * 8 + h + 1], bank[0:1, off:off + 512], axis=AX.X))(),
                [bb], [knorm_b], f"p1nx{t8}_{h}")
        if t8 + 1 < 8:
            p1_ln(t8 + 1)
            p1_fm(t8 + 1)
        for ic in range(32):
            wr, wrb = WR4[ic % 6]
            em.dma("sp", (lambda ic=ic, wr=wr: lambda e: e.dma_start(
                wr[:], wv_bf[ic * 128:(ic + 1) * 128, :]))(),
                [wv_cb[ic // 4]], [wrb], f"p1wv{t8}_{ic}")
            for ts in range(4):
                src, srcb = fmsl(grp_, ic, ts * 128, (ts + 1) * 128)
                for hf in range(2):
                    bank, off, bb = PS[ts * 2 + hf]
                    em.op("pe", (lambda ic=ic, ts=ts, hf=hf, wr=wr, bank=bank, off=off, src=src:
                                 lambda e: e.matmul(
                        bank[:, off:off + 512], src,
                        wr[:, hf * 512:(hf + 1) * 512], start=(ic == 0), stop=(ic == 31)))(),
                        [wrb, srcb], [bb], f"p1v{t8}_{ic}_{ts}_{hf}")
        for hw in range(4):
            hf, hp = hw // 2, (hw % 2) * 2
            for ts in range(4):
                bank, off, bb = PS[ts * 2 + hf]
                for hh in (hp, hp + 1):
                    em.op("act", (lambda bank=bank, off=off, hh=hh, hp=hp, ts=ts: lambda e: e.copy(
                        vstage[:, (hh - hp) * 512 + ts * 128: (hh - hp) * 512 + (ts + 1) * 128],
                        bank[:, off + hh * 128: off + (hh + 1) * 128]))(),
                        [bb], [vstage_b], f"p1ve{t8}_{ts}_{hf*4+hh}")
            for hh in (hp, hp + 1):
                h = hf * 4 + hh
                em.dma("act", (lambda h=h, hh=hh, hp=hp, t0=t0: lambda e: e.dma_start(
                    vscr[h, :, t0:t0 + 512], vstage[:, (hh - hp) * 512:(hh - hp + 1) * 512]))(),
                    [vstage_b], [vscr_b], f"p1vw{t8}_{h}")
        if t8 in P0_DEFER:
            P0_DEFER[t8]()

    def p2a_main(i):
        for dc in range(32):
            em.dma("act", (lambda dc=dc, i=i: lambda e: e.dma_start(
                fm0[:, dc * 512:(dc + 1) * 512],
                lnoscr[i * 512:(i + 1) * 512, dc * 128:(dc + 1) * 128], transpose=True))(),
                [lnoscr_tb[i]], [fm0_b], f"a_f{i}_{dc}")
        em.op("dve", lambda e: e.memset(qacc[:], 0.0), [], [qacc_b], f"a_q0{i}")
        em.dma("sp", (lambda i=i: lambda e: e.dma_start(
            cos_tm[:].rearrange("p (c d) -> p c d", c=4),
            cost_in[i * 512:(i + 1) * 512, :].rearrange("(c p) d -> p c d", p=128)))(),
            [cost_b], [cos_tm_b], f"a_ct{i}")
        em.dma("sp", (lambda i=i: lambda e: e.dma_start(
            sin_tm[:].rearrange("p (c d) -> p c d", c=4),
            sint_in[i * 512:(i + 1) * 512, :].rearrange("(c p) d -> p c d", p=128)))(),
            [sint_b], [sin_tm_b], f"a_st{i}")
        for qtr in range(4):
            for ic in range(32):
                wr, wrb = WR4[ic % 6]
                em.dma("sp", (lambda ic=ic, qtr=qtr, wr=wr: lambda e: e.dma_start(
                    wr[:], wq_bf[ic * 128:(ic + 1) * 128, qtr * 1024:(qtr + 1) * 1024]))(),
                    [wq_bf_b], [wrb], f"a_w{i}_{qtr}_{ic}")
                for ts in range(4):
                    for hf in range(2):
                        bank, off, bb = PS[ts * 2 + hf]
                        em.op("pe", (lambda ic=ic, ts=ts, hf=hf, wr=wr, bank=bank, off=off:
                                     lambda e: e.matmul(
                            bank[:, off:off + 512],
                            fm0[:, ic * 512 + ts * 128: ic * 512 + (ts + 1) * 128],
                            wr[:, hf * 512:(hf + 1) * 512], start=(ic == 0), stop=(ic == 31)))(),
                            [wrb, fm0_b], [bb], f"a_m{i}_{qtr}_{ic}_{ts}_{hf}")
            for ts in range(4):
                sub = ts
                for hf in range(2):
                    bank, off, bb = PS[ts * 2 + hf]
                    em.op("act", (lambda bank=bank, off=off, hf=hf: lambda e: e.copy(
                        stage0[:, hf * 512:(hf + 1) * 512], bank[:, off:off + 512]))(),
                        [bb], [stage0_b], f"a_e{i}_{qtr}_{ts}_{hf}")
                cs = cos_tm[:, sub * 128:(sub + 1) * 128]
                sn = sin_tm[:, sub * 128:(sub + 1) * 128]
                for hb in range(8):
                    c0 = hb * 128
                    em.op("dve", (lambda c0=c0, cs=cs: lambda e: e.tensor_tensor(
                        rt1[:, 0:128], stage0[:, c0:c0 + 128], cs, ALU.mult))(),
                        [stage0_b, cos_tm_b], [rt1_b], f"a_r1{i}_{qtr}_{ts}_{hb}")
                    em.op("dve", (lambda c0=c0, sn=sn: lambda e: e.tensor_tensor(
                        rt2[:, 0:64], stage0[:, c0 + 64:c0 + 128], sn[:, 0:64], ALU.mult))(),
                        [stage0_b, sin_tm_b], [rt2_b], f"a_r2{i}_{qtr}_{ts}_{hb}")
                    em.op("dve", (lambda c0=c0, sn=sn: lambda e: e.tensor_tensor(
                        rt2[:, 64:128], stage0[:, c0:c0 + 64], sn[:, 64:128], ALU.mult))(),
                        [stage0_b, sin_tm_b], [rt2_b], f"a_r3{i}_{qtr}_{ts}_{hb}")
                    em.op("dve", (lambda c0=c0: lambda e: e.tensor_tensor(
                        stage1[:, c0:c0 + 128], rt1[:, 0:128], rt2[:, 0:128], ALU.add))(),
                        [rt1_b, rt2_b], [stage1_b], f"a_ro{i}_{qtr}_{ts}_{hb}")
                em.dma("pool", (lambda i=i, ts=ts, qtr=qtr: lambda e: e.dma_start(
                    qscr[i * 512 + ts * 128:i * 512 + (ts + 1) * 128, qtr * 1024:(qtr + 1) * 1024],
                    stage1[:]))(),
                    [stage1_b], [qscr_b], f"a_qs{i}_{qtr}_{ts}")
                em.op("dve", lambda e: e.tensor_tensor(pt[0][0][:], stage1[:], stage1[:], ALU.mult),
                      [stage1_b], [pt[0][1]], f"a_n2{i}_{qtr}_{ts}")
                em.op("dve", lambda e: e.reduce_sum(gsm[:, 0:8],
                                                    pt[0][0][:].rearrange("p (s d) -> p s d", s=8),
                                                    axis=AX.X),
                      [pt[0][1]], [gsm_b], f"a_nr{i}_{qtr}_{ts}")
                em.op("dve", (lambda qtr=qtr: lambda e: e.tensor_tensor(
                    qacc[:, qtr * 8:(qtr + 1) * 8], qacc[:, qtr * 8:(qtr + 1) * 8],
                    gsm[:, 0:8], ALU.max))(),
                    [qacc_b, gsm_b], [qacc_b], f"a_nx{i}_{qtr}_{ts}")
        em.op("dve", lambda e: e.reduce_max(gsm[:, 16:17], qacc[:], axis=AX.X),
              [qacc_b], [gsm_b], f"a_fx{i}")
        em.op("pe", lambda e: e.transpose(pD[0:8, 0:128], gsm[:, 16:24], ident_sb[:]),
              [gsm_b, ident_sb_b], [pbufs["D0"]], f"a_tr{i}")
        em.op("dve", (lambda i=i: lambda e: e.reduce_max(qnorm[0:1, i:i + 1], pD[0:1, 0:128], axis=AX.X))(),
              [pbufs["D0"]], [qnorm_b], f"a_qn{i}")
        em.op("dve", (lambda i=i: lambda e: e.reduce_max(tiny[0:1, 0:1],
                                                         knorm[0:1, 0:(2 * i + 2) * 8], axis=AX.X))(),
              [knorm_b], [tiny_b], f"b_k{i}")
        em.op("dve", (lambda i=i: lambda e: e.tensor_tensor(tiny[0:1, 1:2], tiny[0:1, 0:1],
                                                            qnorm[0:1, i:i + 1], ALU.mult))(),
              [tiny_b, qnorm_b], [tiny_b], f"b_m{i}")
        em.op("act", lambda e: e.activation(tiny[0:1, 2:3], tiny[0:1, 1:2], AF.Ln),
              [tiny_b], [tiny_b], f"b_l{i}")
        em.op("act", lambda e: e.activation(tiny[0:1, 3:4], tiny[0:1, 2:3], AF.Exp, scale=0.5),
              [tiny_b], [tiny_b], f"b_e{i}")
        em.op("dve", lambda e: e.tensor_scalar(tiny[0:1, 4:5], tiny[0:1, 3:4], -ISQ, None, ALU.mult),
              [tiny_b], [tiny_b], f"b_s{i}")
        em.op("pe", lambda e: e.matmul(pD[:, 512:513], onesr_f[:], tiny[0:1, 4:5],
                                       start=True, stop=True),
              [tiny_b, onesr_f_b], [pbufs["D1"]], f"b_b{i}")
        em.op("act", lambda e: e.copy(biasc[:], pD[:, 512:513]), [pbufs["D1"]], [biasc_b], f"b_c{i}")
        if dbg and i == 0:
            em.dma("sp", lambda e: e.dma_start(dbg_biasc, biasc[:]), [biasc_b], [dbg_bufs[0]], "dbgb")

    def p2b(i, nch, pairs):
        for h in range(NKVH):
            khh, khb = kh[h % 2]
            vhh, vhb = vh[h % 2]
            em.dma("sp", (lambda h=h, khh=khh, nch=nch: lambda e: e.dma_start(
                khh[:, 0:nch * 128], kscr[h, :, 0:nch * 128]))(),
                [kscr_b], [khb], f"t_k{i}_{h}")
            em.dma("sp", (lambda h=h, vhh=vhh, nch=nch: lambda e: e.dma_start(
                vhh[:, 0:nch * 128], vscr[h, :, 0:nch * 128]))(),
                [vscr_b], [vhb], f"t_v{i}_{h}")
            for g in range(GRP):
                hq = h * GRP + g
                qtt, qtb = qt[g % 2]
                em.dma("act", (lambda i=i, hq=hq, qtt=qtt: lambda e: e.dma_start(
                    qtt[:], qscr[i * 512:(i + 1) * 512, hq * 128:(hq + 1) * 128], transpose=True))(),
                    [qscr_b], [qtb], f"t_q{i}_{hq}")
                def score_pair(pr, khh=khh, khb=khb, qtt=qtt, qtb=qtb):
                    c0, c1 = 2 * pr, 2 * pr + 1
                    sbank, sb0, sb1 = (pA, pbufs["A0"], pbufs["A1"]) if pr % 2 == 0 else \
                                      (pB, pbufs["B0"], pbufs["B1"])
                    ptt, ptb = pt[pr % 2]
                    em.op("pe", (lambda c0=c0, sbank=sbank, khh=khh, qtt=qtt: lambda e: e.matmul(
                        sbank[:, 0:512], khh[:, c0 * 128:(c0 + 1) * 128], qtt[:],
                        start=True, stop=True))(),
                        [khb, qtb], [sb0], f"t_s{i}_{hq}_{c0}")
                    em.op("pe", (lambda c1=c1, sbank=sbank, khh=khh, qtt=qtt: lambda e: e.matmul(
                        sbank[:, 512:1024], khh[:, c1 * 128:(c1 + 1) * 128], qtt[:],
                        start=True, stop=True))(),
                        [khb, qtb], [sb1], f"t_s{i}_{hq}_{c1}")
                    em.op("act", (lambda sbank=sbank, ptt=ptt: lambda e: e.activation(
                        ptt[:], sbank[:], AF.Exp, bias=biasc[:], scale=ISQ))(),
                        [sb0, sb1, biasc_b], [ptb], f"t_e{i}_{hq}_{pr}")
                    if pr >= 4 * i:
                        fr = pr - 4 * i
                        em.op("dve", (lambda fr=fr, ptt=ptt: lambda e: e.tensor_tensor(
                            ptt[:], ptt[:], masks_sb[:, fr * 1024:(fr + 1) * 1024], ALU.mult))(),
                            [ptb, masks_sb_b], [ptb], f"t_m{i}_{hq}_{pr}")

                def consume_pair(pr, vhh=vhh, vhb=vhb, pairs=pairs):
                    c0, c1 = 2 * pr, 2 * pr + 1
                    ptt, ptb = pt[pr % 2]
                    ps2, ps2b = psum2[pr % 2]
                    em.op("dve", (lambda ptt=ptt, ps2=ps2: lambda e: e.tensor_tensor(
                        ps2[:], ptt[:, 0:512], ptt[:, 512:1024], ALU.add))(),
                        [ptb], [ps2b], f"t_la{i}_{hq}_{pr}")
                    em.op("pe", (lambda pr=pr, ps2=ps2, pairs=pairs: lambda e: e.matmul(
                        pD[0:1, 0:512], ones_bf[:], ps2[:],
                        start=(pr == 0), stop=(pr == pairs - 1)))(),
                        [ps2b, ones_bf_b], [pbufs["D0"]], f"t_l0{i}_{hq}_{pr}")
                    em.op("pe", (lambda pr=pr, c0=c0, ptt=ptt, vhh=vhh: lambda e: e.matmul(
                        pC[:, 0:512], vhh[:, c0 * 128:(c0 + 1) * 128], ptt[:, 0:512],
                        start=(pr == 0), stop=False))(),
                        [vhb, ptb], [pbufs["C0"]], f"t_c0{i}_{hq}_{pr}")
                    em.op("pe", (lambda pr=pr, c1=c1, ptt=ptt, vhh=vhh, pairs=pairs: lambda e: e.matmul(
                        pC[:, 0:512], vhh[:, c1 * 128:(c1 + 1) * 128], ptt[:, 512:1024],
                        start=False, stop=(pr == pairs - 1)))(),
                        [vhb, ptb], [pbufs["C0"]], f"t_c1{i}_{hq}_{pr}")

                for pr in range(pairs):
                    score_pair(pr)
                    if pr > 0:
                        consume_pair(pr - 1)
                consume_pair(pairs - 1)
                em.op("dve", lambda e: e.tensor_copy(l_sb[:], pD[0:1, 0:512]),
                      [pbufs["D0"]], [l_sb_b], f"t_le{i}_{hq}")
                if dbg and i == 0 and hq == 0:
                    em.dma("sp", lambda e: e.dma_start(dbg_l, l_sb[:]), [l_sb_b], [dbg_bufs[2]], "dbgl")
                em.op("dve", lambda e: e.reciprocal(linv[:], l_sb[:]), [l_sb_b], [linv_b], f"t_li{i}_{hq}")
                em.op("pe", lambda e: e.matmul(pD[:, 512:1024], onesr_f[:], linv[:],
                                               start=True, stop=True),
                      [linv_b, onesr_f_b], [pbufs["D1"]], f"t_lb{i}_{hq}")
                em.op("act", lambda e: e.copy(linvb[:], pD[:, 512:1024]),
                      [pbufs["D1"]], [linvb_b], f"t_lv{i}_{hq}")
                em.op("dve", (lambda hq=hq: lambda e: e.tensor_tensor(
                    fm0[:, hq * 512:(hq + 1) * 512], pC[:, 0:512], linvb[:], ALU.mult))(),
                    [pbufs["C0"], linvb_b], [fm0_b], f"t_cn{i}_{hq}")
                if dbg and i == 0 and hq == 0:
                    em.dma("sp", lambda e: e.dma_start(dbg_ctx, fm0[:, 0:512]), [fm0_b], [dbg_bufs[3]], "dbgc")

    def p2c_wo(i):
        for qtr in range(4):
            for dc in range(32):
                wr, wrb = WR4[dc % 6]
                em.dma("sp", (lambda dc=dc, qtr=qtr, wr=wr: lambda e: e.dma_start(
                    wr[:], wo_bf[dc * 128:(dc + 1) * 128, qtr * 1024:(qtr + 1) * 1024]))(),
                    [wo_bf_b], [wrb], f"c_w{i}_{qtr}_{dc}")
                for ts in range(4):
                    for hf in range(2):
                        bank, off, bb = PS[ts * 2 + hf]
                        em.op("pe", (lambda dc=dc, ts=ts, hf=hf, wr=wr, bank=bank, off=off:
                                     lambda e: e.matmul(
                            bank[:, off:off + 512],
                            fm0[:, dc * 512 + ts * 128: dc * 512 + (ts + 1) * 128],
                            wr[:, hf * 512:(hf + 1) * 512], start=(dc == 0), stop=(dc == 31)))(),
                            [wrb, fm0_b], [bb], f"c_m{i}_{qtr}_{dc}_{ts}_{hf}")
            for ts in range(4):
                for hf in range(2):
                    bank, off, bb = PS[ts * 2 + hf]
                    seg = qtr * 2 + hf
                    em.op("act", (lambda bank=bank, off=off, ts=ts, qtr=qtr, hf=hf, seg=seg:
                                  lambda e: e.activation(
                        tsl[ts][0][:, qtr * 1024 + hf * 512: qtr * 1024 + (hf + 1) * 512],
                        bank[:, off:off + 512], AF.Copy,
                        accum_out=sm0[:, ts * 16 + seg:ts * 16 + seg + 1]))(),
                        [bb], [tsl[ts][1], sm0_b], f"c_e{i}_{qtr}_{ts}_{hf}")
                    em.op("act", (lambda bank=bank, off=off, ts=ts, seg=seg: lambda e: e.activation(
                        lnout[:, 0:512], bank[:, off:off + 512], AF.Square,
                        accum_out=sm0[:, ts * 16 + 8 + seg:ts * 16 + 8 + seg + 1]))(),
                        [bb], [lnout_b, sm0_b], f"c_q{i}_{qtr}_{ts}_{hf}")
        for ts in range(4):
            r0 = i * 512 + ts * 128
            rr, nm = ln_finish(ts * 16, ts * 16 + 8, f"c_L{i}_{ts}")
            em.op("act", (lambda ts=ts, rr=rr, nm=nm: lambda e: e.activation(
                lnout[:], tsl[ts][0][:], AF.Identity, bias=nm, scale=rr))(),
                [tsl[ts][1], gsm_b], [lnout_b], f"c_a{i}_{ts}")
            em.dma("pool", (lambda r0=r0: lambda e: e.dma_start(slab_a[:], xo_in[r0:r0 + 128, :]))(),
                   [xo_in_b], [slab_a_b], f"c_x{i}_{ts}")
            em.op("dve", lambda e: e.tensor_tensor(slab_b[:], slab_a[:], lnout[:], ALU.add),
                  [slab_a_b, lnout_b], [slab_b_b], f"c_r{i}_{ts}")
            em.dma("pool", (lambda ts=ts: lambda e: e.dma_start(
                x1scr[ts * 128:(ts + 1) * 128, :], slab_b[:]))(),
                [slab_b_b], [x1scr_b], f"c_1{i}_{ts}")
            ln_tile(slab_b[:], slab_b_b, lnout[:], lnout_b, f"c_N{i}_{ts}")
            em.dma("act", (lambda ts=ts: lambda e: e.dma_start(
                ln4scr[ts * 128:(ts + 1) * 128, :], lnout[:]))(),
                [lnout_b], [ln4scr_b], f"c_4{i}_{ts}")
    def p2c_ffn(i):
        for dc in range(32):
            dst, dstb = fmsl(1, dc, 0, 512)
            em.dma("act", (lambda dc=dc, dst=dst: lambda e: e.dma_start(
                dst, ln4scr[0:512, dc * 128:(dc + 1) * 128], transpose=True))(),
                [ln4scr_b], [dstb], f"c_f{i}_{dc}")
        for ic in range(32):
            ar, arb = (augr0, augr0_b) if ic % 2 == 0 else (augr1, augr1_b)
            em.dma("sp", (lambda ic=ic, ar=ar: lambda e: e.dma_start(
                ar[:], lataug[ic * 128:(ic + 1) * 128, :]))(),
                [lataug_b], [arb], f"f_a{i}_{ic}")
            src, srcb = fmsl(1, ic, 0, 512)
            em.op("pe", (lambda ic=ic, ar=ar, src=src: lambda e: e.matmul(
                pA[:, 0:512], ar[:, 0:128], src,
                start=(ic == 0), stop=(ic == 31)))(),
                [arb, srcb], [pbufs["A0"]], f"f_h1{i}_{ic}")
            em.op("pe", (lambda ic=ic, ar=ar, src=src: lambda e: e.matmul(
                pA[:, 512:1024], ar[:, 128:256], src,
                start=(ic == 0), stop=(ic == 31)))(),
                [arb, srcb], [pbufs["A1"]], f"f_h2{i}_{ic}")
        em.op("act", lambda e: e.copy(lgt[:], pA[0:NE, 0:512]), [pbufs["A0"]], [lgt_b], f"f_lg{i}")
        for ts in range(4):
            em.op("pe", (lambda ts=ts: lambda e: e.transpose(
                pC[0:128, 0:NE], lgt[:, ts * 128:(ts + 1) * 128], ident_sb[0:NE, 0:NE]))(),
                [lgt_b, ident_sb_b], [pbufs["C0"]], f"f_t{i}_{ts}")
            m1, m2 = gsm[:, 20:21], gsm[:, 21:22]
            d12, ee, p1 = gsm[:, 22:23], gsm[:, 23:24], gsm[:, 24:25]
            p2 = gsm[:, 25:26]
            em.op("dve", lambda e: e.reduce_max(m1, pC[0:128, 0:NE], axis=AX.X),
                  [pbufs["C0"]], [gsm_b], f"f_m1{i}_{ts}")
            em.op("dve", lambda e: e.tensor_scalar(gtmp[:], pC[0:128, 0:NE], m1, None, ALU.is_ge),
                  [pbufs["C0"], gsm_b], [gtmp_b], f"f_i1{i}_{ts}")
            em.op("dve", lambda e: e.scalar_tensor_tensor(gtmp2[:], gtmp[:], -1e30, pC[0:128, 0:NE],
                                                          ALU.mult, ALU.add),
                  [gtmp_b, pbufs["C0"]], [gtmp2_b], f"f_msk{i}_{ts}")
            em.op("dve", lambda e: e.reduce_max(m2, gtmp2[:], axis=AX.X), [gtmp2_b], [gsm_b], f"f_m2{i}_{ts}")
            em.op("dve", lambda e: e.tensor_scalar(gtmp2[:], gtmp2[:], m2, None, ALU.is_ge),
                  [gtmp2_b, gsm_b], [gtmp2_b], f"f_i2{i}_{ts}")
            em.op("dve", lambda e: e.tensor_tensor(d12, m1, m2, ALU.subtract), [gsm_b], [gsm_b], f"f_d{i}_{ts}")
            em.op("act", lambda e: e.activation(ee, d12, AF.Exp, scale=-1.0), [gsm_b], [gsm_b], f"f_e{i}_{ts}")
            em.op("dve", lambda e: e.tensor_scalar(ee, ee, 1.0, None, ALU.add), [gsm_b], [gsm_b], f"f_e1{i}_{ts}")
            em.op("dve", lambda e: e.reciprocal(p1, ee), [gsm_b], [gsm_b], f"f_p1{i}_{ts}")
            em.op("dve", lambda e: e.tensor_scalar(p2, p1, -1.0, 1.0, ALU.mult, ALU.add),
                  [gsm_b], [gsm_b], f"f_p2{i}_{ts}")
            em.op("dve", lambda e: e.tensor_scalar(gate_tm[:], gtmp[:], p1, None, ALU.mult),
                  [gtmp_b, gsm_b], [gate_tm_b], f"f_g1{i}_{ts}")
            em.op("dve", lambda e: e.scalar_tensor_tensor(gate_tm[:], gtmp2[:], p2, gate_tm[:],
                                                          ALU.mult, ALU.add),
                  [gtmp2_b, gsm_b, gate_tm_b], [gate_tm_b], f"f_g2{i}_{ts}")
            em.op("pe", (lambda ts=ts: lambda e: e.transpose(
                pD[0:NE, 0:128], gate_tm[:], ident_sb[:]))(),
                [gate_tm_b, ident_sb_b], [pbufs["D0"]], f"f_gt{i}_{ts}")
            em.op("act", (lambda ts=ts: lambda e: e.copy(g28[:, ts * 128:(ts + 1) * 128],
                                                         pD[0:NE, 0:128]))(),
                  [pbufs["D0"]], [g28_b], f"f_ge{i}_{ts}")
        em.op("pe", lambda e: e.matmul(pC[0:128, 0:512], r1_sb[:], g28[:], start=True, stop=True),
              [r1_sb_b, g28_b], [pbufs["C0"]], f"f_G1{i}")
        em.op("pe", lambda e: e.matmul(pC[0:128, 512:1024], r2_sb[:], g28[:], start=True, stop=True),
              [r2_sb_b, g28_b], [pbufs["C1"]], f"f_G2{i}")
        em.op("act", lambda e: e.copy(gf1[:], pC[0:128, 0:512]), [pbufs["C0"]], [gf1_b], f"f_ge1{i}")
        em.op("act", lambda e: e.copy(gf2[:], pC[0:128, 512:1024]), [pbufs["C1"]], [gf2_b], f"f_ge2{i}")
        em.op("dve", lambda e: e.tensor_tensor(h1p[:], pA[0:128, 0:512], gf1[:], ALU.mult),
              [pbufs["A0"], gf1_b], [h1p_b], f"f_hp1{i}")
        em.op("dve", lambda e: e.tensor_tensor(h2p[:], pA[0:128, 512:1024], gf2[:], ALU.mult),
              [pbufs["A1"], gf2_b], [h2p_b], f"f_hp2{i}")
        for qtr in range(4):
            em.dma("sp", (lambda qtr=qtr: lambda e: e.dma_start(
                lbr1[32:128, :], lbsc[0:96, qtr * 1024:(qtr + 1) * 1024]))(),
                [lbsc_b], [lbr1_b], f"f_b1{i}_{qtr}")
            em.dma("sp", (lambda qtr=qtr: lambda e: e.dma_start(
                lbr2[:], lbsc[96:224, qtr * 1024:(qtr + 1) * 1024]))(),
                [lbsc_b], [lbr2_b], f"f_b2{i}_{qtr}")
            for dc in range(32):
                wr, wrb = WR4[dc % 6]
                em.dma("sp", (lambda dc=dc, qtr=qtr, wr=wr: lambda e: e.dma_start(
                    wr[:], bwT_bf[dc * 128:(dc + 1) * 128, qtr * 1024:(qtr + 1) * 1024]))(),
                    [bwT_bf_b], [wrb], f"f_w{i}_{qtr}_{dc}")
                for ts in range(4):
                    src, srcb = fmsl(1, dc, ts * 128, (ts + 1) * 128)
                    for hf in range(2):
                        bank, off, bb = PS[ts * 2 + hf]
                        em.op("pe", (lambda dc=dc, ts=ts, hf=hf, wr=wr, bank=bank, off=off, src=src:
                                     lambda e: e.matmul(
                            bank[:, off:off + 512], src,
                            wr[:, hf * 512:(hf + 1) * 512], start=(dc == 0), stop=False))(),
                            [wrb, srcb], [bb], f"f_m{i}_{qtr}_{dc}_{ts}_{hf}")
            for ts in range(4):
                for hf in range(2):
                    bank, off, bb = PS[ts * 2 + hf]
                    em.op("pe", (lambda ts=ts, hf=hf, bank=bank, off=off: lambda e: e.matmul(
                        bank[:, off:off + 512], h1p[:, ts * 128:(ts + 1) * 128],
                        lbr1[:, hf * 512:(hf + 1) * 512], start=False, stop=False))(),
                        [h1p_b, lbr1_b], [bb], f"f_l1{i}_{qtr}_{ts}_{hf}")
                    em.op("pe", (lambda ts=ts, hf=hf, bank=bank, off=off: lambda e: e.matmul(
                        bank[:, off:off + 512], h2p[:, ts * 128:(ts + 1) * 128],
                        lbr2[:, hf * 512:(hf + 1) * 512], start=False, stop=True))(),
                        [h2p_b, lbr2_b], [bb], f"f_l2{i}_{qtr}_{ts}_{hf}")
                    seg = qtr * 2 + hf
                    em.op("act", (lambda bank=bank, off=off, ts=ts, qtr=qtr, hf=hf, seg=seg:
                                  lambda e: e.activation(
                        tsl[ts][0][:, qtr * 1024 + hf * 512: qtr * 1024 + (hf + 1) * 512],
                        bank[:, off:off + 512], AF.Copy,
                        accum_out=sm0[:, ts * 16 + seg:ts * 16 + seg + 1]))(),
                        [bb], [tsl[ts][1], sm0_b], f"f_e{i}_{qtr}_{ts}_{hf}")
                    em.op("act", (lambda bank=bank, off=off, ts=ts, seg=seg: lambda e: e.activation(
                        lnout[:, 0:512], bank[:, off:off + 512], AF.Square,
                        accum_out=sm0[:, ts * 16 + 8 + seg:ts * 16 + 8 + seg + 1]))(),
                        [bb], [lnout_b, sm0_b], f"f_q{i}_{qtr}_{ts}_{hf}")
        for ts in range(4):
            r0 = i * 512 + ts * 128
            rr, nm = ln_finish(ts * 16, ts * 16 + 8, f"o_L{i}_{ts}")
            em.op("act", (lambda ts=ts, rr=rr, nm=nm: lambda e: e.activation(
                lnout[:], tsl[ts][0][:], AF.Identity, bias=nm, scale=rr))(),
                [tsl[ts][1], gsm_b], [lnout_b], f"o_a{i}_{ts}")
            em.dma("pool", (lambda ts=ts: lambda e: e.dma_start(
                slab_a[:], x1scr[ts * 128:(ts + 1) * 128, :]))(),
                [x1scr_b], [slab_a_b], f"o_x{i}_{ts}")
            em.op("dve", lambda e: e.tensor_tensor(slab_b[:], slab_a[:], lnout[:], ALU.add),
                  [slab_a_b, lnout_b], [slab_b_b], f"o_r{i}_{ts}")
            em.dma("pool", (lambda r0=r0: lambda e: e.dma_start(out_ap[r0:r0 + 128, :], slab_b[:]))(),
                   [slab_b_b], [out_b], f"o_o{i}_{ts}")

    p2a_main(0)
    for i in range(NT):
        nch = 8 * (i + 1)
        p2b(i, nch, nch // 2)
        p2c_wo(i)
        if i + 1 < NT:
            p2a_main(i + 1)
        p2c_ffn(i)

    tap_map = {"lnscr": (lnscr, lnscr_tb, [T, D], BF16), "kscr": (kscr, [kscr_b], [NKVH, 128, T], BF16),
               "vscr": (vscr, [vscr_b], [NKVH, 128, T], BF16), "qscr": (qscr, [qscr_b], [2048, D], BF16),
               "lnoscr": (lnoscr, lnoscr_tb, [2048, D], BF16),
               "lataug": (lataug, [lataug_b], [D, 256], BF16), "lbsc": (lbsc, [lbsc_b], [NE * RANK, D], BF16),
               "bwT": (bwT_bf, [bwT_bf_b], [D, D], BF16),
               "x1scr": (x1scr, [x1scr_b], [512, D], F32),
               "ln4scr": (ln4scr, [ln4scr_b], [512, D], BF16)}
    for tname in taps:
        ap, bufs_, shp, dt = tap_map[tname]
        tout = nc.dram_tensor("tap_" + tname, shp, dt, kind="ExternalOutput").ap()
        tout_b = em.buf("tap_" + tname)
        em.dma("sp", (lambda ap=ap, tout=tout: lambda e: e.dma_start(tout, ap))(),
               list(bufs_), [tout_b], f"tap_{tname}")
        em.barrier("sp", [tout_b], f"tapw_{tname}")

    if dbg:
        em.barrier("sp", dbg_bufs, "final_dbg")
    em.barrier("sp", [out_b], "final_out")

    em.finalize(nc)
    with nc.Block() as blk:
        @blk.tensor
        def _(t):
            em.emit(t, "pe")

        @blk.scalar
        def _(s):
            em.emit(s, "act")

        @blk.vector
        def _(v):
            em.emit(v, "dve")

        @blk.gpsimd
        def _(g):
            em.emit(g, "pool")

        @blk.sync
        def _(s):
            em.emit(s, "sp")
    return nc


def _rope_tables():
    idx = (np.arange(0, 128, 2, dtype=np.float32) / np.float32(128.0))
    theta = np.float32(500000.0)
    freq = (np.float32(1.0 / (2.0 * np.pi)) * theta ** (-idx)).astype(np.float32)
    factor, lo, hi = np.float32(32.0), np.float32(1.0), np.float32(4.0)
    L0 = np.float32(8192.0)
    freq_low, freq_high = lo / L0, hi / L0
    freq_scaled = np.where(freq < freq_low, freq / factor, freq)
    smooth = np.clip((L0 * freq - lo) / (hi - lo), 0.0, 1.0).astype(np.float32)
    freq_smooth = ((1.0 - smooth) * (freq / factor) + smooth * freq).astype(np.float32)
    is_mid = (freq >= freq_low) & (freq <= freq_high)
    freqf = np.where(is_mid, freq_smooth, freq_scaled).astype(np.float32)
    pos = np.arange(T, dtype=np.float32)
    phase = (np.float32(2.0 * np.pi) * pos[:, None] * freqf[None, :]).astype(np.float32)
    emb = np.concatenate([phase, phase], axis=-1)
    return np.cos(emb).astype(np.float32), np.sin(emb).astype(np.float32)


def _own_rows(role):
    rows = []
    for i in range(NT):
        q0 = 1024 * i + 512 * role
        rows.extend(range(q0, q0 + 512))
    return np.asarray(rows)


def _masks(role):
    m = np.zeros((4, 128, 1024), dtype=np.float32)
    q = np.arange(512)
    kk = np.arange(128)
    for fr in range(4):
        for hh in range(2):
            f = 2 * fr + hh
            valid = (128 * f + kk[:, None]) <= (512 * role + q[None, :])
            m[fr][:, hh * 512:(hh + 1) * 512] = valid.astype(np.float32)
    return m.astype(ml_dtypes.bfloat16)


_BUILD_CACHE = {}


def prepare_in_maps(L0_attn_resid, L0_attn_Wq, L0_attn_Wk, L0_attn_Wv, L0_attn_Wo,
                    L0_ffn_base_W, L0_ffn_gate_W, L0_ffn_lora_A, L0_ffn_lora_B):
    x = np.ascontiguousarray(L0_attn_resid, dtype=np.float32)
    cosT, sinT = _rope_tables()
    sinm_fm = sinT.T.copy()
    sinm_fm[0:64, :] *= -1.0
    cos_fm = cosT.T.copy()
    ident = np.eye(128, dtype=np.float32)
    r1 = np.zeros((NE, 128), dtype=np.float32)
    for j in range(96):
        r1[j // 8, 32 + j] = 2.0
    r2 = np.zeros((NE, 128), dtype=np.float32)
    for j in range(128):
        r2[12 + j // 8, j] = 2.0

    la = np.ascontiguousarray(L0_ffn_lora_A, dtype=np.float32).reshape(NE * RANK, D)
    lb2 = np.ascontiguousarray(
        np.asarray(L0_ffn_lora_B, dtype=np.float32).transpose(0, 2, 1).reshape(NE * RANK, D))
    lataug_h = np.zeros((D, 256), dtype=np.float32)
    lataug_h[:, 0:NE] = np.asarray(L0_ffn_gate_W, np.float32).T
    lataug_h[:, 32:256] = la.T
    bwt_h = np.ascontiguousarray(np.asarray(L0_ffn_base_W, np.float32).T)

    bf = ml_dtypes.bfloat16
    common = dict(
        wq_in=np.ascontiguousarray(L0_attn_Wq, np.float32),
        wk_in=np.ascontiguousarray(L0_attn_Wk, np.float32),
        wv_in=np.ascontiguousarray(L0_attn_Wv, np.float32),
        wo_in=np.ascontiguousarray(L0_attn_Wo, np.float32),
        bwt_in=bwt_h, lataug_in=np.ascontiguousarray(lataug_h), lb2_in=lb2,
        cosf=cos_fm.astype(bf), sinf=sinm_fm.astype(bf),
        ident=ident, r1c=r1.astype(bf), r2c=r2.astype(bf),
    )
    in_maps = []
    own = [_own_rows(r) for r in (0, 1)]
    for core in range(8):
        s, r = core // 2, core % 2
        rows = own[r]
        sin_own = sinT[rows].copy()
        sin_own[:, 0:64] *= -1.0
        m = dict(common)
        m["x_in"] = x[s]
        m["xo_in"] = np.ascontiguousarray(x[s][rows])
        m["cost"] = cosT[rows].astype(bf)
        m["sint"] = sin_own.astype(bf)
        m["masks"] = _masks(r)
        in_maps.append(m)
    return in_maps, own


def kernel(**inputs):
    in_maps, own = prepare_in_maps(**inputs)
    if "nc" not in _BUILD_CACHE:
        _BUILD_CACHE["nc"] = build_program()
    nc = _BUILD_CACHE["nc"]
    res = run_bass_kernel_spmd(nc, in_maps, core_ids=list(range(8)))
    out = np.empty((B, T, D), dtype=np.float32)
    for core in range(8):
        s, r = core // 2, core % 2
        out[s][own[r]] = res.results[core]["out"]
    return out


# version 18
# speedup vs baseline: 1.0459x; 1.0063x over previous
import numpy as np
import ml_dtypes
import concourse.bass as bass
import concourse.mybir as mybir
from concourse.bass_utils import run_bass_kernel_spmd

F32 = mybir.dt.float32
BF16 = mybir.dt.bfloat16
AF = mybir.ActivationFunctionType
ALU = mybir.AluOpType
AX = mybir.AxisListType

B, T, D = 4, 4096, 4096
NQH, NKVH, DH = 32, 8, 128
GRP = NQH // NKVH
NT = 4
NE, RANK = 28, 8
EPS = 1e-5
ISQ = float(1.0 / np.sqrt(128.0))


class Buf:
    __slots__ = ("name", "last_w", "readers", "sem_name", "dma_count", "sink")

    def __init__(self, name, sink=False):
        self.name = name
        self.last_w = None
        self.readers = []
        self.sem_name = None
        self.dma_count = 0
        self.sink = sink


class Em:
    def __init__(self):
        self.ops = []

    def buf(self, name, sink=False):
        return Buf(name, sink)

    def op(self, engine, fn, reads=(), writes=(), tag=""):
        assert engine in ("pe", "act", "dve", "pool"), engine
        self._rec(engine, fn, reads, writes, tag, False)

    def dma(self, queue, fn, reads=(), writes=(), tag=""):
        assert queue in ("sp", "act", "pool"), queue
        self._rec(queue, fn, reads, writes, tag, True)

    def barrier(self, engine, bufs, tag="bar"):
        self._rec(engine, None, bufs, (), tag, False)

    def _rec(self, engine, fn, reads, writes, tag, dma):
        idx = len(self.ops)
        deps = set()
        for b in reads:
            if b.sink:
                continue
            if b.last_w is not None:
                deps.add(b.last_w)
        for b in writes:
            if b.sink:
                continue
            deps.update(b.readers)
            if b.last_w is not None:
                deps.add(b.last_w)
        for b in reads:
            if not b.sink:
                b.readers.append(idx)
        for b in writes:
            if not b.sink:
                b.last_w = idx
                b.readers = []
        self.ops.append(dict(engine=engine, fn=fn, deps=sorted(deps), dma=dma,
                             tag=tag, idx=idx, marked=False,
                             writes=[b for b in writes if not b.sink]))

    def finalize(self, nc):
        ops = self.ops
        for o in ops:
            for j in o["deps"]:
                ops[j]["marked"] = True
        eng_sem = {e: nc.alloc_semaphore(f"es_{e}") for e in ("pe", "act", "dve", "pool", "sp")}
        self.eng_sem = eng_sem
        self.buf_sems = {}
        eng_cnt = {e: 0 for e in eng_sem}
        nsem = 0
        for o in ops:
            if o["dma"]:
                wb = o["writes"]
                assert wb, f"dma without tracked write: {o['tag']}"
                tgt = wb[0]
                if tgt.sem_name is None:
                    tgt.sem_name = f"ds{nsem}"
                    nsem += 1
                    self.buf_sems[tgt.sem_name] = nc.alloc_semaphore(tgt.sem_name)
                tgt.dma_count += 16
                o["sem_key"] = tgt.sem_name
                o["sem_val"] = tgt.dma_count
                for b in wb[1:]:
                    b.sem_name = tgt.sem_name
                    b.dma_count = tgt.dma_count
            else:
                if o["marked"]:
                    eng_cnt[o["engine"]] += 1
                    o["sem_val"] = eng_cnt[o["engine"]]
                else:
                    o["sem_val"] = None
        for o in ops:
            waits = []
            for j in o["deps"]:
                p = ops[j]
                if p["dma"]:
                    waits.append(("d", p["sem_key"], p["sem_val"]))
                else:
                    if p["engine"] == "pe" and o["engine"] == "pe" and not o["dma"]:
                        continue
                    assert p["sem_val"] is not None
                    waits.append(("e", p["engine"], p["sem_val"]))
            o["waits"] = waits

    def emit(self, eng_obj, engine):
        water = {}
        for o in self.ops:
            if o["engine"] != engine:
                continue
            for (kind, key, val) in o["waits"]:
                sem = self.buf_sems[key] if kind == "d" else self.eng_sem[key]
                wk = (kind, key)
                if water.get(wk, 0) >= val:
                    continue
                water[wk] = val
                eng_obj.wait_ge(sem, val)
            if o["fn"] is None:
                continue
            inst = o["fn"](eng_obj)
            if o["dma"]:
                inst.then_inc(self.buf_sems[o["sem_key"]], 16)
            elif o["sem_val"] is not None:
                inst.then_inc(self.eng_sem[engine], 1)


def build_program(taps=(), dbg=False):
    nc = bass.Bass("TRN2", target_bir_lowering=False)
    em = Em()

    def inp(name, shape, dtype=F32):
        return nc.dram_tensor(name, shape, dtype, kind="ExternalInput").ap(), em.buf(name)

    x_in, x_in_b = inp("x_in", [T, D])
    xo_in, xo_in_b = inp("xo_in", [2048, D])
    wq_in, wq_in_b = inp("wq_in", [D, D])
    wk_in, wk_in_b = inp("wk_in", [D, 1024])
    wv_in, wv_in_b = inp("wv_in", [D, 1024])
    wo_in, wo_in_b = inp("wo_in", [D, D])
    bwt_in, bwt_in_b = inp("bwt_in", [D, D])
    lataug_in, lataug_in_b = inp("lataug_in", [D, 256])
    lb2_in, lb2_in_b = inp("lb2_in", [NE * RANK, D])
    cosf_in, cosf_b = inp("cosf", [128, T], BF16)
    sinf_in, sinf_b = inp("sinf", [128, T], BF16)
    cost_in, cost_b = inp("cost", [2048, 128], BF16)
    sint_in, sint_b = inp("sint", [2048, 128], BF16)
    masks_in, masks_b = inp("masks", [4, 128, 1024], BF16)
    ident_in, ident_b = inp("ident", [128, 128], F32)
    r1_in, r1_b = inp("r1c", [NE, 128], BF16)
    r2_in, r2_b = inp("r2c", [NE, 128], BF16)

    out_ap = nc.dram_tensor("out", [2048, D], F32, kind="ExternalOutput").ap()
    out_b = em.buf("out")
    if dbg:
        dbg_biasc = nc.dram_tensor("dbg_biasc", [128, 1], F32, kind="ExternalOutput").ap()
        dbg_pt = nc.dram_tensor("dbg_pt", [128, 1024], BF16, kind="ExternalOutput").ap()
        dbg_l = nc.dram_tensor("dbg_l", [1, 512], F32, kind="ExternalOutput").ap()
        dbg_ctx = nc.dram_tensor("dbg_ctx", [128, 512], BF16, kind="ExternalOutput").ap()
        dbg_bufs = [em.buf(f"dbg{j}") for j in range(4)]

    def scr(name, shape, dtype):
        return nc.dram_tensor(name, shape, dtype, kind="Internal").ap(), em.buf(name)

    wq_bf, wq_bf_b = scr("wq_bf", [D, D], BF16)
    wk_bf, wk_bf_b = scr("wk_bf", [D, 1024], BF16)
    wv_bf, wv_bf_b = scr("wv_bf", [D, 1024], BF16)
    wo_bf, wo_bf_b = scr("wo_bf", [D, D], BF16)
    bwT_bf, bwT_bf_b = scr("bwT_bf", [D, D], BF16)
    lataug, lataug_b = scr("lataug", [D, 256], BF16)
    lbsc, lbsc_b = scr("lbsc", [NE * RANK, D], BF16)
    lnscr, _ = scr("lnscr", [T, D], BF16)
    lnscr_tb = [em.buf(f"lnscr{t}") for t in range(8)]
    lnoscr, _ = scr("lnoscr", [2048, D], BF16)
    lnoscr_tb = [em.buf(f"lnoscr{i}") for i in range(NT)]
    kscr, kscr_b = scr("kscr", [NKVH, 128, T], BF16)
    vscr, vscr_b = scr("vscr", [NKVH, 128, T], BF16)
    qscr, qscr_b = scr("qscr", [2048, D], BF16)
    ln4scr, _ = scr("ln4scr", [512, D], BF16)
    ln4scr_b = em.buf("ln4scr")
    x1scr, _ = scr("x1scr", [512, D], F32)
    x1scr_b = em.buf("x1scr")

    def sbuf(name, shape, dtype):
        return nc.alloc_sbuf_tensor(name, shape, dtype), em.buf(name)

    cosP = [sbuf("cosP0", [128, 512], BF16), sbuf("cosP1", [128, 512], BF16)]
    sinP = [sbuf("sinP0", [128, 512], BF16), sbuf("sinP1", [128, 512], BF16)]
    cos_tm, cos_tm_b = sbuf("cos_tm", [128, 4 * 128], BF16)
    sin_tm, sin_tm_b = sbuf("sin_tm", [128, 4 * 128], BF16)
    masks_sb, masks_sb_b = sbuf("masks_sb", [128, 4 * 1024], BF16)
    ident_sb, ident_sb_b = sbuf("ident_sb", [128, 128], F32)
    ones_bf, ones_bf_b = sbuf("ones_bf", [128, 1], BF16)
    onesr_f, onesr_f_b = sbuf("onesr_f", [1, 128], F32)
    cpool, cpool_b = sbuf("cpool", [128, 2], F32)
    r1_sb, r1_sb_b = sbuf("r1_sb", [NE, 128], BF16)
    r2_sb, r2_sb_b = sbuf("r2_sb", [NE, 128], BF16)
    knorm, knorm_b = sbuf("knorm", [1, 64], F32)
    qnorm, qnorm_b = sbuf("qnorm", [1, NT], F32)
    biasc, biasc_b = sbuf("biasc", [128, 1], F32)
    tiny, tiny_b = sbuf("tiny", [1, 16], F32)
    slab_a, slab_a_b = sbuf("slab_a", [128, D], F32)
    slab_b, slab_b_b = sbuf("slab_b", [128, D], F32)
    tsl = []
    for i in range(4):
        tsl.append(sbuf(f"tsl{i}", [128, D], BF16))
    lnout, lnout_b = sbuf("lnout", [128, D], BF16)
    fm0, fm0_b = sbuf("fm0", [128, 32 * 512], BF16)
    wrow0, wrow0_b = sbuf("wrow0", [128, 1024], BF16)
    wrow1, wrow1_b = sbuf("wrow1", [128, 1024], BF16)
    wrow2, wrow2_b = sbuf("wrow2", [128, 1024], BF16)
    wrow3, wrow3_b = sbuf("wrow3", [128, 1024], BF16)
    wrow4, wrow4_b = sbuf("wrow4", [128, 1024], BF16)
    wrow5, wrow5_b = sbuf("wrow5", [128, 1024], BF16)
    WR4 = [(wrow0, wrow0_b), (wrow1, wrow1_b), (wrow2, wrow2_b), (wrow3, wrow3_b),
           (wrow4, wrow4_b), (wrow5, wrow5_b)]
    augr0, augr0_b = sbuf("augr0", [128, 256], BF16)
    augr1, augr1_b = sbuf("augr1", [128, 256], BF16)
    lbr1, lbr1_b = sbuf("lbr1", [128, 1024], BF16)
    lbr2, lbr2_b = sbuf("lbr2", [128, 1024], BF16)
    kh = [sbuf("kh0", [128, T], BF16), sbuf("kh1", [128, T], BF16)]
    vh = [sbuf("vh0", [128, T], BF16), sbuf("vh1", [128, T], BF16)]
    qt = [sbuf("qt0", [128, 512], BF16), sbuf("qt1", [128, 512], BF16)]
    pt = [sbuf("pt0", [128, 1024], BF16), sbuf("pt1", [128, 1024], BF16),
          sbuf("pt2", [128, 1024], BF16)]
    vstage, vstage_b = sbuf("vstage", [128, 2 * 512], BF16)
    stage0, stage0_b = sbuf("stage0", [128, 1024], BF16)
    stage1, stage1_b = sbuf("stage1", [128, 1024], BF16)
    krotB = [sbuf("krot0", [128, 512], BF16), sbuf("krot1", [128, 512], BF16)]
    rt1B = [sbuf("rt1a", [128, 512], BF16), sbuf("rt1b", [128, 512], BF16)]
    rt2B = [sbuf("rt2a", [128, 512], BF16), sbuf("rt2b", [128, 512], BF16)]
    kstB = [sbuf("kst0", [128, 512], BF16), sbuf("kst1", [128, 512], BF16)]
    krot, krot_b = krotB[0]
    rt1, rt1_b = rt1B[0]
    rt2, rt2_b = rt2B[0]
    psum2 = [sbuf("psum2a", [128, 512], BF16), sbuf("psum2b", [128, 512], BF16)]
    l_sb, l_sb_b = sbuf("l_sb", [1, 512], F32)
    linv, linv_b = sbuf("linv", [1, 512], F32)
    linvb, linvb_b = sbuf("linvb", [128, 512], BF16)
    sm0, sm0_b = sbuf("sm0", [128, 80], F32)
    gsm, gsm_b = sbuf("gsm", [128, 40], F32)
    gtmp, gtmp_b = sbuf("gtmp", [128, NE], F32)
    gtmp2, gtmp2_b = sbuf("gtmp2", [128, NE], F32)
    gate_tm, gate_tm_b = sbuf("gate_tm", [128, NE], F32)
    lgt, lgt_b = sbuf("lgt", [NE, 512], F32)
    g28, g28_b = sbuf("g28", [NE, 512], BF16)
    gf1, gf1_b = sbuf("gf1", [128, 512], BF16)
    gf2, gf2_b = sbuf("gf2", [128, 512], BF16)
    h1p, h1p_b = sbuf("h1p", [128, 512], BF16)
    h2p, h2p_b = sbuf("h2p", [128, 512], BF16)
    qacc, qacc_b = sbuf("qacc", [128, 32], F32)

    pA = nc.alloc_psum_tensor("pA", [128, 1024], F32)
    pB = nc.alloc_psum_tensor("pB", [128, 1024], F32)
    pC = nc.alloc_psum_tensor("pC", [128, 1024], F32)
    pD = nc.alloc_psum_tensor("pD", [128, 1024], F32)
    pbufs = {}
    for nm in ("A0", "A1", "B0", "B1", "C0", "C1", "D0", "D1"):
        pbufs[nm] = em.buf("p" + nm)
    PS = [(pA, 0, pbufs["A0"]), (pA, 512, pbufs["A1"]),
          (pB, 0, pbufs["B0"]), (pB, 512, pbufs["B1"]),
          (pC, 0, pbufs["C0"]), (pC, 512, pbufs["C1"]),
          (pD, 0, pbufs["D0"]), (pD, 512, pbufs["D1"])]

    em.dma("sp", lambda e: e.dma_start(masks_sb[:].rearrange("p (m f) -> p m f", m=4), masks_in.rearrange("m p f -> p m f")),
           [masks_b], [masks_sb_b], "c4")
    em.dma("sp", lambda e: e.dma_start(ident_sb[:], ident_in), [ident_b], [ident_sb_b], "c5")
    em.dma("sp", lambda e: e.dma_start(r1_sb[:], r1_in), [r1_b], [r1_sb_b], "c6")
    em.dma("sp", lambda e: e.dma_start(r2_sb[:], r2_in), [r2_b], [r2_sb_b], "c7")
    em.op("dve", lambda e: e.memset(ones_bf[:], 1.0), [], [ones_bf_b], "ones")
    em.op("dve", lambda e: e.memset(onesr_f[:], 1.0), [], [onesr_f_b], "onesr")
    em.op("dve", lambda e: e.memset(cpool[:, 0:1], EPS), [], [cpool_b], "eps")
    em.op("dve", lambda e: e.memset(qacc[:], 0.0), [], [qacc_b], "qacc0")
    em.op("dve", lambda e: e.memset(lbr1[0:32, :], 0.0), [], [lbr1_b], "lbr1z")

    def cast_w(src_ap, src_b, dst_ap, dst_b, rows, tag):
        for r0 in range(0, rows, 512):
            n = min(512, rows - r0)
            em.dma("pool", (lambda r0=r0, n=n: lambda e: e.dma_start(
                dst_ap[r0:r0 + n, :], src_ap[r0:r0 + n, :]))(),
                [src_b], [dst_b], f"cw_{tag}{r0}")

    wk_cb = [em.buf(f"wkc{j}") for j in range(8)]
    wv_cb = [em.buf(f"wvc{j}") for j in range(8)]
    for j in range(8):
        em.dma("pool", (lambda j=j: lambda e: e.dma_start(
            wk_bf[j * 512:(j + 1) * 512, :], wk_in[j * 512:(j + 1) * 512, :]))(),
            [wk_in_b], [wk_cb[j]], f"cw_k{j}")
        em.dma("pool", (lambda j=j: lambda e: e.dma_start(
            wv_bf[j * 512:(j + 1) * 512, :], wv_in[j * 512:(j + 1) * 512, :]))(),
            [wv_in_b], [wv_cb[j]], f"cw_v{j}")

    def p0_wq():
        cast_w(wq_in, wq_in_b, wq_bf, wq_bf_b, D, "q")

    def p0_wo():
        cast_w(wo_in, wo_in_b, wo_bf, wo_bf_b, D, "o")

    def p0_bwT():
        cast_w(bwt_in, bwt_in_b, bwT_bf, bwT_bf_b, D, "bt")

    def p0_ffn_small():
        cast_w(lataug_in, lataug_in_b, lataug, lataug_b, D, "lat")
        cast_w(lb2_in, lb2_in_b, lbsc, lbsc_b, NE * RANK, "lb2")

    P0_DEFER = {0: p0_wq, 1: p0_wo, 2: p0_bwT, 3: p0_ffn_small,
                4: lambda: p2a_ln(0), 5: lambda: p2a_ln(1),
                6: lambda: p2a_ln(2), 7: lambda: p2a_ln(3)}

    def ln_tile(x_ap, x_b, out_ap, out_bb, tag, dump_ap=None, dump_b=None, col=64):
        if dump_ap is None:
            dump_ap, dump_b = lnout[:], lnout_b
        sq, s1 = sm0[:, col:col + 1], sm0[:, col + 1:col + 2]
        mm_, m2 = sm0[:, col + 2:col + 3], sm0[:, col + 3:col + 4]
        vv, rr, nm = sm0[:, col + 4:col + 5], sm0[:, col + 5:col + 6], sm0[:, col + 6:col + 7]
        em.op("act", lambda e: e.activation(dump_ap, x_ap, AF.Square, accum_out=sq),
              [x_b], [sm0_b, dump_b], f"{tag}sq")
        em.op("dve", lambda e: e.reduce_sum(s1, x_ap, axis=AX.X), [x_b], [sm0_b], f"{tag}s1")
        em.op("dve", lambda e: e.tensor_scalar(mm_, s1, 1.0 / D, None, ALU.mult),
              [sm0_b], [sm0_b], f"{tag}m")
        em.op("dve", lambda e: e.tensor_tensor(m2, mm_, mm_, ALU.mult), [sm0_b], [sm0_b], f"{tag}m2")
        em.op("dve", lambda e: e.scalar_tensor_tensor(vv, sq, 1.0 / D, m2, ALU.mult, ALU.subtract),
              [sm0_b], [sm0_b], f"{tag}v")
        em.op("act", lambda e: e.activation(vv, vv, AF.Ln, bias=cpool[:, 0:1]),
              [sm0_b, cpool_b], [sm0_b], f"{tag}ln")
        em.op("act", lambda e: e.activation(rr, vv, AF.Exp, scale=-0.5), [sm0_b], [sm0_b], f"{tag}ex")
        em.op("dve", lambda e: e.scalar_tensor_tensor(nm, mm_, -1.0, rr, ALU.mult, ALU.mult),
              [sm0_b], [sm0_b], f"{tag}nm")
        em.op("act", lambda e: e.activation(out_ap, x_ap, AF.Identity, bias=nm, scale=rr),
              [x_b, sm0_b], [out_bb], f"{tag}ap")

    def ln_finish(scol, sqcol, tag):
        s1, sq = gsm[:, 8:9], gsm[:, 9:10]
        mm_, m2 = gsm[:, 10:11], gsm[:, 11:12]
        vv, rr, nm = gsm[:, 12:13], gsm[:, 13:14], gsm[:, 14:15]
        em.op("dve", lambda e: e.reduce_sum(s1, sm0[:, scol:scol + 8], axis=AX.X),
              [sm0_b], [gsm_b], f"{tag}rs")
        em.op("dve", lambda e: e.reduce_sum(sq, sm0[:, sqcol:sqcol + 8], axis=AX.X),
              [sm0_b], [gsm_b], f"{tag}rq")
        em.op("dve", lambda e: e.tensor_scalar(mm_, s1, 1.0 / D, None, ALU.mult),
              [gsm_b], [gsm_b], f"{tag}m")
        em.op("dve", lambda e: e.tensor_tensor(m2, mm_, mm_, ALU.mult), [gsm_b], [gsm_b], f"{tag}m2")
        em.op("dve", lambda e: e.scalar_tensor_tensor(vv, sq, 1.0 / D, m2, ALU.mult, ALU.subtract),
              [gsm_b], [gsm_b], f"{tag}v")
        em.op("act", lambda e: e.activation(vv, vv, AF.Ln, bias=cpool[:, 0:1]),
              [gsm_b, cpool_b], [gsm_b], f"{tag}ln")
        em.op("act", lambda e: e.activation(rr, vv, AF.Exp, scale=-0.5), [gsm_b], [gsm_b], f"{tag}ex")
        em.op("dve", lambda e: e.scalar_tensor_tensor(nm, mm_, -1.0, rr, ALU.mult, ALU.mult),
              [gsm_b], [gsm_b], f"{tag}nm")
        return rr, nm

    aslab, aslab_b = tsl[2]
    alnout, alnout_b = tsl[3]

    def p2a_ln(i):
        for ts in range(4):
            r0 = i * 512 + ts * 128
            em.dma("pool", (lambda r0=r0: lambda e: e.dma_start(aslab[:], xo_in[r0:r0 + 128, :]))(),
                   [xo_in_b], [aslab_b], f"a_x{i}_{ts}")
            ln_tile(aslab[:], aslab_b, alnout[:], alnout_b, f"aL{i}_{ts}",
                    dump_ap=alnout[:], dump_b=alnout_b, col=72)
            em.dma("act", (lambda r0=r0: lambda e: e.dma_start(lnoscr[r0:r0 + 128, :], alnout[:]))(),
                   [alnout_b], [lnoscr_tb[i]], f"a_s{i}_{ts}")


    GB = [kh[0], kh[1], vh[0], vh[1]]

    def fmsl(group, dc, c0, c1):
        if group == 0:
            return fm0[:, dc * 512 + c0: dc * 512 + c1], fm0_b
        t, tb = GB[dc // 8]
        return t[:, (dc % 8) * 512 + c0: (dc % 8) * 512 + c1], tb

    def p1_ln(t8):
        t0 = t8 * 512
        for sub in range(4):
            sl, slb = (slab_a, slab_a_b) if sub % 2 == 0 else (slab_b, slab_b_b)
            r0 = t0 + sub * 128
            em.dma("pool", (lambda r0=r0, sl=sl: lambda e: e.dma_start(sl[:], x_in[r0:r0 + 128, :]))(),
                   [x_in_b], [slb], f"p1x{t8}_{sub}")
            ln_tile(sl[:], slb, lnout[:], lnout_b, f"p1L{t8}_{sub}", col=sub * 7)
            em.dma("act", (lambda r0=r0, t8=t8: lambda e: e.dma_start(lnscr[r0:r0 + 128, :], lnout[:]))(),
                   [lnout_b], [lnscr_tb[t8]], f"p1s{t8}_{sub}")

    def p1_fm(t8):
        t0 = t8 * 512
        for dc in range(32):
            dst, dstb = fmsl(t8 % 2, dc, 0, 512)
            em.dma("act", (lambda dc=dc, t0=t0, dst=dst: lambda e: e.dma_start(
                dst, lnscr[t0:t0 + 512, dc * 128:(dc + 1) * 128], transpose=True))(),
                [lnscr_tb[t8]], [dstb], f"p1f{t8}_{dc}")

    p1_ln(0)
    p1_fm(0)
    for t8 in range(8):
        t0 = t8 * 512
        grp_ = t8 % 2
        for ic in range(32):
            wr, wrb = WR4[ic % 3]
            em.dma("sp", (lambda ic=ic, wr=wr: lambda e: e.dma_start(
                wr[:], wk_bf[ic * 128:(ic + 1) * 128, :]))(),
                [wk_cb[ic // 4]], [wrb], f"p1wk{t8}_{ic}")
            src, srcb = fmsl(grp_, ic, 0, 512)
            for h in range(8):
                bank, off, bb = PS[h]
                em.op("pe", (lambda ic=ic, h=h, wr=wr, bank=bank, off=off, src=src: lambda e: e.matmul(
                    bank[:, off:off + 512], wr[:, h * 128:(h + 1) * 128],
                    src, start=(ic == 0), stop=(ic == 31)))(),
                    [wrb, srcb], [bb], f"p1k{t8}_{ic}_{h}")
        cp, cpb = cosP[t8 % 2]
        sp_, spb = sinP[t8 % 2]
        em.dma("sp", (lambda t0=t0, cp=cp: lambda e: e.dma_start(cp[:], cosf_in[:, t0:t0 + 512]))(),
               [cosf_b], [cpb], f"p1cp{t8}")
        em.dma("sp", (lambda t0=t0, sp_=sp_: lambda e: e.dma_start(sp_[:], sinf_in[:, t0:t0 + 512]))(),
               [sinf_b], [spb], f"p1sp{t8}")
        for h in range(8):
            bank, off, bb = PS[h]
            stg = stage1[:, (h % 2) * 512:(h % 2) * 512 + 512]
            kro, krob = krotB[h % 2]
            r1_, r1b = rt1B[h % 2]
            r2_, r2b = rt2B[h % 2]
            kst, kstb = kstB[h % 2]
            em.op("act", (lambda bank=bank, off=off, stg=stg: lambda e: e.copy(stg, bank[:, off:off + 512]))(),
                  [bb], [stage1_b], f"p1ke{t8}_{h}")
            em.op("dve", (lambda stg=stg, r1_=r1_, cp=cp: lambda e: e.tensor_tensor(
                r1_[:], stg, cp[:], ALU.mult))(),
                [stage1_b, cpb], [r1b], f"p1r1{t8}_{h}")
            em.op("act", (lambda h=h, kro=kro: lambda e: e.copy(
                kro[0:64, :], stage1[64:128, (h % 2) * 512:(h % 2) * 512 + 512]))(),
                [stage1_b], [krob], f"p1ra{t8}_{h}")
            em.op("act", (lambda h=h, kro=kro: lambda e: e.copy(
                kro[64:128, :], stage1[0:64, (h % 2) * 512:(h % 2) * 512 + 512]))(),
                [stage1_b], [krob], f"p1rb{t8}_{h}")
            em.op("dve", (lambda kro=kro, r2_=r2_, sp_=sp_: lambda e: e.tensor_tensor(
                r2_[:], kro[:], sp_[:], ALU.mult))(),
                [krob, spb], [r2b], f"p1rc{t8}_{h}")
            em.op("dve", (lambda kst=kst, r1_=r1_, r2_=r2_: lambda e: e.tensor_tensor(
                kst[:], r1_[:], r2_[:], ALU.add))(),
                [r1b, r2b], [kstb], f"p1ro{t8}_{h}")
            em.dma("pool", (lambda h=h, t0=t0, kst=kst: lambda e: e.dma_start(
                kscr[h, :, t0:t0 + 512], kst[:]))(),
                [kstb], [kscr_b], f"p1kw{t8}_{h}")
            em.op("dve", (lambda kst=kst, h=h: lambda e: e.tensor_tensor(
                stage0[:, (h % 2) * 512:(h % 2) * 512 + 512], kst[:], kst[:], ALU.mult))(),
                [kstb], [stage0_b], f"p1n2{t8}_{h}")
            em.op("pe", (lambda bank=bank, off=off, h=h: lambda e: e.matmul(
                bank[0:1, off:off + 512], ones_bf[:],
                stage0[:, (h % 2) * 512:(h % 2) * 512 + 512], start=True, stop=True))(),
                [stage0_b, ones_bf_b], [bb], f"p1nm{t8}_{h}")
            em.op("dve", (lambda bank=bank, off=off, t8=t8, h=h: lambda e: e.reduce_max(
                knorm[0:1, t8 * 8 + h:t8 * 8 + h + 1], bank[0:1, off:off + 512], axis=AX.X))(),
                [bb], [knorm_b], f"p1nx{t8}_{h}")
        if t8 + 1 < 8:
            p1_ln(t8 + 1)
            p1_fm(t8 + 1)
        for ic in range(32):
            wr, wrb = WR4[3 + ic % 3]
            em.dma("sp", (lambda ic=ic, wr=wr: lambda e: e.dma_start(
                wr[:], wv_bf[ic * 128:(ic + 1) * 128, :]))(),
                [wv_cb[ic // 4]], [wrb], f"p1wv{t8}_{ic}")
            for ts in range(4):
                src, srcb = fmsl(grp_, ic, ts * 128, (ts + 1) * 128)
                for hf in range(2):
                    bank, off, bb = PS[ts * 2 + hf]
                    em.op("pe", (lambda ic=ic, ts=ts, hf=hf, wr=wr, bank=bank, off=off, src=src:
                                 lambda e: e.matmul(
                        bank[:, off:off + 512], src,
                        wr[:, hf * 512:(hf + 1) * 512], start=(ic == 0), stop=(ic == 31)))(),
                        [wrb, srcb], [bb], f"p1v{t8}_{ic}_{ts}_{hf}")
        for hw in range(4):
            hf, hp = hw // 2, (hw % 2) * 2
            for ts in range(4):
                bank, off, bb = PS[ts * 2 + hf]
                for hh in (hp, hp + 1):
                    em.op("act", (lambda bank=bank, off=off, hh=hh, hp=hp, ts=ts: lambda e: e.copy(
                        vstage[:, (hh - hp) * 512 + ts * 128: (hh - hp) * 512 + (ts + 1) * 128],
                        bank[:, off + hh * 128: off + (hh + 1) * 128]))(),
                        [bb], [vstage_b], f"p1ve{t8}_{ts}_{hf*4+hh}")
            for hh in (hp, hp + 1):
                h = hf * 4 + hh
                em.dma("act", (lambda h=h, hh=hh, hp=hp, t0=t0: lambda e: e.dma_start(
                    vscr[h, :, t0:t0 + 512], vstage[:, (hh - hp) * 512:(hh - hp + 1) * 512]))(),
                    [vstage_b], [vscr_b], f"p1vw{t8}_{h}")
        if t8 in P0_DEFER:
            P0_DEFER[t8]()

    def p2a_main(i):
        for dc in range(32):
            em.dma("act", (lambda dc=dc, i=i: lambda e: e.dma_start(
                fm0[:, dc * 512:(dc + 1) * 512],
                lnoscr[i * 512:(i + 1) * 512, dc * 128:(dc + 1) * 128], transpose=True))(),
                [lnoscr_tb[i]], [fm0_b], f"a_f{i}_{dc}")
        em.op("dve", lambda e: e.memset(qacc[:], 0.0), [], [qacc_b], f"a_q0{i}")
        em.dma("sp", (lambda i=i: lambda e: e.dma_start(
            cos_tm[:].rearrange("p (c d) -> p c d", c=4),
            cost_in[i * 512:(i + 1) * 512, :].rearrange("(c p) d -> p c d", p=128)))(),
            [cost_b], [cos_tm_b], f"a_ct{i}")
        em.dma("sp", (lambda i=i: lambda e: e.dma_start(
            sin_tm[:].rearrange("p (c d) -> p c d", c=4),
            sint_in[i * 512:(i + 1) * 512, :].rearrange("(c p) d -> p c d", p=128)))(),
            [sint_b], [sin_tm_b], f"a_st{i}")
        for qtr in range(4):
            for ic in range(32):
                wr, wrb = WR4[ic % 6]
                em.dma("sp", (lambda ic=ic, qtr=qtr, wr=wr: lambda e: e.dma_start(
                    wr[:], wq_bf[ic * 128:(ic + 1) * 128, qtr * 1024:(qtr + 1) * 1024]))(),
                    [wq_bf_b], [wrb], f"a_w{i}_{qtr}_{ic}")
                for ts in range(4):
                    for hf in range(2):
                        bank, off, bb = PS[ts * 2 + hf]
                        em.op("pe", (lambda ic=ic, ts=ts, hf=hf, wr=wr, bank=bank, off=off:
                                     lambda e: e.matmul(
                            bank[:, off:off + 512],
                            fm0[:, ic * 512 + ts * 128: ic * 512 + (ts + 1) * 128],
                            wr[:, hf * 512:(hf + 1) * 512], start=(ic == 0), stop=(ic == 31)))(),
                            [wrb, fm0_b], [bb], f"a_m{i}_{qtr}_{ic}_{ts}_{hf}")
            for ts in range(4):
                sub = ts
                for hf in range(2):
                    bank, off, bb = PS[ts * 2 + hf]
                    em.op("act", (lambda bank=bank, off=off, hf=hf: lambda e: e.copy(
                        stage0[:, hf * 512:(hf + 1) * 512], bank[:, off:off + 512]))(),
                        [bb], [stage0_b], f"a_e{i}_{qtr}_{ts}_{hf}")
                cs = cos_tm[:, sub * 128:(sub + 1) * 128]
                sn = sin_tm[:, sub * 128:(sub + 1) * 128]
                for hb in range(8):
                    c0 = hb * 128
                    em.op("dve", (lambda c0=c0, cs=cs: lambda e: e.tensor_tensor(
                        rt1[:, 0:128], stage0[:, c0:c0 + 128], cs, ALU.mult))(),
                        [stage0_b, cos_tm_b], [rt1_b], f"a_r1{i}_{qtr}_{ts}_{hb}")
                    em.op("dve", (lambda c0=c0, sn=sn: lambda e: e.tensor_tensor(
                        rt2[:, 0:64], stage0[:, c0 + 64:c0 + 128], sn[:, 0:64], ALU.mult))(),
                        [stage0_b, sin_tm_b], [rt2_b], f"a_r2{i}_{qtr}_{ts}_{hb}")
                    em.op("dve", (lambda c0=c0, sn=sn: lambda e: e.tensor_tensor(
                        rt2[:, 64:128], stage0[:, c0:c0 + 64], sn[:, 64:128], ALU.mult))(),
                        [stage0_b, sin_tm_b], [rt2_b], f"a_r3{i}_{qtr}_{ts}_{hb}")
                    em.op("dve", (lambda c0=c0: lambda e: e.tensor_tensor(
                        stage1[:, c0:c0 + 128], rt1[:, 0:128], rt2[:, 0:128], ALU.add))(),
                        [rt1_b, rt2_b], [stage1_b], f"a_ro{i}_{qtr}_{ts}_{hb}")
                em.dma("pool", (lambda i=i, ts=ts, qtr=qtr: lambda e: e.dma_start(
                    qscr[i * 512 + ts * 128:i * 512 + (ts + 1) * 128, qtr * 1024:(qtr + 1) * 1024],
                    stage1[:]))(),
                    [stage1_b], [qscr_b], f"a_qs{i}_{qtr}_{ts}")
                em.op("dve", lambda e: e.tensor_tensor(pt[0][0][:], stage1[:], stage1[:], ALU.mult),
                      [stage1_b], [pt[0][1]], f"a_n2{i}_{qtr}_{ts}")
                em.op("dve", lambda e: e.reduce_sum(gsm[:, 0:8],
                                                    pt[0][0][:].rearrange("p (s d) -> p s d", s=8),
                                                    axis=AX.X),
                      [pt[0][1]], [gsm_b], f"a_nr{i}_{qtr}_{ts}")
                em.op("dve", (lambda qtr=qtr: lambda e: e.tensor_tensor(
                    qacc[:, qtr * 8:(qtr + 1) * 8], qacc[:, qtr * 8:(qtr + 1) * 8],
                    gsm[:, 0:8], ALU.max))(),
                    [qacc_b, gsm_b], [qacc_b], f"a_nx{i}_{qtr}_{ts}")
        em.op("dve", lambda e: e.reduce_max(gsm[:, 16:17], qacc[:], axis=AX.X),
              [qacc_b], [gsm_b], f"a_fx{i}")
        em.op("pe", lambda e: e.transpose(pD[0:8, 0:128], gsm[:, 16:24], ident_sb[:]),
              [gsm_b, ident_sb_b], [pbufs["D0"]], f"a_tr{i}")
        em.op("dve", (lambda i=i: lambda e: e.reduce_max(qnorm[0:1, i:i + 1], pD[0:1, 0:128], axis=AX.X))(),
              [pbufs["D0"]], [qnorm_b], f"a_qn{i}")
        em.op("dve", (lambda i=i: lambda e: e.reduce_max(tiny[0:1, 0:1],
                                                         knorm[0:1, 0:(2 * i + 2) * 8], axis=AX.X))(),
              [knorm_b], [tiny_b], f"b_k{i}")
        em.op("dve", (lambda i=i: lambda e: e.tensor_tensor(tiny[0:1, 1:2], tiny[0:1, 0:1],
                                                            qnorm[0:1, i:i + 1], ALU.mult))(),
              [tiny_b, qnorm_b], [tiny_b], f"b_m{i}")
        em.op("act", lambda e: e.activation(tiny[0:1, 2:3], tiny[0:1, 1:2], AF.Ln),
              [tiny_b], [tiny_b], f"b_l{i}")
        em.op("act", lambda e: e.activation(tiny[0:1, 3:4], tiny[0:1, 2:3], AF.Exp, scale=0.5),
              [tiny_b], [tiny_b], f"b_e{i}")
        em.op("dve", lambda e: e.tensor_scalar(tiny[0:1, 4:5], tiny[0:1, 3:4], -ISQ, None, ALU.mult),
              [tiny_b], [tiny_b], f"b_s{i}")
        em.op("pe", lambda e: e.matmul(pD[:, 512:513], onesr_f[:], tiny[0:1, 4:5],
                                       start=True, stop=True),
              [tiny_b, onesr_f_b], [pbufs["D1"]], f"b_b{i}")
        em.op("act", lambda e: e.copy(biasc[:], pD[:, 512:513]), [pbufs["D1"]], [biasc_b], f"b_c{i}")
        if dbg and i == 0:
            em.dma("sp", lambda e: e.dma_start(dbg_biasc, biasc[:]), [biasc_b], [dbg_bufs[0]], "dbgb")

    def p2b(i, nch, pairs):
        for h in range(NKVH):
            khh, khb = kh[h % 2]
            vhh, vhb = vh[h % 2]
            em.dma("sp", (lambda h=h, khh=khh, nch=nch: lambda e: e.dma_start(
                khh[:, 0:nch * 128], kscr[h, :, 0:nch * 128]))(),
                [kscr_b], [khb], f"t_k{i}_{h}")
            em.dma("sp", (lambda h=h, vhh=vhh, nch=nch: lambda e: e.dma_start(
                vhh[:, 0:nch * 128], vscr[h, :, 0:nch * 128]))(),
                [vscr_b], [vhb], f"t_v{i}_{h}")
            for g in range(GRP):
                hq = h * GRP + g
                qtt, qtb = qt[g % 2]
                em.dma("act", (lambda i=i, hq=hq, qtt=qtt: lambda e: e.dma_start(
                    qtt[:], qscr[i * 512:(i + 1) * 512, hq * 128:(hq + 1) * 128], transpose=True))(),
                    [qscr_b], [qtb], f"t_q{i}_{hq}")
                def score_pair(pr, khh=khh, khb=khb, qtt=qtt, qtb=qtb):
                    c0, c1 = 2 * pr, 2 * pr + 1
                    sbank, sb0, sb1 = (pA, pbufs["A0"], pbufs["A1"]) if pr % 2 == 0 else \
                                      (pB, pbufs["B0"], pbufs["B1"])
                    ptt, ptb = pt[pr % 3]
                    em.op("pe", (lambda c0=c0, sbank=sbank, khh=khh, qtt=qtt: lambda e: e.matmul(
                        sbank[:, 0:512], khh[:, c0 * 128:(c0 + 1) * 128], qtt[:],
                        start=True, stop=True))(),
                        [khb, qtb], [sb0], f"t_s{i}_{hq}_{c0}")
                    em.op("pe", (lambda c1=c1, sbank=sbank, khh=khh, qtt=qtt: lambda e: e.matmul(
                        sbank[:, 512:1024], khh[:, c1 * 128:(c1 + 1) * 128], qtt[:],
                        start=True, stop=True))(),
                        [khb, qtb], [sb1], f"t_s{i}_{hq}_{c1}")
                    em.op("act", (lambda sbank=sbank, ptt=ptt: lambda e: e.activation(
                        ptt[:], sbank[:], AF.Exp, bias=biasc[:], scale=ISQ))(),
                        [sb0, sb1, biasc_b], [ptb], f"t_e{i}_{hq}_{pr}")
                    if pr >= 4 * i:
                        fr = pr - 4 * i
                        em.op("dve", (lambda fr=fr, ptt=ptt: lambda e: e.tensor_tensor(
                            ptt[:], ptt[:], masks_sb[:, fr * 1024:(fr + 1) * 1024], ALU.mult))(),
                            [ptb, masks_sb_b], [ptb], f"t_m{i}_{hq}_{pr}")

                def consume_pair(pr, vhh=vhh, vhb=vhb, pairs=pairs):
                    c0, c1 = 2 * pr, 2 * pr + 1
                    ptt, ptb = pt[pr % 3]
                    ps2, ps2b = psum2[pr % 2]
                    em.op("dve", (lambda ptt=ptt, ps2=ps2: lambda e: e.tensor_tensor(
                        ps2[:], ptt[:, 0:512], ptt[:, 512:1024], ALU.add))(),
                        [ptb], [ps2b], f"t_la{i}_{hq}_{pr}")
                    em.op("pe", (lambda pr=pr, ps2=ps2, pairs=pairs: lambda e: e.matmul(
                        pD[0:1, 0:512], ones_bf[:], ps2[:],
                        start=(pr == 0), stop=(pr == pairs - 1)))(),
                        [ps2b, ones_bf_b], [pbufs["D0"]], f"t_l0{i}_{hq}_{pr}")
                    em.op("pe", (lambda pr=pr, c0=c0, ptt=ptt, vhh=vhh: lambda e: e.matmul(
                        pC[:, 0:512], vhh[:, c0 * 128:(c0 + 1) * 128], ptt[:, 0:512],
                        start=(pr == 0), stop=False))(),
                        [vhb, ptb], [pbufs["C0"]], f"t_c0{i}_{hq}_{pr}")
                    em.op("pe", (lambda pr=pr, c1=c1, ptt=ptt, vhh=vhh, pairs=pairs: lambda e: e.matmul(
                        pC[:, 0:512], vhh[:, c1 * 128:(c1 + 1) * 128], ptt[:, 512:1024],
                        start=False, stop=(pr == pairs - 1)))(),
                        [vhb, ptb], [pbufs["C0"]], f"t_c1{i}_{hq}_{pr}")

                for pr in range(pairs):
                    score_pair(pr)
                    if pr > 1:
                        consume_pair(pr - 2)
                if pairs > 1:
                    consume_pair(pairs - 2)
                consume_pair(pairs - 1)
                em.op("dve", lambda e: e.tensor_copy(l_sb[:], pD[0:1, 0:512]),
                      [pbufs["D0"]], [l_sb_b], f"t_le{i}_{hq}")
                if dbg and i == 0 and hq == 0:
                    em.dma("sp", lambda e: e.dma_start(dbg_l, l_sb[:]), [l_sb_b], [dbg_bufs[2]], "dbgl")
                em.op("dve", lambda e: e.reciprocal(linv[:], l_sb[:]), [l_sb_b], [linv_b], f"t_li{i}_{hq}")
                em.op("pe", lambda e: e.matmul(pD[:, 512:1024], onesr_f[:], linv[:],
                                               start=True, stop=True),
                      [linv_b, onesr_f_b], [pbufs["D1"]], f"t_lb{i}_{hq}")
                em.op("act", lambda e: e.copy(linvb[:], pD[:, 512:1024]),
                      [pbufs["D1"]], [linvb_b], f"t_lv{i}_{hq}")
                em.op("dve", (lambda hq=hq: lambda e: e.tensor_tensor(
                    fm0[:, hq * 512:(hq + 1) * 512], pC[:, 0:512], linvb[:], ALU.mult))(),
                    [pbufs["C0"], linvb_b], [fm0_b], f"t_cn{i}_{hq}")
                if dbg and i == 0 and hq == 0:
                    em.dma("sp", lambda e: e.dma_start(dbg_ctx, fm0[:, 0:512]), [fm0_b], [dbg_bufs[3]], "dbgc")

    def p2c_wo(i):
        for qtr in range(4):
            for dc in range(32):
                wr, wrb = WR4[dc % 6]
                em.dma("sp", (lambda dc=dc, qtr=qtr, wr=wr: lambda e: e.dma_start(
                    wr[:], wo_bf[dc * 128:(dc + 1) * 128, qtr * 1024:(qtr + 1) * 1024]))(),
                    [wo_bf_b], [wrb], f"c_w{i}_{qtr}_{dc}")
                for ts in range(4):
                    for hf in range(2):
                        bank, off, bb = PS[ts * 2 + hf]
                        em.op("pe", (lambda dc=dc, ts=ts, hf=hf, wr=wr, bank=bank, off=off:
                                     lambda e: e.matmul(
                            bank[:, off:off + 512],
                            fm0[:, dc * 512 + ts * 128: dc * 512 + (ts + 1) * 128],
                            wr[:, hf * 512:(hf + 1) * 512], start=(dc == 0), stop=(dc == 31)))(),
                            [wrb, fm0_b], [bb], f"c_m{i}_{qtr}_{dc}_{ts}_{hf}")
            for ts in range(4):
                for hf in range(2):
                    bank, off, bb = PS[ts * 2 + hf]
                    seg = qtr * 2 + hf
                    em.op("dve", (lambda bank=bank, off=off, ts=ts, qtr=qtr, hf=hf, seg=seg:
                                  lambda e: e.tensor_scalar(
                        tsl[ts][0][:, qtr * 1024 + hf * 512: qtr * 1024 + (hf + 1) * 512],
                        bank[:, off:off + 512], 1.0, None, ALU.mult, ALU.add,
                        accum_out=sm0[:, ts * 16 + seg:ts * 16 + seg + 1]))(),
                        [bb], [tsl[ts][1], sm0_b], f"c_e{i}_{qtr}_{ts}_{hf}")
                    em.op("act", (lambda bank=bank, off=off, ts=ts, seg=seg: lambda e: e.activation(
                        lnout[:, 0:512], bank[:, off:off + 512], AF.Square,
                        accum_out=sm0[:, ts * 16 + 8 + seg:ts * 16 + 8 + seg + 1]))(),
                        [bb], [lnout_b, sm0_b], f"c_q{i}_{qtr}_{ts}_{hf}")
        for ts in range(4):
            r0 = i * 512 + ts * 128
            rr, nm = ln_finish(ts * 16, ts * 16 + 8, f"c_L{i}_{ts}")
            em.op("act", (lambda ts=ts, rr=rr, nm=nm: lambda e: e.activation(
                lnout[:], tsl[ts][0][:], AF.Identity, bias=nm, scale=rr))(),
                [tsl[ts][1], gsm_b], [lnout_b], f"c_a{i}_{ts}")
            em.dma("pool", (lambda r0=r0: lambda e: e.dma_start(slab_a[:], xo_in[r0:r0 + 128, :]))(),
                   [xo_in_b], [slab_a_b], f"c_x{i}_{ts}")
            em.op("dve", lambda e: e.tensor_tensor(slab_b[:], slab_a[:], lnout[:], ALU.add),
                  [slab_a_b, lnout_b], [slab_b_b], f"c_r{i}_{ts}")
            em.dma("pool", (lambda ts=ts: lambda e: e.dma_start(
                x1scr[ts * 128:(ts + 1) * 128, :], slab_b[:]))(),
                [slab_b_b], [x1scr_b], f"c_1{i}_{ts}")
            ln_tile(slab_b[:], slab_b_b, lnout[:], lnout_b, f"c_N{i}_{ts}")
            em.dma("act", (lambda ts=ts: lambda e: e.dma_start(
                ln4scr[ts * 128:(ts + 1) * 128, :], lnout[:]))(),
                [lnout_b], [ln4scr_b], f"c_4{i}_{ts}")
    def p2c_ffn(i):
        for dc in range(32):
            dst, dstb = fmsl(1, dc, 0, 512)
            em.dma("act", (lambda dc=dc, dst=dst: lambda e: e.dma_start(
                dst, ln4scr[0:512, dc * 128:(dc + 1) * 128], transpose=True))(),
                [ln4scr_b], [dstb], f"c_f{i}_{dc}")
        for ic in range(32):
            ar, arb = (augr0, augr0_b) if ic % 2 == 0 else (augr1, augr1_b)
            em.dma("sp", (lambda ic=ic, ar=ar: lambda e: e.dma_start(
                ar[:], lataug[ic * 128:(ic + 1) * 128, :]))(),
                [lataug_b], [arb], f"f_a{i}_{ic}")
            src, srcb = fmsl(1, ic, 0, 512)
            em.op("pe", (lambda ic=ic, ar=ar, src=src: lambda e: e.matmul(
                pA[:, 0:512], ar[:, 0:128], src,
                start=(ic == 0), stop=(ic == 31)))(),
                [arb, srcb], [pbufs["A0"]], f"f_h1{i}_{ic}")
            em.op("pe", (lambda ic=ic, ar=ar, src=src: lambda e: e.matmul(
                pA[:, 512:1024], ar[:, 128:256], src,
                start=(ic == 0), stop=(ic == 31)))(),
                [arb, srcb], [pbufs["A1"]], f"f_h2{i}_{ic}")
        em.op("act", lambda e: e.copy(lgt[:], pA[0:NE, 0:512]), [pbufs["A0"]], [lgt_b], f"f_lg{i}")
        for ts in range(4):
            em.op("pe", (lambda ts=ts: lambda e: e.transpose(
                pC[0:128, 0:NE], lgt[:, ts * 128:(ts + 1) * 128], ident_sb[0:NE, 0:NE]))(),
                [lgt_b, ident_sb_b], [pbufs["C0"]], f"f_t{i}_{ts}")
            m1, m2 = gsm[:, 20:21], gsm[:, 21:22]
            d12, ee, p1 = gsm[:, 22:23], gsm[:, 23:24], gsm[:, 24:25]
            p2 = gsm[:, 25:26]
            em.op("dve", lambda e: e.reduce_max(m1, pC[0:128, 0:NE], axis=AX.X),
                  [pbufs["C0"]], [gsm_b], f"f_m1{i}_{ts}")
            em.op("dve", lambda e: e.tensor_scalar(gtmp[:], pC[0:128, 0:NE], m1, None, ALU.is_ge),
                  [pbufs["C0"], gsm_b], [gtmp_b], f"f_i1{i}_{ts}")
            em.op("dve", lambda e: e.scalar_tensor_tensor(gtmp2[:], gtmp[:], -1e30, pC[0:128, 0:NE],
                                                          ALU.mult, ALU.add),
                  [gtmp_b, pbufs["C0"]], [gtmp2_b], f"f_msk{i}_{ts}")
            em.op("dve", lambda e: e.reduce_max(m2, gtmp2[:], axis=AX.X), [gtmp2_b], [gsm_b], f"f_m2{i}_{ts}")
            em.op("dve", lambda e: e.tensor_scalar(gtmp2[:], gtmp2[:], m2, None, ALU.is_ge),
                  [gtmp2_b, gsm_b], [gtmp2_b], f"f_i2{i}_{ts}")
            em.op("dve", lambda e: e.tensor_tensor(d12, m1, m2, ALU.subtract), [gsm_b], [gsm_b], f"f_d{i}_{ts}")
            em.op("act", lambda e: e.activation(ee, d12, AF.Exp, scale=-1.0), [gsm_b], [gsm_b], f"f_e{i}_{ts}")
            em.op("dve", lambda e: e.tensor_scalar(ee, ee, 1.0, None, ALU.add), [gsm_b], [gsm_b], f"f_e1{i}_{ts}")
            em.op("dve", lambda e: e.reciprocal(p1, ee), [gsm_b], [gsm_b], f"f_p1{i}_{ts}")
            em.op("dve", lambda e: e.tensor_scalar(p2, p1, -1.0, 1.0, ALU.mult, ALU.add),
                  [gsm_b], [gsm_b], f"f_p2{i}_{ts}")
            em.op("dve", lambda e: e.tensor_scalar(gate_tm[:], gtmp[:], p1, None, ALU.mult),
                  [gtmp_b, gsm_b], [gate_tm_b], f"f_g1{i}_{ts}")
            em.op("dve", lambda e: e.scalar_tensor_tensor(gate_tm[:], gtmp2[:], p2, gate_tm[:],
                                                          ALU.mult, ALU.add),
                  [gtmp2_b, gsm_b, gate_tm_b], [gate_tm_b], f"f_g2{i}_{ts}")
            em.op("pe", (lambda ts=ts: lambda e: e.transpose(
                pD[0:NE, 0:128], gate_tm[:], ident_sb[:]))(),
                [gate_tm_b, ident_sb_b], [pbufs["D0"]], f"f_gt{i}_{ts}")
            em.op("act", (lambda ts=ts: lambda e: e.copy(g28[:, ts * 128:(ts + 1) * 128],
                                                         pD[0:NE, 0:128]))(),
                  [pbufs["D0"]], [g28_b], f"f_ge{i}_{ts}")
        em.op("pe", lambda e: e.matmul(pC[0:128, 0:512], r1_sb[:], g28[:], start=True, stop=True),
              [r1_sb_b, g28_b], [pbufs["C0"]], f"f_G1{i}")
        em.op("pe", lambda e: e.matmul(pC[0:128, 512:1024], r2_sb[:], g28[:], start=True, stop=True),
              [r2_sb_b, g28_b], [pbufs["C1"]], f"f_G2{i}")
        em.op("act", lambda e: e.copy(gf1[:], pC[0:128, 0:512]), [pbufs["C0"]], [gf1_b], f"f_ge1{i}")
        em.op("act", lambda e: e.copy(gf2[:], pC[0:128, 512:1024]), [pbufs["C1"]], [gf2_b], f"f_ge2{i}")
        em.op("dve", lambda e: e.tensor_tensor(h1p[:], pA[0:128, 0:512], gf1[:], ALU.mult),
              [pbufs["A0"], gf1_b], [h1p_b], f"f_hp1{i}")
        em.op("dve", lambda e: e.tensor_tensor(h2p[:], pA[0:128, 512:1024], gf2[:], ALU.mult),
              [pbufs["A1"], gf2_b], [h2p_b], f"f_hp2{i}")
        for qtr in range(4):
            em.dma("sp", (lambda qtr=qtr: lambda e: e.dma_start(
                lbr1[32:128, :], lbsc[0:96, qtr * 1024:(qtr + 1) * 1024]))(),
                [lbsc_b], [lbr1_b], f"f_b1{i}_{qtr}")
            em.dma("sp", (lambda qtr=qtr: lambda e: e.dma_start(
                lbr2[:], lbsc[96:224, qtr * 1024:(qtr + 1) * 1024]))(),
                [lbsc_b], [lbr2_b], f"f_b2{i}_{qtr}")
            for dc in range(32):
                wr, wrb = WR4[dc % 6]
                em.dma("sp", (lambda dc=dc, qtr=qtr, wr=wr: lambda e: e.dma_start(
                    wr[:], bwT_bf[dc * 128:(dc + 1) * 128, qtr * 1024:(qtr + 1) * 1024]))(),
                    [bwT_bf_b], [wrb], f"f_w{i}_{qtr}_{dc}")
                for ts in range(4):
                    src, srcb = fmsl(1, dc, ts * 128, (ts + 1) * 128)
                    for hf in range(2):
                        bank, off, bb = PS[ts * 2 + hf]
                        em.op("pe", (lambda dc=dc, ts=ts, hf=hf, wr=wr, bank=bank, off=off, src=src:
                                     lambda e: e.matmul(
                            bank[:, off:off + 512], src,
                            wr[:, hf * 512:(hf + 1) * 512], start=(dc == 0), stop=False))(),
                            [wrb, srcb], [bb], f"f_m{i}_{qtr}_{dc}_{ts}_{hf}")
            for ts in range(4):
                for hf in range(2):
                    bank, off, bb = PS[ts * 2 + hf]
                    em.op("pe", (lambda ts=ts, hf=hf, bank=bank, off=off: lambda e: e.matmul(
                        bank[:, off:off + 512], h1p[:, ts * 128:(ts + 1) * 128],
                        lbr1[:, hf * 512:(hf + 1) * 512], start=False, stop=False))(),
                        [h1p_b, lbr1_b], [bb], f"f_l1{i}_{qtr}_{ts}_{hf}")
                    em.op("pe", (lambda ts=ts, hf=hf, bank=bank, off=off: lambda e: e.matmul(
                        bank[:, off:off + 512], h2p[:, ts * 128:(ts + 1) * 128],
                        lbr2[:, hf * 512:(hf + 1) * 512], start=False, stop=True))(),
                        [h2p_b, lbr2_b], [bb], f"f_l2{i}_{qtr}_{ts}_{hf}")
                    seg = qtr * 2 + hf
                    em.op("dve", (lambda bank=bank, off=off, ts=ts, qtr=qtr, hf=hf, seg=seg:
                                  lambda e: e.tensor_scalar(
                        tsl[ts][0][:, qtr * 1024 + hf * 512: qtr * 1024 + (hf + 1) * 512],
                        bank[:, off:off + 512], 1.0, None, ALU.mult, ALU.add,
                        accum_out=sm0[:, ts * 16 + seg:ts * 16 + seg + 1]))(),
                        [bb], [tsl[ts][1], sm0_b], f"f_e{i}_{qtr}_{ts}_{hf}")
                    em.op("act", (lambda bank=bank, off=off, ts=ts, seg=seg: lambda e: e.activation(
                        lnout[:, 0:512], bank[:, off:off + 512], AF.Square,
                        accum_out=sm0[:, ts * 16 + 8 + seg:ts * 16 + 8 + seg + 1]))(),
                        [bb], [lnout_b, sm0_b], f"f_q{i}_{qtr}_{ts}_{hf}")
        for ts in range(4):
            r0 = i * 512 + ts * 128
            rr, nm = ln_finish(ts * 16, ts * 16 + 8, f"o_L{i}_{ts}")
            em.op("act", (lambda ts=ts, rr=rr, nm=nm: lambda e: e.activation(
                lnout[:], tsl[ts][0][:], AF.Identity, bias=nm, scale=rr))(),
                [tsl[ts][1], gsm_b], [lnout_b], f"o_a{i}_{ts}")
            em.dma("pool", (lambda ts=ts: lambda e: e.dma_start(
                slab_a[:], x1scr[ts * 128:(ts + 1) * 128, :]))(),
                [x1scr_b], [slab_a_b], f"o_x{i}_{ts}")
            em.op("dve", lambda e: e.tensor_tensor(slab_b[:], slab_a[:], lnout[:], ALU.add),
                  [slab_a_b, lnout_b], [slab_b_b], f"o_r{i}_{ts}")
            em.dma("pool", (lambda r0=r0: lambda e: e.dma_start(out_ap[r0:r0 + 128, :], slab_b[:]))(),
                   [slab_b_b], [out_b], f"o_o{i}_{ts}")

    p2a_main(0)
    for i in range(NT):
        nch = 8 * (i + 1)
        p2b(i, nch, nch // 2)
        p2c_wo(i)
        if i + 1 < NT:
            p2a_main(i + 1)
        p2c_ffn(i)

    tap_map = {"lnscr": (lnscr, lnscr_tb, [T, D], BF16), "kscr": (kscr, [kscr_b], [NKVH, 128, T], BF16),
               "vscr": (vscr, [vscr_b], [NKVH, 128, T], BF16), "qscr": (qscr, [qscr_b], [2048, D], BF16),
               "lnoscr": (lnoscr, lnoscr_tb, [2048, D], BF16),
               "lataug": (lataug, [lataug_b], [D, 256], BF16), "lbsc": (lbsc, [lbsc_b], [NE * RANK, D], BF16),
               "bwT": (bwT_bf, [bwT_bf_b], [D, D], BF16),
               "x1scr": (x1scr, [x1scr_b], [512, D], F32),
               "ln4scr": (ln4scr, [ln4scr_b], [512, D], BF16)}
    for tname in taps:
        ap, bufs_, shp, dt = tap_map[tname]
        tout = nc.dram_tensor("tap_" + tname, shp, dt, kind="ExternalOutput").ap()
        tout_b = em.buf("tap_" + tname)
        em.dma("sp", (lambda ap=ap, tout=tout: lambda e: e.dma_start(tout, ap))(),
               list(bufs_), [tout_b], f"tap_{tname}")
        em.barrier("sp", [tout_b], f"tapw_{tname}")

    if dbg:
        em.barrier("sp", dbg_bufs, "final_dbg")
    em.barrier("sp", [out_b], "final_out")

    em.finalize(nc)
    with nc.Block() as blk:
        @blk.tensor
        def _(t):
            em.emit(t, "pe")

        @blk.scalar
        def _(s):
            em.emit(s, "act")

        @blk.vector
        def _(v):
            em.emit(v, "dve")

        @blk.gpsimd
        def _(g):
            em.emit(g, "pool")

        @blk.sync
        def _(s):
            em.emit(s, "sp")
    return nc


def _rope_tables():
    idx = (np.arange(0, 128, 2, dtype=np.float32) / np.float32(128.0))
    theta = np.float32(500000.0)
    freq = (np.float32(1.0 / (2.0 * np.pi)) * theta ** (-idx)).astype(np.float32)
    factor, lo, hi = np.float32(32.0), np.float32(1.0), np.float32(4.0)
    L0 = np.float32(8192.0)
    freq_low, freq_high = lo / L0, hi / L0
    freq_scaled = np.where(freq < freq_low, freq / factor, freq)
    smooth = np.clip((L0 * freq - lo) / (hi - lo), 0.0, 1.0).astype(np.float32)
    freq_smooth = ((1.0 - smooth) * (freq / factor) + smooth * freq).astype(np.float32)
    is_mid = (freq >= freq_low) & (freq <= freq_high)
    freqf = np.where(is_mid, freq_smooth, freq_scaled).astype(np.float32)
    pos = np.arange(T, dtype=np.float32)
    phase = (np.float32(2.0 * np.pi) * pos[:, None] * freqf[None, :]).astype(np.float32)
    emb = np.concatenate([phase, phase], axis=-1)
    return np.cos(emb).astype(np.float32), np.sin(emb).astype(np.float32)


def _own_rows(role):
    rows = []
    for i in range(NT):
        q0 = 1024 * i + 512 * role
        rows.extend(range(q0, q0 + 512))
    return np.asarray(rows)


def _masks(role):
    m = np.zeros((4, 128, 1024), dtype=np.float32)
    q = np.arange(512)
    kk = np.arange(128)
    for fr in range(4):
        for hh in range(2):
            f = 2 * fr + hh
            valid = (128 * f + kk[:, None]) <= (512 * role + q[None, :])
            m[fr][:, hh * 512:(hh + 1) * 512] = valid.astype(np.float32)
    return m.astype(ml_dtypes.bfloat16)


_BUILD_CACHE = {}


def prepare_in_maps(L0_attn_resid, L0_attn_Wq, L0_attn_Wk, L0_attn_Wv, L0_attn_Wo,
                    L0_ffn_base_W, L0_ffn_gate_W, L0_ffn_lora_A, L0_ffn_lora_B):
    x = np.ascontiguousarray(L0_attn_resid, dtype=np.float32)
    cosT, sinT = _rope_tables()
    sinm_fm = sinT.T.copy()
    sinm_fm[0:64, :] *= -1.0
    cos_fm = cosT.T.copy()
    ident = np.eye(128, dtype=np.float32)
    r1 = np.zeros((NE, 128), dtype=np.float32)
    for j in range(96):
        r1[j // 8, 32 + j] = 2.0
    r2 = np.zeros((NE, 128), dtype=np.float32)
    for j in range(128):
        r2[12 + j // 8, j] = 2.0

    la = np.ascontiguousarray(L0_ffn_lora_A, dtype=np.float32).reshape(NE * RANK, D)
    lb2 = np.ascontiguousarray(
        np.asarray(L0_ffn_lora_B, dtype=np.float32).transpose(0, 2, 1).reshape(NE * RANK, D))
    lataug_h = np.zeros((D, 256), dtype=np.float32)
    lataug_h[:, 0:NE] = np.asarray(L0_ffn_gate_W, np.float32).T
    lataug_h[:, 32:256] = la.T
    bwt_h = np.ascontiguousarray(np.asarray(L0_ffn_base_W, np.float32).T)

    bf = ml_dtypes.bfloat16
    common = dict(
        wq_in=np.ascontiguousarray(L0_attn_Wq, np.float32),
        wk_in=np.ascontiguousarray(L0_attn_Wk, np.float32),
        wv_in=np.ascontiguousarray(L0_attn_Wv, np.float32),
        wo_in=np.ascontiguousarray(L0_attn_Wo, np.float32),
        bwt_in=bwt_h, lataug_in=np.ascontiguousarray(lataug_h), lb2_in=lb2,
        cosf=cos_fm.astype(bf), sinf=sinm_fm.astype(bf),
        ident=ident, r1c=r1.astype(bf), r2c=r2.astype(bf),
    )
    in_maps = []
    own = [_own_rows(r) for r in (0, 1)]
    for core in range(8):
        s, r = core // 2, core % 2
        rows = own[r]
        sin_own = sinT[rows].copy()
        sin_own[:, 0:64] *= -1.0
        m = dict(common)
        m["x_in"] = x[s]
        m["xo_in"] = np.ascontiguousarray(x[s][rows])
        m["cost"] = cosT[rows].astype(bf)
        m["sint"] = sin_own.astype(bf)
        m["masks"] = _masks(r)
        in_maps.append(m)
    return in_maps, own


def kernel(**inputs):
    in_maps, own = prepare_in_maps(**inputs)
    if "nc" not in _BUILD_CACHE:
        _BUILD_CACHE["nc"] = build_program()
    nc = _BUILD_CACHE["nc"]
    res = run_bass_kernel_spmd(nc, in_maps, core_ids=list(range(8)))
    out = np.empty((B, T, D), dtype=np.float32)
    for core in range(8):
        s, r = core // 2, core % 2
        out[s][own[r]] = res.results[core]["out"]
    return out


# version 19
# speedup vs baseline: 1.0777x; 1.0304x over previous
import numpy as np
import ml_dtypes
import concourse.bass as bass
import concourse.mybir as mybir
from concourse.bass_utils import run_bass_kernel_spmd

F32 = mybir.dt.float32
BF16 = mybir.dt.bfloat16
AF = mybir.ActivationFunctionType
ALU = mybir.AluOpType
AX = mybir.AxisListType

B, T, D = 4, 4096, 4096
NQH, NKVH, DH = 32, 8, 128
GRP = NQH // NKVH
NT = 4
NE, RANK = 28, 8
EPS = 1e-5
ISQ = float(1.0 / np.sqrt(128.0))


class Buf:
    __slots__ = ("name", "last_w", "readers", "sem_name", "dma_count", "sink")

    def __init__(self, name, sink=False):
        self.name = name
        self.last_w = None
        self.readers = []
        self.sem_name = None
        self.dma_count = 0
        self.sink = sink


class Em:
    def __init__(self):
        self.ops = []

    def buf(self, name, sink=False):
        return Buf(name, sink)

    def op(self, engine, fn, reads=(), writes=(), tag=""):
        assert engine in ("pe", "act", "dve", "pool"), engine
        self._rec(engine, fn, reads, writes, tag, False)

    def dma(self, queue, fn, reads=(), writes=(), tag=""):
        assert queue in ("sp", "act", "pool"), queue
        self._rec(queue, fn, reads, writes, tag, True)

    def barrier(self, engine, bufs, tag="bar"):
        self._rec(engine, None, bufs, (), tag, False)

    def _rec(self, engine, fn, reads, writes, tag, dma):
        idx = len(self.ops)
        deps = set()
        for b in reads:
            if b.sink:
                continue
            if b.last_w is not None:
                deps.add(b.last_w)
        for b in writes:
            if b.sink:
                continue
            deps.update(b.readers)
            if b.last_w is not None:
                deps.add(b.last_w)
        for b in reads:
            if not b.sink:
                b.readers.append(idx)
        for b in writes:
            if not b.sink:
                b.last_w = idx
                b.readers = []
        self.ops.append(dict(engine=engine, fn=fn, deps=sorted(deps), dma=dma,
                             tag=tag, idx=idx, marked=False,
                             writes=[b for b in writes if not b.sink]))

    def finalize(self, nc):
        ops = self.ops
        for o in ops:
            for j in o["deps"]:
                ops[j]["marked"] = True
        eng_sem = {e: nc.alloc_semaphore(f"es_{e}") for e in ("pe", "act", "dve", "pool", "sp")}
        self.eng_sem = eng_sem
        self.buf_sems = {}
        eng_cnt = {e: 0 for e in eng_sem}
        nsem = 0
        for o in ops:
            if o["dma"]:
                wb = o["writes"]
                assert wb, f"dma without tracked write: {o['tag']}"
                tgt = wb[0]
                if tgt.sem_name is None:
                    tgt.sem_name = f"ds{nsem}"
                    nsem += 1
                    self.buf_sems[tgt.sem_name] = nc.alloc_semaphore(tgt.sem_name)
                tgt.dma_count += 16
                o["sem_key"] = tgt.sem_name
                o["sem_val"] = tgt.dma_count
                for b in wb[1:]:
                    b.sem_name = tgt.sem_name
                    b.dma_count = tgt.dma_count
            else:
                if o["marked"]:
                    eng_cnt[o["engine"]] += 1
                    o["sem_val"] = eng_cnt[o["engine"]]
                else:
                    o["sem_val"] = None
        for o in ops:
            waits = []
            for j in o["deps"]:
                p = ops[j]
                if p["dma"]:
                    waits.append(("d", p["sem_key"], p["sem_val"]))
                else:
                    if p["engine"] == "pe" and o["engine"] == "pe" and not o["dma"]:
                        continue
                    assert p["sem_val"] is not None
                    waits.append(("e", p["engine"], p["sem_val"]))
            o["waits"] = waits

    def emit(self, eng_obj, engine):
        water = {}
        for o in self.ops:
            if o["engine"] != engine:
                continue
            for (kind, key, val) in o["waits"]:
                sem = self.buf_sems[key] if kind == "d" else self.eng_sem[key]
                wk = (kind, key)
                if water.get(wk, 0) >= val:
                    continue
                water[wk] = val
                eng_obj.wait_ge(sem, val)
            if o["fn"] is None:
                continue
            inst = o["fn"](eng_obj)
            if o["dma"]:
                inst.then_inc(self.buf_sems[o["sem_key"]], 16)
            elif o["sem_val"] is not None:
                inst.then_inc(self.eng_sem[engine], 1)


def build_program(taps=(), dbg=False):
    nc = bass.Bass("TRN2", target_bir_lowering=False)
    em = Em()

    def inp(name, shape, dtype=F32):
        return nc.dram_tensor(name, shape, dtype, kind="ExternalInput").ap(), em.buf(name)

    x_in, x_in_b = inp("x_in", [T, D])
    xo_in, xo_in_b = inp("xo_in", [2048, D])
    wq_in, wq_in_b = inp("wq_in", [D, D])
    wk_in, wk_in_b = inp("wk_in", [D, 1024])
    wv_in, wv_in_b = inp("wv_in", [D, 1024])
    wo_in, wo_in_b = inp("wo_in", [D, D])
    bwt_in, bwt_in_b = inp("bwt_in", [D, D])
    lataug_in, lataug_in_b = inp("lataug_in", [D, 256])
    lb2_in, lb2_in_b = inp("lb2_in", [NE * RANK, D])
    cosf_in, cosf_b = inp("cosf", [128, T], BF16)
    sinf_in, sinf_b = inp("sinf", [128, T], BF16)
    cost_in, cost_b = inp("cost", [2048, 128], BF16)
    sint_in, sint_b = inp("sint", [2048, 128], BF16)
    masks_in, masks_b = inp("masks", [4, 128, 1024], BF16)
    ident_in, ident_b = inp("ident", [128, 128], F32)
    r1_in, r1_b = inp("r1c", [NE, 128], BF16)
    r2_in, r2_b = inp("r2c", [NE, 128], BF16)

    out_ap = nc.dram_tensor("out", [2048, D], F32, kind="ExternalOutput").ap()
    out_b = em.buf("out")
    if dbg:
        dbg_biasc = nc.dram_tensor("dbg_biasc", [128, 1], F32, kind="ExternalOutput").ap()
        dbg_pt = nc.dram_tensor("dbg_pt", [128, 1024], BF16, kind="ExternalOutput").ap()
        dbg_l = nc.dram_tensor("dbg_l", [1, 512], F32, kind="ExternalOutput").ap()
        dbg_ctx = nc.dram_tensor("dbg_ctx", [128, 512], BF16, kind="ExternalOutput").ap()
        dbg_bufs = [em.buf(f"dbg{j}") for j in range(4)]

    def scr(name, shape, dtype):
        return nc.dram_tensor(name, shape, dtype, kind="Internal").ap(), em.buf(name)

    wq_bf, wq_bf_b = scr("wq_bf", [D, D], BF16)
    wk_bf, wk_bf_b = scr("wk_bf", [D, 1024], BF16)
    wv_bf, wv_bf_b = scr("wv_bf", [D, 1024], BF16)
    wo_bf, wo_bf_b = scr("wo_bf", [D, D], BF16)
    bwT_bf, bwT_bf_b = scr("bwT_bf", [D, D], BF16)
    lataug, lataug_b = scr("lataug", [D, 256], BF16)
    lbsc, lbsc_b = scr("lbsc", [NE * RANK, D], BF16)
    lnscr, _ = scr("lnscr", [T, D], BF16)
    lnscr_tb = [em.buf(f"lnscr{t}") for t in range(8)]
    lnoscr, _ = scr("lnoscr", [2048, D], BF16)
    lnoscr_tb = [em.buf(f"lnoscr{i}") for i in range(NT)]
    kscr, kscr_b = scr("kscr", [NKVH, 128, T], BF16)
    vscr, vscr_b = scr("vscr", [NKVH, 128, T], BF16)
    qscr, qscr_b = scr("qscr", [2048, D], BF16)
    ln4scr, _ = scr("ln4scr", [512, D], BF16)
    ln4scr_b = em.buf("ln4scr")
    x1scr, _ = scr("x1scr", [512, D], F32)
    x1scr_b = em.buf("x1scr")

    def sbuf(name, shape, dtype):
        return nc.alloc_sbuf_tensor(name, shape, dtype), em.buf(name)

    cosP = [sbuf("cosP0", [128, 512], BF16), sbuf("cosP1", [128, 512], BF16)]
    sinP = [sbuf("sinP0", [128, 512], BF16), sbuf("sinP1", [128, 512], BF16)]
    cos_tm, cos_tm_b = sbuf("cos_tm", [128, 4 * 128], BF16)
    sin_tm, sin_tm_b = sbuf("sin_tm", [128, 4 * 128], BF16)
    masks_sb, masks_sb_b = sbuf("masks_sb", [128, 4 * 1024], BF16)
    ident_sb, ident_sb_b = sbuf("ident_sb", [128, 128], F32)
    ones_bf, ones_bf_b = sbuf("ones_bf", [128, 1], BF16)
    onesr_f, onesr_f_b = sbuf("onesr_f", [1, 128], F32)
    cpool, cpool_b = sbuf("cpool", [128, 2], F32)
    r1_sb, r1_sb_b = sbuf("r1_sb", [NE, 128], BF16)
    r2_sb, r2_sb_b = sbuf("r2_sb", [NE, 128], BF16)
    knorm, knorm_b = sbuf("knorm", [1, 64], F32)
    qnorm, qnorm_b = sbuf("qnorm", [1, NT], F32)
    biasc, biasc_b = sbuf("biasc", [128, 1], F32)
    tiny, tiny_b = sbuf("tiny", [1, 16], F32)
    slab_a, slab_a_b = sbuf("slab_a", [128, D], F32)
    slab_b, slab_b_b = sbuf("slab_b", [128, D], F32)
    tsl = []
    for i in range(4):
        tsl.append(sbuf(f"tsl{i}", [128, D], BF16))
    lnout, lnout_b = sbuf("lnout", [128, D], BF16)
    fm0, fm0_b = sbuf("fm0", [128, 32 * 512], BF16)
    wrow0, wrow0_b = sbuf("wrow0", [128, 1024], BF16)
    wrow1, wrow1_b = sbuf("wrow1", [128, 1024], BF16)
    wrow2, wrow2_b = sbuf("wrow2", [128, 1024], BF16)
    wrow3, wrow3_b = sbuf("wrow3", [128, 1024], BF16)
    wrow4, wrow4_b = sbuf("wrow4", [128, 1024], BF16)
    wrow5, wrow5_b = sbuf("wrow5", [128, 1024], BF16)
    WR4 = [(wrow0, wrow0_b), (wrow1, wrow1_b), (wrow2, wrow2_b), (wrow3, wrow3_b),
           (wrow4, wrow4_b), (wrow5, wrow5_b)]
    augr0, augr0_b = sbuf("augr0", [128, 256], BF16)
    augr1, augr1_b = sbuf("augr1", [128, 256], BF16)
    lbr1, lbr1_b = sbuf("lbr1", [128, 1024], BF16)
    lbr2, lbr2_b = sbuf("lbr2", [128, 1024], BF16)
    kh = [sbuf("kh0", [128, T], BF16), sbuf("kh1", [128, T], BF16)]
    vh = [sbuf("vh0", [128, T], BF16), sbuf("vh1", [128, T], BF16)]
    qt = [sbuf("qt0", [128, 512], BF16), sbuf("qt1", [128, 512], BF16)]
    pt = [sbuf("pt0", [128, 1024], BF16), sbuf("pt1", [128, 1024], BF16),
          sbuf("pt2", [128, 1024], BF16)]
    vstage, vstage_b = sbuf("vstage", [128, 2 * 512], BF16)
    stage0, stage0_b = sbuf("stage0", [128, 1024], BF16)
    stage1, stage1_b = sbuf("stage1", [128, 1024], BF16)
    krotB = [sbuf("krot0", [128, 512], BF16), sbuf("krot1", [128, 512], BF16)]
    rt1B = [sbuf("rt1a", [128, 512], BF16), sbuf("rt1b", [128, 512], BF16)]
    rt2B = [sbuf("rt2a", [128, 512], BF16), sbuf("rt2b", [128, 512], BF16)]
    kstB = [sbuf("kst0", [128, 512], BF16), sbuf("kst1", [128, 512], BF16)]
    krot, krot_b = krotB[0]
    rt1, rt1_b = rt1B[0]
    rt2, rt2_b = rt2B[0]
    psum2 = [sbuf("psum2a", [128, 512], BF16), sbuf("psum2b", [128, 512], BF16)]
    l_sb, l_sb_b = sbuf("l_sb", [1, 512], F32)
    linv, linv_b = sbuf("linv", [1, 512], F32)
    linvb, linvb_b = sbuf("linvb", [128, 512], BF16)
    sm0, sm0_b = sbuf("sm0", [128, 80], F32)
    gsm, gsm_b = sbuf("gsm", [128, 40], F32)
    gtmp, gtmp_b = sbuf("gtmp", [128, NE], F32)
    gtmp2, gtmp2_b = sbuf("gtmp2", [128, NE], F32)
    gate_tm, gate_tm_b = sbuf("gate_tm", [128, NE], F32)
    lgt, lgt_b = sbuf("lgt", [NE, 512], F32)
    g28, g28_b = sbuf("g28", [NE, 512], BF16)
    gf1, gf1_b = sbuf("gf1", [128, 512], BF16)
    gf2, gf2_b = sbuf("gf2", [128, 512], BF16)
    h1p, h1p_b = sbuf("h1p", [128, 512], BF16)
    h2p, h2p_b = sbuf("h2p", [128, 512], BF16)
    qacc, qacc_b = sbuf("qacc", [128, 32], F32)

    pA = nc.alloc_psum_tensor("pA", [128, 1024], F32)
    pB = nc.alloc_psum_tensor("pB", [128, 1024], F32)
    pC = nc.alloc_psum_tensor("pC", [128, 1024], F32)
    pD = nc.alloc_psum_tensor("pD", [128, 1024], F32)
    pbufs = {}
    for nm in ("A0", "A1", "B0", "B1", "C0", "C1", "D0", "D1"):
        pbufs[nm] = em.buf("p" + nm)
    PS = [(pA, 0, pbufs["A0"]), (pA, 512, pbufs["A1"]),
          (pB, 0, pbufs["B0"]), (pB, 512, pbufs["B1"]),
          (pC, 0, pbufs["C0"]), (pC, 512, pbufs["C1"]),
          (pD, 0, pbufs["D0"]), (pD, 512, pbufs["D1"])]

    em.dma("sp", lambda e: e.dma_start(masks_sb[:].rearrange("p (m f) -> p m f", m=4), masks_in.rearrange("m p f -> p m f")),
           [masks_b], [masks_sb_b], "c4")
    em.dma("sp", lambda e: e.dma_start(ident_sb[:], ident_in), [ident_b], [ident_sb_b], "c5")
    em.dma("sp", lambda e: e.dma_start(r1_sb[:], r1_in), [r1_b], [r1_sb_b], "c6")
    em.dma("sp", lambda e: e.dma_start(r2_sb[:], r2_in), [r2_b], [r2_sb_b], "c7")
    em.op("dve", lambda e: e.memset(ones_bf[:], 1.0), [], [ones_bf_b], "ones")
    em.op("dve", lambda e: e.memset(onesr_f[:], 1.0), [], [onesr_f_b], "onesr")
    em.op("dve", lambda e: e.memset(cpool[:, 0:1], EPS), [], [cpool_b], "eps")
    em.op("dve", lambda e: e.memset(qacc[:], 0.0), [], [qacc_b], "qacc0")
    em.op("dve", lambda e: e.memset(lbr1[0:32, :], 0.0), [], [lbr1_b], "lbr1z")

    def cast_w(src_ap, src_b, dst_ap, dst_b, rows, tag):
        for r0 in range(0, rows, 512):
            n = min(512, rows - r0)
            em.dma("pool", (lambda r0=r0, n=n: lambda e: e.dma_start(
                dst_ap[r0:r0 + n, :], src_ap[r0:r0 + n, :]))(),
                [src_b], [dst_b], f"cw_{tag}{r0}")

    wk_cb = [em.buf(f"wkc{j}") for j in range(8)]
    wv_cb = [em.buf(f"wvc{j}") for j in range(8)]
    for j in range(8):
        em.dma("pool", (lambda j=j: lambda e: e.dma_start(
            wk_bf[j * 512:(j + 1) * 512, :], wk_in[j * 512:(j + 1) * 512, :]))(),
            [wk_in_b], [wk_cb[j]], f"cw_k{j}")
        em.dma("pool", (lambda j=j: lambda e: e.dma_start(
            wv_bf[j * 512:(j + 1) * 512, :], wv_in[j * 512:(j + 1) * 512, :]))(),
            [wv_in_b], [wv_cb[j]], f"cw_v{j}")

    def p0_wq():
        cast_w(wq_in, wq_in_b, wq_bf, wq_bf_b, D, "q")

    def p0_wo():
        cast_w(wo_in, wo_in_b, wo_bf, wo_bf_b, D, "o")

    def p0_bwT():
        cast_w(bwt_in, bwt_in_b, bwT_bf, bwT_bf_b, D, "bt")

    def p0_ffn_small():
        cast_w(lataug_in, lataug_in_b, lataug, lataug_b, D, "lat")
        cast_w(lb2_in, lb2_in_b, lbsc, lbsc_b, NE * RANK, "lb2")

    P0_DEFER = {0: p0_wq, 1: p0_wo, 2: p0_bwT, 3: p0_ffn_small,
                4: lambda: p2a_ln(0), 5: lambda: p2a_ln(1),
                6: lambda: p2a_ln(2), 7: lambda: p2a_ln(3)}

    def ln_tile(x_ap, x_b, out_ap, out_bb, tag, dump_ap=None, dump_b=None, col=64):
        if dump_ap is None:
            dump_ap, dump_b = lnout[:], lnout_b
        sq, s1 = sm0[:, col:col + 1], sm0[:, col + 1:col + 2]
        mm_, m2 = sm0[:, col + 2:col + 3], sm0[:, col + 3:col + 4]
        vv, rr, nm = sm0[:, col + 4:col + 5], sm0[:, col + 5:col + 6], sm0[:, col + 6:col + 7]
        em.op("act", lambda e: e.activation(dump_ap, x_ap, AF.Square, accum_out=sq),
              [x_b], [sm0_b, dump_b], f"{tag}sq")
        em.op("dve", lambda e: e.reduce_sum(s1, x_ap, axis=AX.X), [x_b], [sm0_b], f"{tag}s1")
        em.op("dve", lambda e: e.tensor_scalar(mm_, s1, 1.0 / D, None, ALU.mult),
              [sm0_b], [sm0_b], f"{tag}m")
        em.op("dve", lambda e: e.tensor_tensor(m2, mm_, mm_, ALU.mult), [sm0_b], [sm0_b], f"{tag}m2")
        em.op("dve", lambda e: e.scalar_tensor_tensor(vv, sq, 1.0 / D, m2, ALU.mult, ALU.subtract),
              [sm0_b], [sm0_b], f"{tag}v")
        em.op("act", lambda e: e.activation(vv, vv, AF.Ln, bias=cpool[:, 0:1]),
              [sm0_b, cpool_b], [sm0_b], f"{tag}ln")
        em.op("act", lambda e: e.activation(rr, vv, AF.Exp, scale=-0.5), [sm0_b], [sm0_b], f"{tag}ex")
        em.op("dve", lambda e: e.scalar_tensor_tensor(nm, mm_, -1.0, rr, ALU.mult, ALU.mult),
              [sm0_b], [sm0_b], f"{tag}nm")
        em.op("act", lambda e: e.activation(out_ap, x_ap, AF.Identity, bias=nm, scale=rr),
              [x_b, sm0_b], [out_bb], f"{tag}ap")

    def ln_finish(scol, sqcol, tag):
        s1, sq = gsm[:, 8:9], gsm[:, 9:10]
        mm_, m2 = gsm[:, 10:11], gsm[:, 11:12]
        vv, rr, nm = gsm[:, 12:13], gsm[:, 13:14], gsm[:, 14:15]
        em.op("dve", lambda e: e.reduce_sum(s1, sm0[:, scol:scol + 8], axis=AX.X),
              [sm0_b], [gsm_b], f"{tag}rs")
        em.op("dve", lambda e: e.reduce_sum(sq, sm0[:, sqcol:sqcol + 8], axis=AX.X),
              [sm0_b], [gsm_b], f"{tag}rq")
        em.op("dve", lambda e: e.tensor_scalar(mm_, s1, 1.0 / D, None, ALU.mult),
              [gsm_b], [gsm_b], f"{tag}m")
        em.op("dve", lambda e: e.tensor_tensor(m2, mm_, mm_, ALU.mult), [gsm_b], [gsm_b], f"{tag}m2")
        em.op("dve", lambda e: e.scalar_tensor_tensor(vv, sq, 1.0 / D, m2, ALU.mult, ALU.subtract),
              [gsm_b], [gsm_b], f"{tag}v")
        em.op("act", lambda e: e.activation(vv, vv, AF.Ln, bias=cpool[:, 0:1]),
              [gsm_b, cpool_b], [gsm_b], f"{tag}ln")
        em.op("act", lambda e: e.activation(rr, vv, AF.Exp, scale=-0.5), [gsm_b], [gsm_b], f"{tag}ex")
        em.op("dve", lambda e: e.scalar_tensor_tensor(nm, mm_, -1.0, rr, ALU.mult, ALU.mult),
              [gsm_b], [gsm_b], f"{tag}nm")
        return rr, nm

    aslab, aslab_b = tsl[2]
    alnout, alnout_b = tsl[3]

    def p2a_ln(i):
        for ts in range(4):
            r0 = i * 512 + ts * 128
            em.dma("pool", (lambda r0=r0: lambda e: e.dma_start(aslab[:], xo_in[r0:r0 + 128, :]))(),
                   [xo_in_b], [aslab_b], f"a_x{i}_{ts}")
            ln_tile(aslab[:], aslab_b, alnout[:], alnout_b, f"aL{i}_{ts}",
                    dump_ap=alnout[:], dump_b=alnout_b, col=72)
            em.dma("act", (lambda r0=r0: lambda e: e.dma_start(lnoscr[r0:r0 + 128, :], alnout[:]))(),
                   [alnout_b], [lnoscr_tb[i]], f"a_s{i}_{ts}")


    GB = [kh[0], kh[1], vh[0], vh[1]]

    def fmsl(group, dc, c0, c1):
        if group == 0:
            return fm0[:, dc * 512 + c0: dc * 512 + c1], fm0_b
        t, tb = GB[dc // 8]
        return t[:, (dc % 8) * 512 + c0: (dc % 8) * 512 + c1], tb

    def p1_ln(t8):
        t0 = t8 * 512
        for sub in range(4):
            sl, slb = (slab_a, slab_a_b) if sub % 2 == 0 else (slab_b, slab_b_b)
            r0 = t0 + sub * 128
            em.dma("pool", (lambda r0=r0, sl=sl: lambda e: e.dma_start(sl[:], x_in[r0:r0 + 128, :]))(),
                   [x_in_b], [slb], f"p1x{t8}_{sub}")
            ln_tile(sl[:], slb, lnout[:], lnout_b, f"p1L{t8}_{sub}", col=sub * 7)
            em.dma("act", (lambda r0=r0, t8=t8: lambda e: e.dma_start(lnscr[r0:r0 + 128, :], lnout[:]))(),
                   [lnout_b], [lnscr_tb[t8]], f"p1s{t8}_{sub}")

    def p1_fm(t8):
        t0 = t8 * 512
        for dc in range(32):
            dst, dstb = fmsl(t8 % 2, dc, 0, 512)
            em.dma("act", (lambda dc=dc, t0=t0, dst=dst: lambda e: e.dma_start(
                dst, lnscr[t0:t0 + 512, dc * 128:(dc + 1) * 128], transpose=True))(),
                [lnscr_tb[t8]], [dstb], f"p1f{t8}_{dc}")

    p1_ln(0)
    p1_fm(0)
    for t8 in range(8):
        t0 = t8 * 512
        grp_ = t8 % 2
        for ic in range(32):
            wr, wrb = WR4[ic % 3]
            em.dma("sp", (lambda ic=ic, wr=wr: lambda e: e.dma_start(
                wr[:], wk_bf[ic * 128:(ic + 1) * 128, :]))(),
                [wk_cb[ic // 4]], [wrb], f"p1wk{t8}_{ic}")
            src, srcb = fmsl(grp_, ic, 0, 512)
            for h in range(8):
                bank, off, bb = PS[h]
                em.op("pe", (lambda ic=ic, h=h, wr=wr, bank=bank, off=off, src=src: lambda e: e.matmul(
                    bank[:, off:off + 512], wr[:, h * 128:(h + 1) * 128],
                    src, start=(ic == 0), stop=(ic == 31)))(),
                    [wrb, srcb], [bb], f"p1k{t8}_{ic}_{h}")
        cp, cpb = cosP[t8 % 2]
        sp_, spb = sinP[t8 % 2]
        em.dma("sp", (lambda t0=t0, cp=cp: lambda e: e.dma_start(cp[:], cosf_in[:, t0:t0 + 512]))(),
               [cosf_b], [cpb], f"p1cp{t8}")
        em.dma("sp", (lambda t0=t0, sp_=sp_: lambda e: e.dma_start(sp_[:], sinf_in[:, t0:t0 + 512]))(),
               [sinf_b], [spb], f"p1sp{t8}")
        for h in range(8):
            bank, off, bb = PS[h]
            stg = stage1[:, (h % 2) * 512:(h % 2) * 512 + 512]
            kro, krob = krotB[h % 2]
            r1_, r1b = rt1B[h % 2]
            r2_, r2b = rt2B[h % 2]
            kst, kstb = kstB[h % 2]
            em.op("act", (lambda bank=bank, off=off, stg=stg: lambda e: e.copy(stg, bank[:, off:off + 512]))(),
                  [bb], [stage1_b], f"p1ke{t8}_{h}")
            em.op("dve", (lambda stg=stg, r1_=r1_, cp=cp: lambda e: e.tensor_tensor(
                r1_[:], stg, cp[:], ALU.mult))(),
                [stage1_b, cpb], [r1b], f"p1r1{t8}_{h}")
            em.op("act", (lambda h=h, kro=kro: lambda e: e.copy(
                kro[0:64, :], stage1[64:128, (h % 2) * 512:(h % 2) * 512 + 512]))(),
                [stage1_b], [krob], f"p1ra{t8}_{h}")
            em.op("act", (lambda h=h, kro=kro: lambda e: e.copy(
                kro[64:128, :], stage1[0:64, (h % 2) * 512:(h % 2) * 512 + 512]))(),
                [stage1_b], [krob], f"p1rb{t8}_{h}")
            em.op("dve", (lambda kro=kro, r2_=r2_, sp_=sp_: lambda e: e.tensor_tensor(
                r2_[:], kro[:], sp_[:], ALU.mult))(),
                [krob, spb], [r2b], f"p1rc{t8}_{h}")
            em.op("dve", (lambda kst=kst, r1_=r1_, r2_=r2_: lambda e: e.tensor_tensor(
                kst[:], r1_[:], r2_[:], ALU.add))(),
                [r1b, r2b], [kstb], f"p1ro{t8}_{h}")
            em.dma("pool", (lambda h=h, t0=t0, kst=kst: lambda e: e.dma_start(
                kscr[h, :, t0:t0 + 512], kst[:]))(),
                [kstb], [kscr_b], f"p1kw{t8}_{h}")
            em.op("dve", (lambda kst=kst, h=h: lambda e: e.tensor_tensor(
                stage0[:, (h % 2) * 512:(h % 2) * 512 + 512], kst[:], kst[:], ALU.mult))(),
                [kstb], [stage0_b], f"p1n2{t8}_{h}")
            em.op("pe", (lambda bank=bank, off=off, h=h: lambda e: e.matmul(
                bank[0:1, off:off + 512], ones_bf[:],
                stage0[:, (h % 2) * 512:(h % 2) * 512 + 512], start=True, stop=True))(),
                [stage0_b, ones_bf_b], [bb], f"p1nm{t8}_{h}")
            em.op("dve", (lambda bank=bank, off=off, t8=t8, h=h: lambda e: e.reduce_max(
                knorm[0:1, t8 * 8 + h:t8 * 8 + h + 1], bank[0:1, off:off + 512], axis=AX.X))(),
                [bb], [knorm_b], f"p1nx{t8}_{h}")
        if t8 + 1 < 8:
            p1_ln(t8 + 1)
            p1_fm(t8 + 1)
        for ic in range(32):
            wr, wrb = WR4[3 + ic % 3]
            em.dma("sp", (lambda ic=ic, wr=wr: lambda e: e.dma_start(
                wr[:], wv_bf[ic * 128:(ic + 1) * 128, :]))(),
                [wv_cb[ic // 4]], [wrb], f"p1wv{t8}_{ic}")
            for ts in range(4):
                src, srcb = fmsl(grp_, ic, ts * 128, (ts + 1) * 128)
                for hf in range(2):
                    bank, off, bb = PS[ts * 2 + hf]
                    em.op("pe", (lambda ic=ic, ts=ts, hf=hf, wr=wr, bank=bank, off=off, src=src:
                                 lambda e: e.matmul(
                        bank[:, off:off + 512], src,
                        wr[:, hf * 512:(hf + 1) * 512], start=(ic == 0), stop=(ic == 31)))(),
                        [wrb, srcb], [bb], f"p1v{t8}_{ic}_{ts}_{hf}")
        for hw in range(4):
            hf, hp = hw // 2, (hw % 2) * 2
            for ts in range(4):
                bank, off, bb = PS[ts * 2 + hf]
                for hh in (hp, hp + 1):
                    em.op("act", (lambda bank=bank, off=off, hh=hh, hp=hp, ts=ts: lambda e: e.copy(
                        vstage[:, (hh - hp) * 512 + ts * 128: (hh - hp) * 512 + (ts + 1) * 128],
                        bank[:, off + hh * 128: off + (hh + 1) * 128]))(),
                        [bb], [vstage_b], f"p1ve{t8}_{ts}_{hf*4+hh}")
            for hh in (hp, hp + 1):
                h = hf * 4 + hh
                em.dma("act", (lambda h=h, hh=hh, hp=hp, t0=t0: lambda e: e.dma_start(
                    vscr[h, :, t0:t0 + 512], vstage[:, (hh - hp) * 512:(hh - hp + 1) * 512]))(),
                    [vstage_b], [vscr_b], f"p1vw{t8}_{h}")
        if t8 in P0_DEFER:
            P0_DEFER[t8]()

    def p2a_main(i):
        for dc in range(32):
            em.dma("act", (lambda dc=dc, i=i: lambda e: e.dma_start(
                fm0[:, dc * 512:(dc + 1) * 512],
                lnoscr[i * 512:(i + 1) * 512, dc * 128:(dc + 1) * 128], transpose=True))(),
                [lnoscr_tb[i]], [fm0_b], f"a_f{i}_{dc}")
        em.op("dve", lambda e: e.memset(qacc[:], 0.0), [], [qacc_b], f"a_q0{i}")
        em.dma("sp", (lambda i=i: lambda e: e.dma_start(
            cos_tm[:].rearrange("p (c d) -> p c d", c=4),
            cost_in[i * 512:(i + 1) * 512, :].rearrange("(c p) d -> p c d", p=128)))(),
            [cost_b], [cos_tm_b], f"a_ct{i}")
        em.dma("sp", (lambda i=i: lambda e: e.dma_start(
            sin_tm[:].rearrange("p (c d) -> p c d", c=4),
            sint_in[i * 512:(i + 1) * 512, :].rearrange("(c p) d -> p c d", p=128)))(),
            [sint_b], [sin_tm_b], f"a_st{i}")
        for qtr in range(4):
            for ic in range(32):
                wr, wrb = WR4[ic % 6]
                em.dma("sp", (lambda ic=ic, qtr=qtr, wr=wr: lambda e: e.dma_start(
                    wr[:], wq_bf[ic * 128:(ic + 1) * 128, qtr * 1024:(qtr + 1) * 1024]))(),
                    [wq_bf_b], [wrb], f"a_w{i}_{qtr}_{ic}")
                for ts in range(4):
                    for hf in range(2):
                        bank, off, bb = PS[ts * 2 + hf]
                        em.op("pe", (lambda ic=ic, ts=ts, hf=hf, wr=wr, bank=bank, off=off:
                                     lambda e: e.matmul(
                            bank[:, off:off + 512],
                            fm0[:, ic * 512 + ts * 128: ic * 512 + (ts + 1) * 128],
                            wr[:, hf * 512:(hf + 1) * 512], start=(ic == 0), stop=(ic == 31)))(),
                            [wrb, fm0_b], [bb], f"a_m{i}_{qtr}_{ic}_{ts}_{hf}")
            for ts in range(4):
                sub = ts
                for hf in range(2):
                    bank, off, bb = PS[ts * 2 + hf]
                    em.op("act", (lambda bank=bank, off=off, hf=hf: lambda e: e.copy(
                        stage0[:, hf * 512:(hf + 1) * 512], bank[:, off:off + 512]))(),
                        [bb], [stage0_b], f"a_e{i}_{qtr}_{ts}_{hf}")
                cs = cos_tm[:, sub * 128:(sub + 1) * 128]
                sn = sin_tm[:, sub * 128:(sub + 1) * 128]
                for hb in range(8):
                    c0 = hb * 128
                    em.op("dve", (lambda c0=c0, cs=cs: lambda e: e.tensor_tensor(
                        rt1[:, 0:128], stage0[:, c0:c0 + 128], cs, ALU.mult))(),
                        [stage0_b, cos_tm_b], [rt1_b], f"a_r1{i}_{qtr}_{ts}_{hb}")
                    em.op("dve", (lambda c0=c0, sn=sn: lambda e: e.tensor_tensor(
                        rt2[:, 0:64], stage0[:, c0 + 64:c0 + 128], sn[:, 0:64], ALU.mult))(),
                        [stage0_b, sin_tm_b], [rt2_b], f"a_r2{i}_{qtr}_{ts}_{hb}")
                    em.op("dve", (lambda c0=c0, sn=sn: lambda e: e.tensor_tensor(
                        rt2[:, 64:128], stage0[:, c0:c0 + 64], sn[:, 64:128], ALU.mult))(),
                        [stage0_b, sin_tm_b], [rt2_b], f"a_r3{i}_{qtr}_{ts}_{hb}")
                    em.op("dve", (lambda c0=c0: lambda e: e.tensor_tensor(
                        stage1[:, c0:c0 + 128], rt1[:, 0:128], rt2[:, 0:128], ALU.add))(),
                        [rt1_b, rt2_b], [stage1_b], f"a_ro{i}_{qtr}_{ts}_{hb}")
                em.dma("pool", (lambda i=i, ts=ts, qtr=qtr: lambda e: e.dma_start(
                    qscr[i * 512 + ts * 128:i * 512 + (ts + 1) * 128, qtr * 1024:(qtr + 1) * 1024],
                    stage1[:]))(),
                    [stage1_b], [qscr_b], f"a_qs{i}_{qtr}_{ts}")
                em.op("dve", lambda e: e.tensor_tensor(pt[0][0][:], stage1[:], stage1[:], ALU.mult),
                      [stage1_b], [pt[0][1]], f"a_n2{i}_{qtr}_{ts}")
                em.op("dve", lambda e: e.reduce_sum(gsm[:, 0:8],
                                                    pt[0][0][:].rearrange("p (s d) -> p s d", s=8),
                                                    axis=AX.X),
                      [pt[0][1]], [gsm_b], f"a_nr{i}_{qtr}_{ts}")
                em.op("dve", (lambda qtr=qtr: lambda e: e.tensor_tensor(
                    qacc[:, qtr * 8:(qtr + 1) * 8], qacc[:, qtr * 8:(qtr + 1) * 8],
                    gsm[:, 0:8], ALU.max))(),
                    [qacc_b, gsm_b], [qacc_b], f"a_nx{i}_{qtr}_{ts}")
        em.op("dve", lambda e: e.reduce_max(gsm[:, 16:17], qacc[:], axis=AX.X),
              [qacc_b], [gsm_b], f"a_fx{i}")
        em.op("pe", lambda e: e.transpose(pD[0:8, 0:128], gsm[:, 16:24], ident_sb[:]),
              [gsm_b, ident_sb_b], [pbufs["D0"]], f"a_tr{i}")
        em.op("dve", (lambda i=i: lambda e: e.reduce_max(qnorm[0:1, i:i + 1], pD[0:1, 0:128], axis=AX.X))(),
              [pbufs["D0"]], [qnorm_b], f"a_qn{i}")
        em.op("dve", (lambda i=i: lambda e: e.reduce_max(tiny[0:1, 0:1],
                                                         knorm[0:1, 0:(2 * i + 2) * 8], axis=AX.X))(),
              [knorm_b], [tiny_b], f"b_k{i}")
        em.op("dve", (lambda i=i: lambda e: e.tensor_tensor(tiny[0:1, 1:2], tiny[0:1, 0:1],
                                                            qnorm[0:1, i:i + 1], ALU.mult))(),
              [tiny_b, qnorm_b], [tiny_b], f"b_m{i}")
        em.op("act", lambda e: e.activation(tiny[0:1, 2:3], tiny[0:1, 1:2], AF.Ln),
              [tiny_b], [tiny_b], f"b_l{i}")
        em.op("act", lambda e: e.activation(tiny[0:1, 3:4], tiny[0:1, 2:3], AF.Exp, scale=0.5),
              [tiny_b], [tiny_b], f"b_e{i}")
        em.op("dve", lambda e: e.tensor_scalar(tiny[0:1, 4:5], tiny[0:1, 3:4], -ISQ, None, ALU.mult),
              [tiny_b], [tiny_b], f"b_s{i}")
        em.op("pe", lambda e: e.matmul(pD[:, 512:513], onesr_f[:], tiny[0:1, 4:5],
                                       start=True, stop=True),
              [tiny_b, onesr_f_b], [pbufs["D1"]], f"b_b{i}")
        em.op("act", lambda e: e.copy(biasc[:], pD[:, 512:513]), [pbufs["D1"]], [biasc_b], f"b_c{i}")
        if dbg and i == 0:
            em.dma("sp", lambda e: e.dma_start(dbg_biasc, biasc[:]), [biasc_b], [dbg_bufs[0]], "dbgb")

    def p2b(i, nch, pairs):
        for h in range(NKVH):
            khh, khb = kh[h % 2]
            vhh, vhb = vh[h % 2]
            em.dma("sp", (lambda h=h, khh=khh, nch=nch: lambda e: e.dma_start(
                khh[:, 0:nch * 128], kscr[h, :, 0:nch * 128]))(),
                [kscr_b], [khb], f"t_k{i}_{h}")
            em.dma("sp", (lambda h=h, vhh=vhh, nch=nch: lambda e: e.dma_start(
                vhh[:, 0:nch * 128], vscr[h, :, 0:nch * 128]))(),
                [vscr_b], [vhb], f"t_v{i}_{h}")
            for g in range(GRP):
                hq = h * GRP + g
                qtt, qtb = qt[g % 2]
                em.dma("act", (lambda i=i, hq=hq, qtt=qtt: lambda e: e.dma_start(
                    qtt[:], qscr[i * 512:(i + 1) * 512, hq * 128:(hq + 1) * 128], transpose=True))(),
                    [qscr_b], [qtb], f"t_q{i}_{hq}")
                def score_pair(pr, khh=khh, khb=khb, qtt=qtt, qtb=qtb):
                    c0, c1 = 2 * pr, 2 * pr + 1
                    sbank, sb0, sb1 = (pA, pbufs["A0"], pbufs["A1"]) if pr % 2 == 0 else \
                                      (pB, pbufs["B0"], pbufs["B1"])
                    ptt, ptb = pt[pr % 3]
                    em.op("pe", (lambda c0=c0, sbank=sbank, khh=khh, qtt=qtt: lambda e: e.matmul(
                        sbank[:, 0:512], khh[:, c0 * 128:(c0 + 1) * 128], qtt[:],
                        start=True, stop=True))(),
                        [khb, qtb], [sb0], f"t_s{i}_{hq}_{c0}")
                    em.op("pe", (lambda c1=c1, sbank=sbank, khh=khh, qtt=qtt: lambda e: e.matmul(
                        sbank[:, 512:1024], khh[:, c1 * 128:(c1 + 1) * 128], qtt[:],
                        start=True, stop=True))(),
                        [khb, qtb], [sb1], f"t_s{i}_{hq}_{c1}")
                    em.op("act", (lambda sbank=sbank, ptt=ptt: lambda e: e.activation(
                        ptt[:], sbank[:], AF.Exp, bias=biasc[:], scale=ISQ))(),
                        [sb0, sb1, biasc_b], [ptb], f"t_e{i}_{hq}_{pr}")
                    if pr >= 4 * i:
                        fr = pr - 4 * i
                        em.op("dve", (lambda fr=fr, ptt=ptt: lambda e: e.tensor_tensor(
                            ptt[:], ptt[:], masks_sb[:, fr * 1024:(fr + 1) * 1024], ALU.mult))(),
                            [ptb, masks_sb_b], [ptb], f"t_m{i}_{hq}_{pr}")

                def consume_pair(pr, vhh=vhh, vhb=vhb, pairs=pairs):
                    c0, c1 = 2 * pr, 2 * pr + 1
                    ptt, ptb = pt[pr % 3]
                    ps2, ps2b = psum2[pr % 2]
                    em.op("dve", (lambda ptt=ptt, ps2=ps2: lambda e: e.tensor_tensor(
                        ps2[:], ptt[:, 0:512], ptt[:, 512:1024], ALU.add))(),
                        [ptb], [ps2b], f"t_la{i}_{hq}_{pr}")
                    em.op("pe", (lambda pr=pr, ps2=ps2, pairs=pairs: lambda e: e.matmul(
                        pD[0:1, 0:512], ones_bf[:], ps2[:],
                        start=(pr == 0), stop=(pr == pairs - 1)))(),
                        [ps2b, ones_bf_b], [pbufs["D0"]], f"t_l0{i}_{hq}_{pr}")
                    em.op("pe", (lambda pr=pr, c0=c0, ptt=ptt, vhh=vhh: lambda e: e.matmul(
                        pC[:, 0:512], vhh[:, c0 * 128:(c0 + 1) * 128], ptt[:, 0:512],
                        start=(pr == 0), stop=False))(),
                        [vhb, ptb], [pbufs["C0"]], f"t_c0{i}_{hq}_{pr}")
                    em.op("pe", (lambda pr=pr, c1=c1, ptt=ptt, vhh=vhh, pairs=pairs: lambda e: e.matmul(
                        pC[:, 0:512], vhh[:, c1 * 128:(c1 + 1) * 128], ptt[:, 512:1024],
                        start=False, stop=(pr == pairs - 1)))(),
                        [vhb, ptb], [pbufs["C0"]], f"t_c1{i}_{hq}_{pr}")

                for pr in range(pairs):
                    score_pair(pr)
                    if pr > 1:
                        consume_pair(pr - 2)
                if pairs > 1:
                    consume_pair(pairs - 2)
                consume_pair(pairs - 1)
                em.op("dve", lambda e: e.tensor_copy(l_sb[:], pD[0:1, 0:512]),
                      [pbufs["D0"]], [l_sb_b], f"t_le{i}_{hq}")
                if dbg and i == 0 and hq == 0:
                    em.dma("sp", lambda e: e.dma_start(dbg_l, l_sb[:]), [l_sb_b], [dbg_bufs[2]], "dbgl")
                em.op("dve", lambda e: e.reciprocal(linv[:], l_sb[:]), [l_sb_b], [linv_b], f"t_li{i}_{hq}")
                em.op("pe", lambda e: e.matmul(pD[:, 512:1024], onesr_f[:], linv[:],
                                               start=True, stop=True),
                      [linv_b, onesr_f_b], [pbufs["D1"]], f"t_lb{i}_{hq}")
                em.op("act", lambda e: e.copy(linvb[:], pD[:, 512:1024]),
                      [pbufs["D1"]], [linvb_b], f"t_lv{i}_{hq}")
                em.op("dve", (lambda hq=hq: lambda e: e.tensor_tensor(
                    fm0[:, hq * 512:(hq + 1) * 512], pC[:, 0:512], linvb[:], ALU.mult))(),
                    [pbufs["C0"], linvb_b], [fm0_b], f"t_cn{i}_{hq}")
                if dbg and i == 0 and hq == 0:
                    em.dma("sp", lambda e: e.dma_start(dbg_ctx, fm0[:, 0:512]), [fm0_b], [dbg_bufs[3]], "dbgc")

    def p2c_wo(i):
        for qtr in range(4):
            for dc in range(32):
                wr, wrb = WR4[dc % 6]
                em.dma("sp", (lambda dc=dc, qtr=qtr, wr=wr: lambda e: e.dma_start(
                    wr[:], wo_bf[dc * 128:(dc + 1) * 128, qtr * 1024:(qtr + 1) * 1024]))(),
                    [wo_bf_b], [wrb], f"c_w{i}_{qtr}_{dc}")
                for ts in range(4):
                    for hf in range(2):
                        bank, off, bb = PS[ts * 2 + hf]
                        em.op("pe", (lambda dc=dc, ts=ts, hf=hf, wr=wr, bank=bank, off=off:
                                     lambda e: e.matmul(
                            bank[:, off:off + 512],
                            fm0[:, dc * 512 + ts * 128: dc * 512 + (ts + 1) * 128],
                            wr[:, hf * 512:(hf + 1) * 512], start=(dc == 0), stop=(dc == 31)))(),
                            [wrb, fm0_b], [bb], f"c_m{i}_{qtr}_{dc}_{ts}_{hf}")
            for ts in range(4):
                for hf in range(2):
                    bank, off, bb = PS[ts * 2 + hf]
                    seg = qtr * 2 + hf
                    em.op("act", (lambda bank=bank, off=off, ts=ts, qtr=qtr, hf=hf, seg=seg:
                                  lambda e: e.activation(
                        tsl[ts][0][:, qtr * 1024 + hf * 512: qtr * 1024 + (hf + 1) * 512],
                        bank[:, off:off + 512], AF.Copy,
                        accum_out=sm0[:, ts * 16 + seg:ts * 16 + seg + 1]))(),
                        [bb], [tsl[ts][1], sm0_b], f"c_e{i}_{qtr}_{ts}_{hf}")
                    em.op("act", (lambda bank=bank, off=off, ts=ts, seg=seg: lambda e: e.activation(
                        lnout[:, 0:512], bank[:, off:off + 512], AF.Square,
                        accum_out=sm0[:, ts * 16 + 8 + seg:ts * 16 + 8 + seg + 1]))(),
                        [bb], [lnout_b, sm0_b], f"c_q{i}_{qtr}_{ts}_{hf}")
        for ts in range(4):
            r0 = i * 512 + ts * 128
            rr, nm = ln_finish(ts * 16, ts * 16 + 8, f"c_L{i}_{ts}")
            em.op("act", (lambda ts=ts, rr=rr, nm=nm: lambda e: e.activation(
                lnout[:], tsl[ts][0][:], AF.Identity, bias=nm, scale=rr))(),
                [tsl[ts][1], gsm_b], [lnout_b], f"c_a{i}_{ts}")
            em.dma("pool", (lambda r0=r0: lambda e: e.dma_start(slab_a[:], xo_in[r0:r0 + 128, :]))(),
                   [xo_in_b], [slab_a_b], f"c_x{i}_{ts}")
            em.op("dve", lambda e: e.tensor_tensor(slab_b[:], slab_a[:], lnout[:], ALU.add),
                  [slab_a_b, lnout_b], [slab_b_b], f"c_r{i}_{ts}")
            em.dma("pool", (lambda ts=ts: lambda e: e.dma_start(
                x1scr[ts * 128:(ts + 1) * 128, :], slab_b[:]))(),
                [slab_b_b], [x1scr_b], f"c_1{i}_{ts}")
            ln_tile(slab_b[:], slab_b_b, lnout[:], lnout_b, f"c_N{i}_{ts}")
            em.dma("act", (lambda ts=ts: lambda e: e.dma_start(
                ln4scr[ts * 128:(ts + 1) * 128, :], lnout[:]))(),
                [lnout_b], [ln4scr_b], f"c_4{i}_{ts}")
    def p2c_ffn(i):
        for dc in range(32):
            dst, dstb = fmsl(1, dc, 0, 512)
            em.dma("act", (lambda dc=dc, dst=dst: lambda e: e.dma_start(
                dst, ln4scr[0:512, dc * 128:(dc + 1) * 128], transpose=True))(),
                [ln4scr_b], [dstb], f"c_f{i}_{dc}")
        for ic in range(32):
            ar, arb = (augr0, augr0_b) if ic % 2 == 0 else (augr1, augr1_b)
            em.dma("sp", (lambda ic=ic, ar=ar: lambda e: e.dma_start(
                ar[:], lataug[ic * 128:(ic + 1) * 128, :]))(),
                [lataug_b], [arb], f"f_a{i}_{ic}")
            src, srcb = fmsl(1, ic, 0, 512)
            em.op("pe", (lambda ic=ic, ar=ar, src=src: lambda e: e.matmul(
                pA[:, 0:512], ar[:, 0:128], src,
                start=(ic == 0), stop=(ic == 31)))(),
                [arb, srcb], [pbufs["A0"]], f"f_h1{i}_{ic}")
            em.op("pe", (lambda ic=ic, ar=ar, src=src: lambda e: e.matmul(
                pA[:, 512:1024], ar[:, 128:256], src,
                start=(ic == 0), stop=(ic == 31)))(),
                [arb, srcb], [pbufs["A1"]], f"f_h2{i}_{ic}")
        em.op("act", lambda e: e.copy(lgt[:], pA[0:NE, 0:512]), [pbufs["A0"]], [lgt_b], f"f_lg{i}")
        for ts in range(4):
            em.op("pe", (lambda ts=ts: lambda e: e.transpose(
                pC[0:128, 0:NE], lgt[:, ts * 128:(ts + 1) * 128], ident_sb[0:NE, 0:NE]))(),
                [lgt_b, ident_sb_b], [pbufs["C0"]], f"f_t{i}_{ts}")
            m1, m2 = gsm[:, 20:21], gsm[:, 21:22]
            d12, ee, p1 = gsm[:, 22:23], gsm[:, 23:24], gsm[:, 24:25]
            p2 = gsm[:, 25:26]
            em.op("dve", lambda e: e.reduce_max(m1, pC[0:128, 0:NE], axis=AX.X),
                  [pbufs["C0"]], [gsm_b], f"f_m1{i}_{ts}")
            em.op("dve", lambda e: e.tensor_scalar(gtmp[:], pC[0:128, 0:NE], m1, None, ALU.is_ge),
                  [pbufs["C0"], gsm_b], [gtmp_b], f"f_i1{i}_{ts}")
            em.op("dve", lambda e: e.scalar_tensor_tensor(gtmp2[:], gtmp[:], -1e30, pC[0:128, 0:NE],
                                                          ALU.mult, ALU.add),
                  [gtmp_b, pbufs["C0"]], [gtmp2_b], f"f_msk{i}_{ts}")
            em.op("dve", lambda e: e.reduce_max(m2, gtmp2[:], axis=AX.X), [gtmp2_b], [gsm_b], f"f_m2{i}_{ts}")
            em.op("dve", lambda e: e.tensor_scalar(gtmp2[:], gtmp2[:], m2, None, ALU.is_ge),
                  [gtmp2_b, gsm_b], [gtmp2_b], f"f_i2{i}_{ts}")
            em.op("dve", lambda e: e.tensor_tensor(d12, m1, m2, ALU.subtract), [gsm_b], [gsm_b], f"f_d{i}_{ts}")
            em.op("act", lambda e: e.activation(ee, d12, AF.Exp, scale=-1.0), [gsm_b], [gsm_b], f"f_e{i}_{ts}")
            em.op("dve", lambda e: e.tensor_scalar(ee, ee, 1.0, None, ALU.add), [gsm_b], [gsm_b], f"f_e1{i}_{ts}")
            em.op("dve", lambda e: e.reciprocal(p1, ee), [gsm_b], [gsm_b], f"f_p1{i}_{ts}")
            em.op("dve", lambda e: e.tensor_scalar(p2, p1, -1.0, 1.0, ALU.mult, ALU.add),
                  [gsm_b], [gsm_b], f"f_p2{i}_{ts}")
            em.op("dve", lambda e: e.tensor_scalar(gate_tm[:], gtmp[:], p1, None, ALU.mult),
                  [gtmp_b, gsm_b], [gate_tm_b], f"f_g1{i}_{ts}")
            em.op("dve", lambda e: e.scalar_tensor_tensor(gate_tm[:], gtmp2[:], p2, gate_tm[:],
                                                          ALU.mult, ALU.add),
                  [gtmp2_b, gsm_b, gate_tm_b], [gate_tm_b], f"f_g2{i}_{ts}")
            em.op("pe", (lambda ts=ts: lambda e: e.transpose(
                pD[0:NE, 0:128], gate_tm[:], ident_sb[:]))(),
                [gate_tm_b, ident_sb_b], [pbufs["D0"]], f"f_gt{i}_{ts}")
            em.op("act", (lambda ts=ts: lambda e: e.copy(g28[:, ts * 128:(ts + 1) * 128],
                                                         pD[0:NE, 0:128]))(),
                  [pbufs["D0"]], [g28_b], f"f_ge{i}_{ts}")
        em.op("pe", lambda e: e.matmul(pC[0:128, 0:512], r1_sb[:], g28[:], start=True, stop=True),
              [r1_sb_b, g28_b], [pbufs["C0"]], f"f_G1{i}")
        em.op("pe", lambda e: e.matmul(pC[0:128, 512:1024], r2_sb[:], g28[:], start=True, stop=True),
              [r2_sb_b, g28_b], [pbufs["C1"]], f"f_G2{i}")
        em.op("act", lambda e: e.copy(gf1[:], pC[0:128, 0:512]), [pbufs["C0"]], [gf1_b], f"f_ge1{i}")
        em.op("act", lambda e: e.copy(gf2[:], pC[0:128, 512:1024]), [pbufs["C1"]], [gf2_b], f"f_ge2{i}")
        em.op("dve", lambda e: e.tensor_tensor(h1p[:], pA[0:128, 0:512], gf1[:], ALU.mult),
              [pbufs["A0"], gf1_b], [h1p_b], f"f_hp1{i}")
        em.op("dve", lambda e: e.tensor_tensor(h2p[:], pA[0:128, 512:1024], gf2[:], ALU.mult),
              [pbufs["A1"], gf2_b], [h2p_b], f"f_hp2{i}")
        for qtr in range(4):
            em.dma("sp", (lambda qtr=qtr: lambda e: e.dma_start(
                lbr1[32:128, :], lbsc[0:96, qtr * 1024:(qtr + 1) * 1024]))(),
                [lbsc_b], [lbr1_b], f"f_b1{i}_{qtr}")
            em.dma("sp", (lambda qtr=qtr: lambda e: e.dma_start(
                lbr2[:], lbsc[96:224, qtr * 1024:(qtr + 1) * 1024]))(),
                [lbsc_b], [lbr2_b], f"f_b2{i}_{qtr}")
            for dc in range(32):
                wr, wrb = WR4[dc % 6]
                em.dma("sp", (lambda dc=dc, qtr=qtr, wr=wr: lambda e: e.dma_start(
                    wr[:], bwT_bf[dc * 128:(dc + 1) * 128, qtr * 1024:(qtr + 1) * 1024]))(),
                    [bwT_bf_b], [wrb], f"f_w{i}_{qtr}_{dc}")
                for ts in range(4):
                    src, srcb = fmsl(1, dc, ts * 128, (ts + 1) * 128)
                    for hf in range(2):
                        bank, off, bb = PS[ts * 2 + hf]
                        em.op("pe", (lambda dc=dc, ts=ts, hf=hf, wr=wr, bank=bank, off=off, src=src:
                                     lambda e: e.matmul(
                            bank[:, off:off + 512], src,
                            wr[:, hf * 512:(hf + 1) * 512], start=(dc == 0), stop=False))(),
                            [wrb, srcb], [bb], f"f_m{i}_{qtr}_{dc}_{ts}_{hf}")
            for ts in range(4):
                for hf in range(2):
                    bank, off, bb = PS[ts * 2 + hf]
                    em.op("pe", (lambda ts=ts, hf=hf, bank=bank, off=off: lambda e: e.matmul(
                        bank[:, off:off + 512], h1p[:, ts * 128:(ts + 1) * 128],
                        lbr1[:, hf * 512:(hf + 1) * 512], start=False, stop=False))(),
                        [h1p_b, lbr1_b], [bb], f"f_l1{i}_{qtr}_{ts}_{hf}")
                    em.op("pe", (lambda ts=ts, hf=hf, bank=bank, off=off: lambda e: e.matmul(
                        bank[:, off:off + 512], h2p[:, ts * 128:(ts + 1) * 128],
                        lbr2[:, hf * 512:(hf + 1) * 512], start=False, stop=True))(),
                        [h2p_b, lbr2_b], [bb], f"f_l2{i}_{qtr}_{ts}_{hf}")
                    seg = qtr * 2 + hf
                    em.op("act", (lambda bank=bank, off=off, ts=ts, qtr=qtr, hf=hf, seg=seg:
                                  lambda e: e.activation(
                        tsl[ts][0][:, qtr * 1024 + hf * 512: qtr * 1024 + (hf + 1) * 512],
                        bank[:, off:off + 512], AF.Copy,
                        accum_out=sm0[:, ts * 16 + seg:ts * 16 + seg + 1]))(),
                        [bb], [tsl[ts][1], sm0_b], f"f_e{i}_{qtr}_{ts}_{hf}")
                    em.op("act", (lambda bank=bank, off=off, ts=ts, seg=seg: lambda e: e.activation(
                        lnout[:, 0:512], bank[:, off:off + 512], AF.Square,
                        accum_out=sm0[:, ts * 16 + 8 + seg:ts * 16 + 8 + seg + 1]))(),
                        [bb], [lnout_b, sm0_b], f"f_q{i}_{qtr}_{ts}_{hf}")
        for ts in range(4):
            r0 = i * 512 + ts * 128
            rr, nm = ln_finish(ts * 16, ts * 16 + 8, f"o_L{i}_{ts}")
            em.op("act", (lambda ts=ts, rr=rr, nm=nm: lambda e: e.activation(
                lnout[:], tsl[ts][0][:], AF.Identity, bias=nm, scale=rr))(),
                [tsl[ts][1], gsm_b], [lnout_b], f"o_a{i}_{ts}")
            em.dma("pool", (lambda ts=ts: lambda e: e.dma_start(
                slab_a[:], x1scr[ts * 128:(ts + 1) * 128, :]))(),
                [x1scr_b], [slab_a_b], f"o_x{i}_{ts}")
            em.op("dve", lambda e: e.tensor_tensor(slab_b[:], slab_a[:], lnout[:], ALU.add),
                  [slab_a_b, lnout_b], [slab_b_b], f"o_r{i}_{ts}")
            em.dma("pool", (lambda r0=r0: lambda e: e.dma_start(out_ap[r0:r0 + 128, :], slab_b[:]))(),
                   [slab_b_b], [out_b], f"o_o{i}_{ts}")

    p2a_main(0)
    for i in range(NT):
        nch = 8 * (i + 1)
        p2b(i, nch, nch // 2)
        p2c_wo(i)
        if i + 1 < NT:
            p2a_main(i + 1)
        p2c_ffn(i)

    tap_map = {"lnscr": (lnscr, lnscr_tb, [T, D], BF16), "kscr": (kscr, [kscr_b], [NKVH, 128, T], BF16),
               "vscr": (vscr, [vscr_b], [NKVH, 128, T], BF16), "qscr": (qscr, [qscr_b], [2048, D], BF16),
               "lnoscr": (lnoscr, lnoscr_tb, [2048, D], BF16),
               "lataug": (lataug, [lataug_b], [D, 256], BF16), "lbsc": (lbsc, [lbsc_b], [NE * RANK, D], BF16),
               "bwT": (bwT_bf, [bwT_bf_b], [D, D], BF16),
               "x1scr": (x1scr, [x1scr_b], [512, D], F32),
               "ln4scr": (ln4scr, [ln4scr_b], [512, D], BF16)}
    for tname in taps:
        ap, bufs_, shp, dt = tap_map[tname]
        tout = nc.dram_tensor("tap_" + tname, shp, dt, kind="ExternalOutput").ap()
        tout_b = em.buf("tap_" + tname)
        em.dma("sp", (lambda ap=ap, tout=tout: lambda e: e.dma_start(tout, ap))(),
               list(bufs_), [tout_b], f"tap_{tname}")
        em.barrier("sp", [tout_b], f"tapw_{tname}")

    if dbg:
        em.barrier("sp", dbg_bufs, "final_dbg")
    em.barrier("sp", [out_b], "final_out")

    em.finalize(nc)
    with nc.Block() as blk:
        @blk.tensor
        def _(t):
            em.emit(t, "pe")

        @blk.scalar
        def _(s):
            em.emit(s, "act")

        @blk.vector
        def _(v):
            em.emit(v, "dve")

        @blk.gpsimd
        def _(g):
            em.emit(g, "pool")

        @blk.sync
        def _(s):
            em.emit(s, "sp")
    return nc


def _rope_tables():
    idx = (np.arange(0, 128, 2, dtype=np.float32) / np.float32(128.0))
    theta = np.float32(500000.0)
    freq = (np.float32(1.0 / (2.0 * np.pi)) * theta ** (-idx)).astype(np.float32)
    factor, lo, hi = np.float32(32.0), np.float32(1.0), np.float32(4.0)
    L0 = np.float32(8192.0)
    freq_low, freq_high = lo / L0, hi / L0
    freq_scaled = np.where(freq < freq_low, freq / factor, freq)
    smooth = np.clip((L0 * freq - lo) / (hi - lo), 0.0, 1.0).astype(np.float32)
    freq_smooth = ((1.0 - smooth) * (freq / factor) + smooth * freq).astype(np.float32)
    is_mid = (freq >= freq_low) & (freq <= freq_high)
    freqf = np.where(is_mid, freq_smooth, freq_scaled).astype(np.float32)
    pos = np.arange(T, dtype=np.float32)
    phase = (np.float32(2.0 * np.pi) * pos[:, None] * freqf[None, :]).astype(np.float32)
    emb = np.concatenate([phase, phase], axis=-1)
    return np.cos(emb).astype(np.float32), np.sin(emb).astype(np.float32)


def _own_rows(role):
    rows = []
    for i in range(NT):
        q0 = 1024 * i + 512 * role
        rows.extend(range(q0, q0 + 512))
    return np.asarray(rows)


def _masks(role):
    m = np.zeros((4, 128, 1024), dtype=np.float32)
    q = np.arange(512)
    kk = np.arange(128)
    for fr in range(4):
        for hh in range(2):
            f = 2 * fr + hh
            valid = (128 * f + kk[:, None]) <= (512 * role + q[None, :])
            m[fr][:, hh * 512:(hh + 1) * 512] = valid.astype(np.float32)
    return m.astype(ml_dtypes.bfloat16)


_BUILD_CACHE = {}


def prepare_in_maps(L0_attn_resid, L0_attn_Wq, L0_attn_Wk, L0_attn_Wv, L0_attn_Wo,
                    L0_ffn_base_W, L0_ffn_gate_W, L0_ffn_lora_A, L0_ffn_lora_B):
    x = np.ascontiguousarray(L0_attn_resid, dtype=np.float32)
    cosT, sinT = _rope_tables()
    sinm_fm = sinT.T.copy()
    sinm_fm[0:64, :] *= -1.0
    cos_fm = cosT.T.copy()
    ident = np.eye(128, dtype=np.float32)
    r1 = np.zeros((NE, 128), dtype=np.float32)
    for j in range(96):
        r1[j // 8, 32 + j] = 2.0
    r2 = np.zeros((NE, 128), dtype=np.float32)
    for j in range(128):
        r2[12 + j // 8, j] = 2.0

    la = np.ascontiguousarray(L0_ffn_lora_A, dtype=np.float32).reshape(NE * RANK, D)
    lb2 = np.ascontiguousarray(
        np.asarray(L0_ffn_lora_B, dtype=np.float32).transpose(0, 2, 1).reshape(NE * RANK, D))
    lataug_h = np.zeros((D, 256), dtype=np.float32)
    lataug_h[:, 0:NE] = np.asarray(L0_ffn_gate_W, np.float32).T
    lataug_h[:, 32:256] = la.T
    bwt_h = np.ascontiguousarray(np.asarray(L0_ffn_base_W, np.float32).T)

    bf = ml_dtypes.bfloat16
    common = dict(
        wq_in=np.ascontiguousarray(L0_attn_Wq, np.float32),
        wk_in=np.ascontiguousarray(L0_attn_Wk, np.float32),
        wv_in=np.ascontiguousarray(L0_attn_Wv, np.float32),
        wo_in=np.ascontiguousarray(L0_attn_Wo, np.float32),
        bwt_in=bwt_h, lataug_in=np.ascontiguousarray(lataug_h), lb2_in=lb2,
        cosf=cos_fm.astype(bf), sinf=sinm_fm.astype(bf),
        ident=ident, r1c=r1.astype(bf), r2c=r2.astype(bf),
    )
    in_maps = []
    own = [_own_rows(r) for r in (0, 1)]
    for core in range(8):
        s, r = core // 2, core % 2
        rows = own[r]
        sin_own = sinT[rows].copy()
        sin_own[:, 0:64] *= -1.0
        m = dict(common)
        m["x_in"] = x[s]
        m["xo_in"] = np.ascontiguousarray(x[s][rows])
        m["cost"] = cosT[rows].astype(bf)
        m["sint"] = sin_own.astype(bf)
        m["masks"] = _masks(r)
        in_maps.append(m)
    return in_maps, own


def kernel(**inputs):
    in_maps, own = prepare_in_maps(**inputs)
    if "nc" not in _BUILD_CACHE:
        _BUILD_CACHE["nc"] = build_program()
    nc = _BUILD_CACHE["nc"]
    res = run_bass_kernel_spmd(nc, in_maps, core_ids=list(range(8)))
    out = np.empty((B, T, D), dtype=np.float32)
    for core in range(8):
        s, r = core // 2, core % 2
        out[s][own[r]] = res.results[core]["out"]
    return out
